# version 33
# baseline (speedup 1.0000x reference)
import contextlib
import numpy as np
import concourse.bass as bass
import concourse.mybir as mybir
from concourse.bass_utils import run_bass_kernel_spmd

F32 = mybir.dt.float32
BF16 = mybir.dt.bfloat16
AT = mybir.ActivationFunctionType
OP = mybir.AluOpType

ENGS = (("t", "tensor"), ("v", "vector"), ("s", "scalar"), ("g", "gpsimd"), ("y", "sync"))

OWN_SEM = {"t": "t", "v": "v", "s": "s"}


class Sched:

    def __init__(self, nc, semnames):
        self.nc = nc
        self.sems = {}
        for n in semnames:
            self.sems[n] = nc.semaphore(n).__enter__()
        self.cnt = {n: 0 for n in semnames}
        self.q = {k: [] for k, _ in ENGS}
        self.wm = {k: {} for k, _ in ENGS}

    def op(self, eng, build, inc=None, inc_n=1, deps=()):
        self.wait(eng, deps)
        ev = None
        semh = None
        if inc is not None:
            self.cnt[inc] += inc_n
            ev = (inc, self.cnt[inc])
            semh = self.sems[inc]
        def run(e, build=build, semh=semh, inc_n=inc_n):
            ins = build(e)
            if semh is not None:
                ins.then_inc(semh, inc_n)
        self.q[eng].append(run)
        return ev

    def wait(self, eng, deps):
        own = OWN_SEM.get(eng)
        for d in deps:
            if d is None:
                continue
            sem, val = d
            if val is None or val <= 0:
                continue
            if sem == own:
                continue
            if self.wm[eng].get(sem, 0) >= val:
                continue
            self.wm[eng][sem] = val
            h = self.sems[sem]
            self.q[eng].append(lambda e, h=h, v=val: e.wait_ge(h, v))

    COMPUTE_SEMS = ("t", "v", "s")

    def flush(self):
        with self.nc.Block() as blk:
            for key, attr in ENGS:
                q = self.q[key]
                if not q:
                    continue
                getattr(blk, attr)(lambda e, q=q: [f(e) for f in q])
        self.q = {k: [] for k, _ in ENGS}
        for k, _ in ENGS:
            for sname in self.COMPUTE_SEMS:
                if sname in self.cnt:
                    self.wm[k][sname] = self.cnt[sname]


class Pool:

    def __init__(self, slots):
        self.slots = [[s, None] for s in slots]
        self.i = 0

    def acquire(self):
        slot = self.slots[self.i % len(self.slots)]
        self.i += 1
        t, ev = slot
        return t, ([ev] if ev else []), slot

    def release(self, slot, ev):
        slot[1] = ev


def build_nc(T=2048, D=1024, HID=4096, C=512, taus=(512, 768, 768)):
    assert T % C == 0 and C % 128 == 0 and D % 128 == 0 and HID % 128 == 0
    assert sum(taus) == T
    ND = D // 128
    NH = HID // 128
    NKT = C // 128
    NCH = T // C
    CXN = T + 16
    TAUM = max(taus)
    EPS = 1e-5

    nc = bass.Bass("TRN2", target_bir_lowering=False)

    xT = nc.dram_tensor("xT", [128, (T // 512) * (D // 128) * 512], F32, kind="ExternalInput")
    wwt = nc.dram_tensor("wwt", [128, ND * D], F32, kind="ExternalInput")
    wrt = nc.dram_tensor("wrt", [128, ND * D], F32, kind="ExternalInput")
    w1 = nc.dram_tensor("w1", [128, NH * D], F32, kind="ExternalInput")
    w2 = nc.dram_tensor("w2", [128, NH * D], F32, kind="ExternalInput")
    h1 = nc.dram_tensor("h1", [128, NH * HID], F32, kind="ExternalInput")
    h2 = nc.dram_tensor("h2", [128, NH * HID], F32, kind="ExternalInput")
    w3 = nc.dram_tensor("w3", [128, ND * HID], F32, kind="ExternalInput")
    NSV = 4 * NH + ND + 2
    NCS = C + NKT + 2 * NKT * C
    svd = nc.dram_tensor("svd", [128, NSV], F32, kind="ExternalInput")
    csd = nc.dram_tensor("csd", [128, NCS], F32, kind="ExternalInput")
    ident_d = nc.dram_tensor("ident_d", [128, 128], F32, kind="ExternalInput")
    out_d = nc.dram_tensor("out", [D, T], F32, kind="ExternalOutput")

    S = Sched(nc, ["t", "v", "s", "g", "gw", "o", "gcs", "gid",
                   "x0", "x1", "x2", "x3", "x4", "x5", "x6", "x7",
                   "w0", "w1s", "w2s", "l0", "l1", "l2", "l3", "l4", "l5"])
    wsems = ["w0", "w1s", "w2s"]

    def sb(st, name, shape, dt):
        return st.enter_context(nc.sbuf_tensor(name, shape, dt))

    def ps(st, name, shape, dt=F32):
        return st.enter_context(nc.psum_tensor(name, shape, dt))

    stP = contextlib.ExitStack()
    XN = sb(stP, "XN", [128, ND * CXN], BF16)
    Pacc = sb(stP, "Pacc", [128, ND * T], BF16)
    gam_v = sb(stP, "gam_v", [128, 1], F32)
    lg_v = sb(stP, "lg_v", [128, 1], F32)
    al_v = sb(stP, "al_v", [128, 1], F32)
    gc_v = sb(stP, "gc_v", [128, 1], F32)
    SV = sb(stP, "SV", [128, NSV], F32)
    s1v = SV[:, 0:NH]
    s2v = SV[:, NH:2 * NH]
    sh1v = SV[:, 2 * NH:3 * NH]
    sh2v = SV[:, 3 * NH:4 * NH]
    s3v = SV[:, 4 * NH:4 * NH + ND]
    dec_v = SV[:, 4 * NH + ND:4 * NH + ND + 1]
    lad_v = SV[:, 4 * NH + ND + 1:4 * NH + ND + 2]
    eps_v = sb(stP, "eps_v", [128, 1], F32)
    cc_v = sb(stP, "cc_v", [128, 1], F32)
    invd_v = sb(stP, "invd_v", [128, 1], F32)
    nh_v = sb(stP, "nh_v", [128, 1], F32)
    wslabs_l = [sb(stP, "wl%d" % i, [128, D], BF16) for i in range(6)]

    stB = contextlib.ExitStack()
    Mt = sb(stB, "Mt", [128, NKT * C], BF16)
    GPbf = sb(stB, "GPbf", [128, C], BF16)
    gwv = sb(stB, "gwv", [128, NKT], F32)
    identb = sb(stB, "identb", [128, 128], BF16)
    onesb = sb(stB, "onesb", [128, 128], BF16)
    Wwt_sb = sb(stB, "Wwt_sb", [128, ND * D], BF16)
    Wrt_sb = sb(stB, "Wrt_sb", [128, ND * D], BF16)
    Vtm = sb(stB, "Vtm", [128, NKT * D], BF16)
    WKtm = sb(stB, "WKtm", [128, NKT * D], BF16)
    Stm = sb(stB, "Stm", [128, NKT * C], BF16)

    stA = contextlib.ExitStack()
    MEAN = sb(stA, "MEAN", [128, T], F32)
    XBF = sb(stA, "XBF", [128, ND * T], BF16)
    Xq0 = sb(stA, "Xq0", [128, T], BF16)
    RSTDb = sb(stA, "RSTDb", [128, T], BF16)
    MRb = sb(stA, "MRb", [128, T], BF16)
    NQ = max(T // 512, 1)
    QW = T // NQ
    psA = [ps(stA, "psA%d" % q, [128, QW]) for q in range(NQ)]
    psB = [ps(stA, "psB%d" % q, [128, QW]) for q in range(NQ)]
    stT = contextlib.ExitStack()
    CS = sb(stT, "CS", [128, NCS], F32)
    prow_s = CS[:, 0:C]
    revw_s = CS[:, C:C + NKT]
    difft_s = CS[:, C + NKT:C + NKT + NKT * C]
    causal_s = CS[:, C + NKT + NKT * C:NCS]
    assert nc.sbuf_bytes_remaining >= 0

    ev_g = S.op("y", lambda y: y.dma_start(out=SV[:], in_=svd[:]), inc="g", inc_n=16)
    ev_cs = S.op("y", lambda y: y.dma_start(out=CS[:], in_=csd[:]), inc="gcs", inc_n=16)
    ev_id = S.op("g", lambda g: g.dma_start(out=identb[:], in_=ident_d[:]), inc="gid", inc_n=16)
    assert QW == C
    NEH = max(D // 512, 1)
    EW = min(D, 512)
    HWW = ND * EW
    ev_ldq = {}
    for q in range(NQ):
        if q == NQ - 1:
            S.op("g", lambda g: g.dma_start(out=Wwt_sb[:, 0:HWW], in_=wwt[:, 0:HWW]), inc="gw", inc_n=16)
            S.op("g", lambda g: g.dma_start(out=Wwt_sb[:, HWW:2 * HWW], in_=wwt[:, HWW:2 * HWW]), inc="gw", inc_n=16)
        S.op("g", lambda g, q=q: g.dma_start(
            out=XBF[:, q * ND * QW:(q + 1) * ND * QW],
            in_=xT[:, q * ND * QW:(q + 1) * ND * QW]), inc="x%d" % q, inc_n=16)
        ev_ldq[q] = ("x%d" % q, 16)
    ev_gww = [("gw", 16), ("gw", 32)]
    ev_gw = S.op("g", lambda g: g.dma_start(out=Wrt_sb[:], in_=wrt[:]), inc="gw", inc_n=16)

    def xbf(q, dc):
        return XBF[:, (q * ND + dc) * QW:(q * ND + dc + 1) * QW]

    ev_eps = S.op("v", lambda v: v.memset(eps_v[:], EPS), inc="v")
    ev_cc = S.op("v", lambda v: v.memset(cc_v[:], float(C)), inc="v")
    S.op("v", lambda v: v.memset(invd_v[:], 1.0 / D), inc="v")
    ev_nh = S.op("v", lambda v: v.memset(nh_v[:], -0.5), inc="v")
    ev_gam = S.op("s", lambda s: s.activation(gam_v[:], dec_v[:], AT.Sigmoid), inc="s", deps=[ev_g])
    ev_lg = S.op("s", lambda s: s.activation(lg_v[:], gam_v[:], AT.Ln), inc="s", deps=[ev_gam])
    S.op("s", lambda s: s.activation(al_v[:], lad_v[:], AT.Exp), inc="s", deps=[ev_lg])
    S.op("s", lambda s: s.activation(gc_v[:], lg_v[:], AT.Exp, scale=cc_v[:, 0:1]), inc="s", deps=[ev_lg, ev_cc])
    S.op("s", lambda s: s.activation(GPbf[:], prow_s[:], AT.Exp, scale=lg_v[:, 0:1]), inc="s", deps=[ev_lg, ev_cs])
    S.op("s", lambda s: s.activation(gwv[:], revw_s[:], AT.Exp, scale=lg_v[:, 0:1]), inc="s", deps=[ev_lg])
    ev_me = S.op("s", lambda s: s.activation(difft_s[:], difft_s[:], AT.Exp, scale=lg_v[:, 0:1]), inc="s", deps=[ev_lg])
    ev_ones = S.op("v", lambda v: v.memset(onesb[:], 1.0), inc="v")
    ev_z = S.op("v", lambda v: v.memset(XN[:].rearrange("p (c f) -> p c f", f=CXN)[:, :, 0:8], 0.0), inc="v")
    ev_mt = S.op("v", lambda v: v.tensor_tensor(Mt[:], difft_s[:], causal_s[:], OP.mult), inc="v", deps=[ev_me])

    stT.close()

    ev_sums_q = {}
    sq_last = {0: None, 1: None}

    def emit_sums(q):
        mm = None
        for dc in range(ND):
            b = dc % 2
            sq = S.op("v", lambda v, q=q, dc=dc, b=b: v.tensor_tensor(
                Xq0[:, b * QW:(b + 1) * QW], xbf(q, dc), xbf(q, dc), OP.mult),
                inc="v", deps=[ev_ldq[q], sq_last[b]])
            S.op("t", lambda t, q=q, dc=dc: t.matmul(
                psA[q][:, 0:QW], onesb[:], xbf(q, dc),
                start=(dc == 0), stop=(dc == ND - 1)),
                deps=[ev_ldq[q], ev_ones] if dc == 0 else [])
            mm = S.op("t", lambda t, q=q, dc=dc, b=b: t.matmul(
                psB[q][:, 0:QW], onesb[:], Xq0[:, b * QW:(b + 1) * QW],
                start=(dc == 0), stop=(dc == ND - 1)), deps=[sq], inc="t")
            sq_last[b] = mm
        ev_sums_q[q] = mm

    ev_ea = [None] * NQ
    ev_rsq = [None] * NQ
    ev_mrb = [None] * NQ

    def stats_q(q):
        sl = slice(q * QW, (q + 1) * QW)
        ev_ea[q] = S.op("v", lambda v, q=q, sl=sl: v.tensor_scalar_mul(
            MEAN[:, sl], psA[q][:, 0:QW], 1.0 / D), inc="v", deps=[ev_sums_q[q]])
        e3 = S.op("s", lambda s, q=q, sl=sl: s.activation(
            MRb[:, sl], MEAN[:, sl], AT.Square), inc="s", deps=[ev_ea[q]])
        e4 = S.op("v", lambda v, q=q, sl=sl: v.scalar_tensor_tensor(
            psB[q][:, 0:QW], psB[q][:, 0:QW], invd_v[:, 0:1], MRb[:, sl],
            OP.mult, OP.subtract), inc="v", deps=[e3])
        e5 = S.op("s", lambda s, q=q: s.activation(
            psB[q][:, 0:QW], psB[q][:, 0:QW], AT.Ln, bias=eps_v[:, 0:1]), inc="s", deps=[e4, ev_eps])
        ev_rsq[q] = S.op("s", lambda s, q=q, sl=sl: s.activation(
            RSTDb[:, sl], psB[q][:, 0:QW], AT.Exp, scale=nh_v[:, 0:1]), inc="s", deps=[e5, ev_nh])
        ev_mrb[q] = S.op("v", lambda v, q=q, sl=sl: v.tensor_tensor(
            MRb[:, sl], MEAN[:, sl], RSTDb[:, sl], OP.mult), inc="v", deps=[ev_rsq[q]])

    ev_xn0_dc = {}

    def p2_chunk0():
        scr = Xq0[:, 2 * QW: 2 * QW + C]
        for dc in range(ND):
            o2 = dc * CXN + 8
            S.op("v", lambda v, dc=dc, scr=scr: v.tensor_tensor(
                scr, xbf(0, dc), RSTDb[:, 0:C], OP.mult), inc="v",
                deps=[ev_rsq[0]])
            ev_xn0_dc[dc] = S.op("v", lambda v, o2=o2, scr=scr: v.tensor_tensor(
                XN[:, o2: o2 + C], scr, MRb[:, 0:C], OP.subtract), inc="v",
                deps=[ev_mrb[0], ev_z])

    ev_xn_dc = {}
    ev_p2q = [None] * NQ
    ev_pacc0 = [None]

    def emit_pacc0():
        prev = None
        for dc in range(ND):
            o2 = dc * CXN + 8
            prev = S.op("v", lambda v, dc=dc, o2=o2: v.tensor_tensor(
                Pacc[:, dc * T: dc * T + C], XN[:, o2: o2 + C], xbf(0, dc), OP.add),
                inc="v", deps=[])
        ev_pacc0[0] = prev

    def p2_rest(q):
        scr = Xq0[:, 0:C]
        prev = None
        for dc in range(ND):
            o2 = dc * CXN + 8 + q * C
            S.op("v", lambda v, q=q, dc=dc, scr=scr: v.tensor_tensor(
                scr, xbf(q, dc), RSTDb[:, q * QW:(q + 1) * QW], OP.mult), inc="v",
                deps=[ev_rsq[q]])
            ev_xn_dc[(q, dc)] = S.op("v", lambda v, o2=o2, scr=scr, q=q: v.tensor_tensor(
                XN[:, o2: o2 + C], scr, MRb[:, q * QW:(q + 1) * QW], OP.subtract),
                inc="v", deps=[ev_mrb[q]])
            prev = S.op("v", lambda v, q=q, dc=dc, o2=o2: v.tensor_tensor(
                Pacc[:, dc * T + q * C: dc * T + (q + 1) * C], XN[:, o2: o2 + C],
                xbf(q, dc), OP.add), inc="v", deps=[])
        ev_p2q[q] = prev

    stT.close()
    stA.close()

    stS = contextlib.ExitStack()
    VG = sb(stS, "VG", [128, NKT * D], BF16)
    RKG = sb(stS, "RKG", [128, ND * C], BF16)
    RDS = sb(stS, "RDS", [128, ND * C], BF16)
    Af = sb(stS, "Af", [128, ND * D], F32)
    Abf = sb(stS, "Abf", [128, ND * D], BF16)
    tps = [ps(stS, "tp%d" % i, [128, 1024], BF16) for i in range(2)]
    qs = [ps(stS, "q%d" % i, [128, 512]) for i in range(5)]
    assert nc.sbuf_bytes_remaining >= 0

    p512 = Pool([qs[2], qs[3], qs[0], qs[4], qs[1]])
    ptp = Pool(tps)
    ev_intra_prev = None
    ev_vg_prev = None
    ev_da_prev = None
    ev_abf_prev = None
    ev_ro_prev = None

    def xdep(c, dc):
        if c == 0:
            return ev_xn0_dc[dc]
        return ev_xn_dc[(c, dc)]

    ev_S_groups = {}
    ev_ve = {}
    ev_wk = {}
    ve_l = {}
    wk_l = {}
    mask_l = {}
    vg_l = {}
    intra_l = {}
    abf_l = {}
    DTH = ND // NEH

    def emit_v(c):
        base = c * C
        last = None
        ve_l[c] = {}
        for eh in range(NEH):
            for tt in range(NKT):
                slot_t, sdeps, slot = p512.acquire()
                for dc in range(ND):
                    deps = [xdep(c, dc)]
                    if dc == 0:
                        deps += sdeps + [ev_gww[eh]]
                    S.op("t", lambda t, slot_t=slot_t, tt=tt, eh=eh, dc=dc, base=base: t.matmul(
                        slot_t[:, 0:EW],
                        XN[:, dc * CXN + 8 + base + tt * 128: dc * CXN + 8 + base + tt * 128 + 128],
                        Wwt_sb[:, eh * HWW + dc * EW: eh * HWW + dc * EW + EW],
                        start=(dc == 0), stop=(dc == ND - 1)),
                        deps=deps,
                        inc="t" if dc == ND - 1 else None)
                evm = ("t", S.cnt["t"])
                dprev = []
                if c > 0:
                    dprev = [intra_l[c - 1][eh * DTH + DTH - 1], vg_l[c - 1][tt]]
                last = S.op("s", lambda s, slot_t=slot_t, tt=tt, eh=eh: s.copy(
                    Vtm[:, tt * D + eh * EW: tt * D + eh * EW + EW], slot_t[:, 0:EW]),
                    inc="s", deps=[evm] + dprev)
                ve_l[c][(tt, eh)] = last
                p512.release(slot, last)
        ev_ve[c] = last

    def emit_wk(c):
        base = c * C
        last = None
        wk_l[c] = []
        for kt in range(NKT):
            slot_t, sdeps, slot = ptp.acquire()
            for dc in range(ND):
                deps = [xdep(c, dc)]
                if dc == 0:
                    deps += sdeps
                S.op("t", lambda t, slot_t=slot_t, kt=kt, dc=dc, base=base: t.transpose(
                    slot_t[:, dc * 128:(dc + 1) * 128],
                    XN[:, dc * CXN + 7 + base + kt * 128: dc * CXN + 7 + base + kt * 128 + 128],
                    identb[:]), deps=deps + ([ev_id] if dc == 0 else []),
                    inc="t" if dc == ND - 1 else None)
            evm = ("t", S.cnt["t"])
            last = S.op("s", lambda s, slot_t=slot_t, kt=kt: s.copy(
                WKtm[:, kt * D:(kt + 1) * D], slot_t[:, 0:ND * 128]), inc="s",
                deps=[evm, ev_da_prev])
            wk_l[c].append(last)
            ptp.release(slot, last)
        ev_wk[c] = last

    def emit_S(c):
        base = c * C
        groups = []
        for kt in range(NKT):
            slot_t, sdeps, slot = p512.acquire()
            for dc in range(ND):
                deps = [xdep(c, dc)]
                if dc == 0:
                    deps += sdeps
                S.op("t", lambda t, slot_t=slot_t, kt=kt, dc=dc, base=base: t.matmul(
                    slot_t[:, 0:C],
                    XN[:, dc * CXN + 7 + base + kt * 128: dc * CXN + 7 + base + kt * 128 + 128],
                    XN[:, dc * CXN + 8 + base: dc * CXN + 8 + base + C],
                    start=(dc == 0), stop=(dc == ND - 1)),
                    deps=deps,
                    inc="t" if dc == ND - 1 else None)
            groups.append((slot_t, slot, ("t", S.cnt["t"])))
        ev_S_groups[c] = groups

    ev_masks = {}

    def emit_masks(c):
        last = None
        mask_l[c] = []
        for kt, (slot_t, slot, evm) in enumerate(ev_S_groups.pop(c)):
            last = S.op("v", lambda v, slot_t=slot_t, kt=kt: v.tensor_tensor(
                Stm[:, kt * C:(kt + 1) * C], slot_t[:, 0:C], Mt[:, kt * C:(kt + 1) * C], OP.mult),
                inc="v", deps=[evm, ev_intra_prev, ev_mt])
            mask_l[c].append(last)
            p512.release(slot, last)
        ev_masks[c] = last

    emit_sums(0)
    stats_q(0)
    emit_sums(1)
    stats_q(1)
    p2_chunk0()
    ptp.slots[0][1] = ev_ea[0]
    ptp.slots[1][1] = ev_ea[1]
    emit_wk(0)
    emit_sums(2)
    stats_q(2)
    p512.slots[0][1] = ev_rsq[0]
    p512.slots[1][1] = ev_rsq[1]
    p512.slots[2][1] = ev_ea[2]
    p512.slots[3][1] = ev_rsq[2]
    emit_S(0)
    emit_sums(3)
    stats_q(3)
    p512.slots[4][1] = ev_ea[3]
    emit_masks(0)
    p2_rest(1)
    emit_pacc0()
    emit_v(0)

    for c in range(NCH):
        base = c * C
        need_state = (c < NCH - 1) or NCH == 1
        ev_rkg_l = []
        if c > 0:
            for jc in range(ND):
                ev_rkg_l.append(S.op("v", lambda v, jc=jc, base=base: v.tensor_tensor(
                    RKG[:, jc * C:(jc + 1) * C],
                    XN[:, jc * CXN + 8 + base: jc * CXN + 8 + base + C],
                    GPbf[:], OP.mult), inc="v", deps=[ev_intra_prev, ev_pacc0[0]]))
        ev_st_last = ev_masks[c]

        ev_vg = None
        if need_state:
            vg_l[c] = []
            for tt in range(NKT):
                ev_vg = S.op("s", lambda s, tt=tt: s.activation(
                    VG[:, tt * D:(tt + 1) * D], Vtm[:, tt * D:(tt + 1) * D],
                    AT.Identity, scale=gwv[:, tt:tt + 1]), inc="s",
                    deps=[ve_l[c][(tt, 0)], ve_l[c][(tt, 1)], ev_da_prev, ev_mrb[NQ - 1]])
                vg_l[c].append(ev_vg)
            ev_vg_prev = ev_vg

        ev_intra = None
        ev_rd_last = None
        intra_l[c] = []
        rd_l = []
        for dt in range(ND):
            slot_t, sdeps, slot = p512.acquire()
            for kt in range(NKT):
                deps = []
                if kt == 0:
                    deps = list(sdeps)
                if dt % DTH == 0:
                    deps = deps + [mask_l[c][kt], ve_l[c][(kt, dt // DTH)]]
                S.op("t", lambda t, slot_t=slot_t, dt=dt, kt=kt, c=c: t.matmul(
                    slot_t[:, 0:C],
                    Vtm[:, kt * D + dt * 128: kt * D + dt * 128 + 128],
                    Stm[:, kt * C:(kt + 1) * C],
                    start=(kt == 0), stop=(kt == NKT - 1 and c == 0)), deps=deps,
                    inc="t" if (c == 0 and kt == NKT - 1) else None)
            if c > 0:
                for jc in range(ND):
                    deps = []
                    if dt == 0:
                        deps = [ev_rkg_l[jc], abf_l[c - 1][jc]]
                    S.op("t", lambda t, slot_t=slot_t, dt=dt, jc=jc: t.matmul(
                        slot_t[:, 0:C],
                        Abf[:, jc * D + dt * 128: jc * D + dt * 128 + 128],
                        RKG[:, jc * C:(jc + 1) * C],
                        start=False, stop=(jc == ND - 1)), deps=deps,
                        inc="t" if jc == ND - 1 else None)
            ev_intra = ("t", S.cnt["t"])
            intra_l[c].append(ev_intra)
            ev_rd_last = S.op("s", lambda s, slot_t=slot_t, dt=dt: s.copy(
                RDS[:, dt * C:(dt + 1) * C], slot_t[:, 0:C]), inc="s",
                deps=[ev_intra, ev_ro_prev, ev_p2q[1]])
            rd_l.append(ev_rd_last)
            p512.release(slot, ev_rd_last)
        ev_intra_prev = ev_intra

        def emit_da(c=c):
            ev_af = []
            ev_da = None
            for jc in range(ND):
                slots = []
                for ih in range(NEH):
                    slot_t, sdeps, slot = p512.acquire()
                    for kt in range(NKT):
                        deps = list(sdeps) if kt == 0 else []
                        if jc == 0 and ih == 0:
                            deps = deps + [wk_l[c][kt], vg_l[c][kt]]
                        S.op("t", lambda t, slot_t=slot_t, jc=jc, ih=ih, kt=kt: t.matmul(
                            slot_t[:, 0:EW],
                            WKtm[:, kt * D + jc * 128: kt * D + jc * 128 + 128],
                            VG[:, kt * D + ih * EW: kt * D + ih * EW + EW],
                            start=(kt == 0), stop=(kt == NKT - 1)), deps=deps,
                            inc="t" if kt == NKT - 1 else None)
                    slots.append((ih, slot_t, slot, ("t", S.cnt["t"])))
                ev_da = slots[-1][3]
                for (ih, slot_t, slot, evm) in slots:
                    sl = slice(jc * D + ih * EW, jc * D + ih * EW + EW)
                    if c == 0:
                        evf = S.op("s", lambda s, slot_t=slot_t, sl=sl: s.copy(
                            Af[:, sl], slot_t[:, 0:EW]), inc="s",
                            deps=[evm, ev_p2q[NQ - 1]])
                    else:
                        evf = S.op("v", lambda v, slot_t=slot_t, sl=sl: v.scalar_tensor_tensor(
                            Af[:, sl], Af[:, sl], gc_v[:, 0:1], slot_t[:, 0:EW],
                            OP.mult, OP.add), inc="v", deps=[evm])
                    ev_af.append(evf)
                    p512.release(slot, evf)
            return ev_da, ev_af

        def emit_ro(base=base):
            ev_ro = None
            for et in range(ND):
                slot_t, sdeps, slot = p512.acquire()
                for it in range(ND):
                    deps = list(sdeps) + [ev_gw] if it == 0 else []
                    if et == 0:
                        deps = deps + [rd_l[it]]
                    S.op("t", lambda t, slot_t=slot_t, et=et, it=it: t.matmul(
                        slot_t[:, 0:C],
                        Wrt_sb[:, et * D + it * 128: et * D + it * 128 + 128],
                        RDS[:, it * C:(it + 1) * C],
                        start=(it == 0), stop=(it == ND - 1)), deps=deps,
                        inc="t" if it == ND - 1 else None)
                ev_ro = ("t", S.cnt["t"])
                evp = S.op("v", lambda v, slot_t=slot_t, et=et, base=base: v.scalar_tensor_tensor(
                    Pacc[:, et * T + base: et * T + base + C], slot_t[:, 0:C],
                    al_v[:, 0:1], Pacc[:, et * T + base: et * T + base + C],
                    OP.mult, OP.add), inc="v", deps=[ev_ro])
                p512.release(slot, evp)
            return ev_ro

        if c == 0 and need_state:
            for q in range(2, NQ):
                p2_rest(q)
            ev_ro_prev = emit_ro()
            ev_da_prev, ev_af = emit_da()
        else:
            if need_state:
                ev_da_prev, ev_af = emit_da()
            ev_ro_prev = emit_ro()

        if need_state:
            ev_abf = None
            abf_l[c] = []
            for jc in range(ND):
                deps = [ev_af[jc * NEH + NEH - 1]]
                if c > 0:
                    deps.append(ev_intra_prev)
                ev_abf = S.op("s", lambda s, jc=jc: s.copy(
                    Abf[:, jc * D:(jc + 1) * D], Af[:, jc * D:(jc + 1) * D]), inc="s", deps=deps)
                abf_l[c].append(ev_abf)
            ev_abf_prev = ev_abf

        if c + 1 < NCH:
            emit_v(c + 1)
            if c + 1 < NCH - 1:
                emit_wk(c + 1)
            emit_S(c + 1)
            emit_masks(c + 1)

    lsems = ["l0", "l1", "l2", "l3", "l4", "l5"]
    lstate = [[None, 0] for _ in range(6)]
    lidx = [0]
    pre_l = []

    def issue_slab_l(src, ob):
        i = lidx[0] % 6
        lidx[0] += 1
        consumer, n = lstate[i]
        S.op("g", lambda g, i=i, src=src, ob=ob: g.dma_start(
            out=wslabs_l[i][:], in_=src[:, ob * D:(ob + 1) * D]),
            inc=lsems[i], inc_n=16, deps=[consumer] if consumer else [])
        lstate[i][1] = n + 1
        return i, (lsems[i], 16 * (n + 1))

    OB_ORDER = [NH - 1] + list(range(NH - 1))
    for (psrc, pob) in ((w1, OB_ORDER[0]), (w2, OB_ORDER[0]), (w1, OB_ORDER[1]), (w2, OB_ORDER[1])):
        pre_l.append(issue_slab_l(psrc, pob))

    S.flush()
    stS.close()
    stB.close()

    stC = contextlib.ExitStack()
    Ha = sb(stC, "Ha", [128, NH * TAUM], BF16)
    Hb = sb(stC, "Hb", [128, NH * TAUM], BF16)
    wslabs = [sb(stC, "wsl%d" % i, [128, HID], BF16) for i in range(3)]
    sa_tiles = [sb(stC, "sa%d" % i, [128, TAUM], BF16) for i in range(2)]
    ostg = sb(stC, "ostg", [128, 2 * TAUM], F32)
    fs = [ps(stC, "f%d" % i, [128, TAUM]) for i in range(4)]
    assert nc.sbuf_bytes_remaining >= 0

    fps = Pool(fs)
    sas = Pool(sa_tiles)
    wstate = [[None, 0] for _ in range(3)]
    widx = [0]
    n_pre = [0]

    def load_slab(src, ob, K):
        i = widx[0] % 3
        widx[0] += 1
        consumer, n = wstate[i]
        S.op("g", lambda g, i=i, src=src, ob=ob, K=K: g.dma_start(
            out=wslabs[i][:, 0:K], in_=src[:, ob * K:(ob + 1) * K]),
            inc=wsems[i], inc_n=16, deps=[consumer] if consumer else [])
        wstate[i][1] = n + 1
        return i, (wsems[i], 16 * (n + 1))

    def load_slab_l(src, ob):
        if n_pre[0] < len(pre_l):
            i, ev = pre_l[n_pre[0]]
            n_pre[0] += 1
            return i, ev
        return issue_slab_l(src, ob)

    def pieces(tau):
        out, off = [], 0
        while off < tau:
            w = min(512, tau - off)
            out.append((off, w))
            off += w
        return out

    ev_l4_mm_prev = None
    ev_ostg_dma = [None, None]
    t0 = 0
    for ti, tau in enumerate(taus):
        pcs = pieces(tau)
        ev_h1_list = [None] * NH
        ev_h1 = None
        for ob in OB_ORDER:
            ia, eva = load_slab_l(w1, ob)
            ib, evb = load_slab_l(w2, ob)
            grp = []
            for (wsl_i, evw) in ((ia, eva), (ib, evb)):
                slot_t, sdeps, slot = fps.acquire()
                for dc in range(ND):
                    for (off, w) in pcs:
                        deps = []
                        if dc == 0 and off == 0:
                            deps = list(sdeps) + [evw, ev_l4_mm_prev]
                        S.op("t", lambda t, slot_t=slot_t, wsl_i=wsl_i, dc=dc, off=off, w=w, t0=t0: t.matmul(
                            slot_t[:, off:off + w],
                            wslabs_l[wsl_i][:, dc * 128:(dc + 1) * 128],
                            XN[:, dc * CXN + 8 + t0 + off: dc * CXN + 8 + t0 + off + w],
                            start=(dc == 0), stop=(dc == ND - 1)), deps=deps,
                            inc="t" if (dc == ND - 1 and (off, w) == pcs[-1]) else None)
                grp.append((slot_t, slot, ("t", S.cnt["t"])))
            (pa_t, pa_slot, ev_ma), (pb_t, pb_slot, ev_mb) = grp
            lstate[ia][0] = ev_ma
            lstate[ib][0] = ev_mb
            sa_t, sa_deps, sa_slot = sas.acquire()
            ev_sa = S.op("s", lambda s, sa_t=sa_t, pa_t=pa_t, ob=ob, tau=tau: s.activation(
                sa_t[:, 0:tau], pa_t[:, 0:tau], AT.Silu, scale=s1v[:, ob:ob + 1]),
                inc="s", deps=[ev_ma] + sa_deps)
            ev_h1 = S.op("v", lambda v, pb_t=pb_t, sa_t=sa_t, ob=ob, tau=tau: v.scalar_tensor_tensor(
                Ha[:, ob * TAUM: ob * TAUM + tau], pb_t[:, 0:tau], s2v[:, ob:ob + 1],
                sa_t[:, 0:tau], OP.mult, OP.mult), inc="v",
                deps=[ev_mb, ev_sa, ev_l4_mm_prev])
            ev_h1_list[ob] = ev_h1
            fps.release(pa_slot, ev_sa)
            fps.release(pb_slot, ev_h1)
            sas.release(sa_slot, ev_h1)

        ev_prev_list = ev_h1_list
        ev_l2_mm = None
        for li, (wsrc, ssrc, Hin, Hout) in enumerate(((h1, sh1v, Ha, Hb), (h2, sh2v, Hb, Ha))):
            ev_hl = None
            ev_hl_list = [None] * NH
            for ob in OB_ORDER:
                iw, evw = load_slab(wsrc, ob, HID)
                slot_t, sdeps, slot = fps.acquire()
                for hc in range(NH):
                    for (off, w) in pcs:
                        deps = []
                        if hc == 0 and off == 0:
                            deps = list(sdeps) + [evw]
                        if ob == OB_ORDER[0] and off == 0:
                            deps = deps + [ev_prev_list[hc]]
                        S.op("t", lambda t, slot_t=slot_t, iw=iw, hc=hc, off=off, w=w, Hin=Hin: t.matmul(
                            slot_t[:, off:off + w],
                            wslabs[iw][:, hc * 128:(hc + 1) * 128],
                            Hin[:, hc * TAUM + off: hc * TAUM + off + w],
                            start=(hc == 0), stop=(hc == NH - 1)), deps=deps,
                            inc="t" if (hc == NH - 1 and (off, w) == pcs[-1]) else None)
                ev_mm = ("t", S.cnt["t"])
                wstate[iw][0] = ev_mm
                if li == 0:
                    ev_l2_mm = ev_mm
                deps = [ev_mm]
                if li == 1:
                    deps.append(ev_l2_mm)
                ev_hl = S.op("s", lambda s, slot_t=slot_t, ob=ob, tau=tau, ssrc=ssrc, Hout=Hout: s.activation(
                    Hout[:, ob * TAUM: ob * TAUM + tau], slot_t[:, 0:tau], AT.Gelu,
                    scale=ssrc[:, ob:ob + 1]), inc="s", deps=deps)
                ev_hl_list[ob] = ev_hl
                fps.release(slot, ev_hl)
            ev_prev_list = ev_hl_list

        ev_l4_mm = None
        for ob in range(ND):
            iw, evw = load_slab(w3, ob, HID)
            slot_t, sdeps, slot = fps.acquire()
            for hc in range(NH):
                for (off, w) in pcs:
                    deps = []
                    if hc == 0 and off == 0:
                        deps = list(sdeps) + [evw]
                    if ob == 0 and off == 0:
                        deps = deps + [ev_prev_list[hc]]
                    S.op("t", lambda t, slot_t=slot_t, iw=iw, hc=hc, off=off, w=w: t.matmul(
                        slot_t[:, off:off + w],
                        wslabs[iw][:, hc * 128:(hc + 1) * 128],
                        Ha[:, hc * TAUM + off: hc * TAUM + off + w],
                        start=(hc == 0), stop=(hc == NH - 1)), deps=deps,
                        inc="t" if (hc == NH - 1 and (off, w) == pcs[-1]) else None)
            ev_mm = ("t", S.cnt["t"])
            ev_l4_mm = ev_mm
            wstate[iw][0] = ev_mm
            osl = ob % 2
            evs = S.op("v", lambda v, slot_t=slot_t, ob=ob, osl=osl, tau=tau, t0=t0: v.scalar_tensor_tensor(
                ostg[:, osl * TAUM: osl * TAUM + tau], slot_t[:, 0:tau],
                s3v[:, ob:ob + 1], Pacc[:, ob * T + t0: ob * T + t0 + tau],
                OP.mult, OP.add), inc="v", deps=[ev_mm, ev_ostg_dma[osl]])
            fps.release(slot, evs)
            if ti == len(taus) - 1 and ob == ND - 1:
                h = tau // 2
                src_a = ostg[:].rearrange("p (c f) -> p c f", f=TAUM)[:, osl:osl + 1, 0:h]
                dst_a = out_d.ap().rearrange("(c p) t -> p c t", p=128)[:, ob:ob + 1, t0:t0 + h]
                src_b = ostg[:].rearrange("p (c f) -> p c f", f=TAUM)[:, osl:osl + 1, h:tau]
                dst_b = out_d.ap().rearrange("(c p) t -> p c t", p=128)[:, ob:ob + 1, t0 + h:t0 + tau]
                S.op("y", lambda y, src_a=src_a, dst_a=dst_a: y.dma_start(
                    out=dst_a, in_=src_a), inc="o", inc_n=16, deps=[evs])
                ev_ostg_dma[osl] = S.op("g", lambda g, src_b=src_b, dst_b=dst_b: g.dma_start(
                    out=dst_b, in_=src_b), inc="o", inc_n=16, deps=[evs])
            else:
                src_ap = ostg[:].rearrange("p (c f) -> p c f", f=TAUM)[:, osl:osl + 1, 0:tau]
                dst_ap = out_d.ap().rearrange("(c p) t -> p c t", p=128)[:, ob:ob + 1, t0:t0 + tau]
                ev_ostg_dma[osl] = S.op("y", lambda y, src_ap=src_ap, dst_ap=dst_ap: y.dma_start(
                    out=dst_ap, in_=src_ap), inc="o", inc_n=16, deps=[evs])
        ev_l4_mm_prev = ev_l4_mm
        t0 += tau

    S.wait("y", [("o", S.cnt["o"])])
    S.flush()
    stC.close()
    stP.close()

    return nc



def _block_weight(W):
    O, K = W.shape
    Wt = np.ascontiguousarray(W.T)
    a = Wt.reshape(K // 128, 128, O // 128, 128)
    return np.ascontiguousarray(a.transpose(1, 2, 0, 3)).reshape(128, (O // 128) * K)


def _moving_weight(W):
    Wt = np.ascontiguousarray(W.T)
    Dd, E = Wt.shape
    NEH, EW = max(E // 512, 1), min(E, 512)
    a = Wt.reshape(Dd // 128, 128, NEH, EW)
    return np.ascontiguousarray(a.transpose(1, 2, 0, 3)).reshape(128, NEH * (Dd // 128) * EW)


def _scale_vec(s):
    return np.ascontiguousarray(np.asarray(s, dtype=np.float32).reshape(-1, 128).T)


def _tables(C, NKT):
    p = np.arange(128, dtype=np.float32)
    prow = np.broadcast_to(np.arange(C, dtype=np.float32), (128, C)).copy()
    revw = np.empty((128, NKT), dtype=np.float32)
    difft = np.empty((128, NKT * C), dtype=np.float32)
    causal = np.empty((128, NKT * C), dtype=np.float32)
    cc = np.arange(C, dtype=np.float32)
    for kt in range(NKT):
        kg = kt * 128 + p
        revw[:, kt] = (C - 1) - kg
        d = cc[None, :] - 1.0 - kg[:, None]
        difft[:, kt * C:(kt + 1) * C] = np.maximum(d, 0.0)
        causal[:, kt * C:(kt + 1) * C] = (cc[None, :] > kg[:, None]).astype(np.float32)
    return prow, revw, difft, causal


def make_in_maps(inputs, C=512):
    NKT = C // 128
    x = np.asarray(inputs["L0_attn_out"], dtype=np.float32)
    B = x.shape[0]
    prow, revw, difft, causal = _tables(C, NKT)
    dec = np.full((128, 1), np.float32(np.asarray(inputs["L0_attn_decay"], dtype=np.float32)), dtype=np.float32)
    la = np.full((128, 1), np.float32(np.asarray(inputs["L0_attn_log_alpha"], dtype=np.float32)), dtype=np.float32)
    shared = {
        "wwt": _moving_weight(np.asarray(inputs["L0_attn_W_write"], dtype=np.float32)),
        "wrt": _block_weight(np.asarray(inputs["L0_attn_W_read"], dtype=np.float32)),
        "w1": _block_weight(np.asarray(inputs["L0_ffn_w1_w"], dtype=np.float32)),
        "w2": _block_weight(np.asarray(inputs["L0_ffn_w2_w"], dtype=np.float32)),
        "h1": _block_weight(np.asarray(inputs["L0_ffn_h1_w"], dtype=np.float32)),
        "h2": _block_weight(np.asarray(inputs["L0_ffn_h2_w"], dtype=np.float32)),
        "w3": _block_weight(np.asarray(inputs["L0_ffn_w3_w"], dtype=np.float32)),
        "svd": np.ascontiguousarray(np.concatenate([
            _scale_vec(inputs["L0_ffn_w1_s"]), _scale_vec(inputs["L0_ffn_w2_s"]),
            _scale_vec(inputs["L0_ffn_h1_s"]), _scale_vec(inputs["L0_ffn_h2_s"]),
            _scale_vec(inputs["L0_ffn_w3_s"]), dec, la], axis=1)),
        "csd": np.ascontiguousarray(np.concatenate([prow, revw, difft, causal], axis=1)),
        "ident_d": np.eye(128, dtype=np.float32),
    }
    maps = []
    NQ, QW = 4, 512
    for b in range(B):
        m = dict(shared)
        xt = x[b].T
        Dd, T = xt.shape
        m["xT"] = np.ascontiguousarray(
            xt.reshape(Dd // 128, 128, NQ, QW).transpose(1, 2, 0, 3).reshape(128, NQ * (Dd // 128) * QW))
        maps.append(m)
    return maps


def kernel(**inputs):
    nc = build_nc()
    in_maps = make_in_maps(inputs)
    res = run_bass_kernel_spmd(nc, in_maps, core_ids=list(range(8)))
    out = np.stack([np.ascontiguousarray(r["out"].T) for r in res.results])
    return out.astype(np.float32)


# version 34
# speedup vs baseline: 1.1044x; 1.1044x over previous
import contextlib
import numpy as np
import concourse.bass as bass
import concourse.mybir as mybir
from concourse.bass_utils import run_bass_kernel_spmd

F32 = mybir.dt.float32
BF16 = mybir.dt.bfloat16
AT = mybir.ActivationFunctionType
OP = mybir.AluOpType

ENGS = (("t", "tensor"), ("v", "vector"), ("s", "scalar"), ("g", "gpsimd"), ("y", "sync"))

OWN_SEM = {"t": "t", "v": "v", "s": "s"}


class Sched:

    def __init__(self, nc, semnames):
        self.nc = nc
        self.sems = {}
        for n in semnames:
            self.sems[n] = nc.semaphore(n).__enter__()
        self.cnt = {n: 0 for n in semnames}
        self.q = {k: [] for k, _ in ENGS}
        self.wm = {k: {} for k, _ in ENGS}

    def op(self, eng, build, inc=None, inc_n=1, deps=()):
        self.wait(eng, deps)
        ev = None
        semh = None
        if inc is not None:
            self.cnt[inc] += inc_n
            ev = (inc, self.cnt[inc])
            semh = self.sems[inc]
        def run(e, build=build, semh=semh, inc_n=inc_n):
            ins = build(e)
            if semh is not None:
                ins.then_inc(semh, inc_n)
        self.q[eng].append(run)
        return ev

    def wait(self, eng, deps):
        own = OWN_SEM.get(eng)
        for d in deps:
            if d is None:
                continue
            sem, val = d
            if val is None or val <= 0:
                continue
            if sem == own:
                continue
            if self.wm[eng].get(sem, 0) >= val:
                continue
            self.wm[eng][sem] = val
            h = self.sems[sem]
            self.q[eng].append(lambda e, h=h, v=val: e.wait_ge(h, v))

    COMPUTE_SEMS = ("t", "v", "s")

    def flush(self):
        with self.nc.Block() as blk:
            for key, attr in ENGS:
                q = self.q[key]
                if not q:
                    continue
                getattr(blk, attr)(lambda e, q=q: [f(e) for f in q])
        self.q = {k: [] for k, _ in ENGS}
        for k, _ in ENGS:
            for sname in self.COMPUTE_SEMS:
                if sname in self.cnt:
                    self.wm[k][sname] = self.cnt[sname]


class Pool:

    def __init__(self, slots):
        self.slots = [[s, None] for s in slots]
        self.i = 0

    def acquire(self):
        slot = self.slots[self.i % len(self.slots)]
        self.i += 1
        t, ev = slot
        return t, ([ev] if ev else []), slot

    def release(self, slot, ev):
        slot[1] = ev


def build_nc(T=2048, D=1024, HID=4096, C=512, taus=(512, 768, 768)):
    assert T % C == 0 and C % 128 == 0 and D % 128 == 0 and HID % 128 == 0
    assert sum(taus) == T
    ND = D // 128
    NH = HID // 128
    NKT = C // 128
    NCH = T // C
    CXN = T + 16
    TAUM = max(taus)
    EPS = 1e-5

    nc = bass.Bass("TRN2", target_bir_lowering=False)

    xT = nc.dram_tensor("xT", [128, (T // 512) * (D // 128) * 512], F32, kind="ExternalInput")
    wwt = nc.dram_tensor("wwt", [128, ND * D], F32, kind="ExternalInput")
    wrt = nc.dram_tensor("wrt", [128, ND * D], F32, kind="ExternalInput")
    w1 = nc.dram_tensor("w1", [128, NH * D], F32, kind="ExternalInput")
    w2 = nc.dram_tensor("w2", [128, NH * D], F32, kind="ExternalInput")
    h1 = nc.dram_tensor("h1", [128, NH * HID], F32, kind="ExternalInput")
    h2 = nc.dram_tensor("h2", [128, NH * HID], F32, kind="ExternalInput")
    w3 = nc.dram_tensor("w3", [128, ND * HID], F32, kind="ExternalInput")
    NSV = 4 * NH + ND + 2
    NCS = C + NKT + 2 * NKT * C
    svd = nc.dram_tensor("svd", [128, NSV], F32, kind="ExternalInput")
    csd = nc.dram_tensor("csd", [128, NCS], F32, kind="ExternalInput")
    ident_d = nc.dram_tensor("ident_d", [128, 128], F32, kind="ExternalInput")
    out_d = nc.dram_tensor("out", [D, T], F32, kind="ExternalOutput")

    S = Sched(nc, ["t", "v", "s", "g", "gw", "o", "gcs", "gid",
                   "x0", "x1", "x2", "x3", "x4", "x5", "x6", "x7",
                   "w0", "w1s", "w2s", "l0", "l1", "l2", "l3", "l4", "l5"])
    wsems = ["w0", "w1s", "w2s"]

    def sb(st, name, shape, dt):
        return st.enter_context(nc.sbuf_tensor(name, shape, dt))

    def ps(st, name, shape, dt=F32):
        return st.enter_context(nc.psum_tensor(name, shape, dt))

    stP = contextlib.ExitStack()
    XN = sb(stP, "XN", [128, ND * CXN], BF16)
    Pacc = sb(stP, "Pacc", [128, ND * T], BF16)
    gam_v = sb(stP, "gam_v", [128, 1], F32)
    lg_v = sb(stP, "lg_v", [128, 1], F32)
    al_v = sb(stP, "al_v", [128, 1], F32)
    gc_v = sb(stP, "gc_v", [128, 1], F32)
    SV = sb(stP, "SV", [128, NSV], F32)
    s1v = SV[:, 0:NH]
    s2v = SV[:, NH:2 * NH]
    sh1v = SV[:, 2 * NH:3 * NH]
    sh2v = SV[:, 3 * NH:4 * NH]
    s3v = SV[:, 4 * NH:4 * NH + ND]
    dec_v = SV[:, 4 * NH + ND:4 * NH + ND + 1]
    lad_v = SV[:, 4 * NH + ND + 1:4 * NH + ND + 2]
    eps_v = sb(stP, "eps_v", [128, 1], F32)
    cc_v = sb(stP, "cc_v", [128, 1], F32)
    invd_v = sb(stP, "invd_v", [128, 1], F32)
    nh_v = sb(stP, "nh_v", [128, 1], F32)
    wslabs_l = [sb(stP, "wl%d" % i, [128, D], BF16) for i in range(6)]

    stB = contextlib.ExitStack()
    Mt = sb(stB, "Mt", [128, NKT * C], BF16)
    GPbf = sb(stB, "GPbf", [128, C], BF16)
    gwv = sb(stB, "gwv", [128, NKT], F32)
    identb = sb(stB, "identb", [128, 128], BF16)
    onesb = sb(stB, "onesb", [128, 128], BF16)
    Wwt_sb = sb(stB, "Wwt_sb", [128, ND * D], BF16)
    Wrt_sb = sb(stB, "Wrt_sb", [128, ND * D], BF16)
    Vtm = sb(stB, "Vtm", [128, NKT * D], BF16)
    WKtm = sb(stB, "WKtm", [128, NKT * D], BF16)
    Stm = sb(stB, "Stm", [128, NKT * C], BF16)

    stA = contextlib.ExitStack()
    MEAN = sb(stA, "MEAN", [128, T], F32)
    XBF = sb(stA, "XBF", [128, ND * T], BF16)
    Xq0 = sb(stA, "Xq0", [128, T], BF16)
    RSTDb = sb(stA, "RSTDb", [128, T], BF16)
    MRb = sb(stA, "MRb", [128, T], BF16)
    NQ = max(T // 512, 1)
    QW = T // NQ
    psA = [ps(stA, "psA%d" % q, [128, QW]) for q in range(NQ)]
    psB = [ps(stA, "psB%d" % q, [128, QW]) for q in range(NQ)]
    stT = contextlib.ExitStack()
    CS = sb(stT, "CS", [128, NCS], F32)
    prow_s = CS[:, 0:C]
    revw_s = CS[:, C:C + NKT]
    difft_s = CS[:, C + NKT:C + NKT + NKT * C]
    causal_s = CS[:, C + NKT + NKT * C:NCS]
    assert nc.sbuf_bytes_remaining >= 0

    ev_g = S.op("y", lambda y: y.dma_start(out=SV[:], in_=svd[:]), inc="g", inc_n=16)
    ev_cs = S.op("y", lambda y: y.dma_start(out=CS[:], in_=csd[:]), inc="gcs", inc_n=16)
    ev_id = S.op("g", lambda g: g.dma_start(out=identb[:], in_=ident_d[:]), inc="gid", inc_n=16)
    assert QW == C
    NEH = max(D // 512, 1)
    EW = min(D, 512)
    HWW = ND * EW
    ev_ldq = {}
    HXQ = (ND // 2) * QW
    for q in range(NQ):
        if q == NQ - 1:
            S.op("g", lambda g: g.dma_start(out=Wwt_sb[:, 0:HWW], in_=wwt[:, 0:HWW]), inc="gw", inc_n=16)
            S.op("g", lambda g: g.dma_start(out=Wwt_sb[:, HWW:2 * HWW], in_=wwt[:, HWW:2 * HWW]), inc="gw", inc_n=16)
        o = q * ND * QW
        S.op("g", lambda g, o=o: g.dma_start(
            out=XBF[:, o:o + HXQ], in_=xT[:, o:o + HXQ]), inc="x%d" % q, inc_n=16)
        S.op("g", lambda g, o=o: g.dma_start(
            out=XBF[:, o + HXQ:o + 2 * HXQ], in_=xT[:, o + HXQ:o + 2 * HXQ]), inc="x%d" % (q + NQ), inc_n=16)
        ev_ldq[(q, 0)] = ("x%d" % q, 16)
        ev_ldq[(q, 1)] = ("x%d" % (q + NQ), 16)
    ev_gww = [("gw", 16), ("gw", 32)]
    ev_gw = S.op("g", lambda g: g.dma_start(out=Wrt_sb[:], in_=wrt[:]), inc="gw", inc_n=16)

    def xbf(q, dc):
        return XBF[:, (q * ND + dc) * QW:(q * ND + dc + 1) * QW]

    ev_eps = S.op("v", lambda v: v.memset(eps_v[:], EPS), inc="v")
    ev_cc = S.op("v", lambda v: v.memset(cc_v[:], float(C)), inc="v")
    S.op("v", lambda v: v.memset(invd_v[:], 1.0 / D), inc="v")
    ev_nh = S.op("v", lambda v: v.memset(nh_v[:], -0.5), inc="v")
    ev_gam = S.op("s", lambda s: s.activation(gam_v[:], dec_v[:], AT.Sigmoid), inc="s", deps=[ev_g])
    ev_lg = S.op("s", lambda s: s.activation(lg_v[:], gam_v[:], AT.Ln), inc="s", deps=[ev_gam])
    S.op("s", lambda s: s.activation(al_v[:], lad_v[:], AT.Exp), inc="s", deps=[ev_lg])
    S.op("s", lambda s: s.activation(gc_v[:], lg_v[:], AT.Exp, scale=cc_v[:, 0:1]), inc="s", deps=[ev_lg, ev_cc])
    S.op("s", lambda s: s.activation(GPbf[:], prow_s[:], AT.Exp, scale=lg_v[:, 0:1]), inc="s", deps=[ev_lg, ev_cs])
    S.op("s", lambda s: s.activation(gwv[:], revw_s[:], AT.Exp, scale=lg_v[:, 0:1]), inc="s", deps=[ev_lg])
    ev_me = S.op("s", lambda s: s.activation(difft_s[:], difft_s[:], AT.Exp, scale=lg_v[:, 0:1]), inc="s", deps=[ev_lg])
    ev_ones = S.op("v", lambda v: v.memset(onesb[:], 1.0), inc="v")
    ev_z = S.op("v", lambda v: v.memset(XN[:].rearrange("p (c f) -> p c f", f=CXN)[:, :, 0:8], 0.0), inc="v")
    ev_mt = S.op("v", lambda v: v.tensor_tensor(Mt[:], difft_s[:], causal_s[:], OP.mult), inc="v", deps=[ev_me])

    stT.close()

    ev_sums_q = {}
    sq_last = {0: None, 1: None}

    def emit_sums(q):
        mm = None
        for dc in range(ND):
            b = dc % 2
            h = dc // (ND // 2)
            sq = S.op("v", lambda v, q=q, dc=dc, b=b: v.tensor_tensor(
                Xq0[:, b * QW:(b + 1) * QW], xbf(q, dc), xbf(q, dc), OP.mult),
                inc="v", deps=[ev_ldq[(q, h)], sq_last[b]])
            deps = [ev_ldq[(q, h)]] if dc % (ND // 2) == 0 else []
            if dc == 0:
                deps = deps + [ev_ones]
            S.op("t", lambda t, q=q, dc=dc: t.matmul(
                psA[q][:, 0:QW], onesb[:], xbf(q, dc),
                start=(dc == 0), stop=(dc == ND - 1)),
                deps=deps)
            mm = S.op("t", lambda t, q=q, dc=dc, b=b: t.matmul(
                psB[q][:, 0:QW], onesb[:], Xq0[:, b * QW:(b + 1) * QW],
                start=(dc == 0), stop=(dc == ND - 1)), deps=[sq], inc="t")
            sq_last[b] = mm
        ev_sums_q[q] = mm

    ev_ea = [None] * NQ
    ev_rsq = [None] * NQ
    ev_mrb = [None] * NQ

    def stats_q(q):
        sl = slice(q * QW, (q + 1) * QW)
        ev_ea[q] = S.op("v", lambda v, q=q, sl=sl: v.tensor_scalar_mul(
            MEAN[:, sl], psA[q][:, 0:QW], 1.0 / D), inc="v", deps=[ev_sums_q[q]])
        e3 = S.op("s", lambda s, q=q, sl=sl: s.activation(
            MRb[:, sl], MEAN[:, sl], AT.Square), inc="s", deps=[ev_ea[q]])
        e4 = S.op("v", lambda v, q=q, sl=sl: v.scalar_tensor_tensor(
            psB[q][:, 0:QW], psB[q][:, 0:QW], invd_v[:, 0:1], MRb[:, sl],
            OP.mult, OP.subtract), inc="v", deps=[e3])
        e5 = S.op("s", lambda s, q=q: s.activation(
            psB[q][:, 0:QW], psB[q][:, 0:QW], AT.Ln, bias=eps_v[:, 0:1]), inc="s", deps=[e4, ev_eps])
        ev_rsq[q] = S.op("s", lambda s, q=q, sl=sl: s.activation(
            RSTDb[:, sl], psB[q][:, 0:QW], AT.Exp, scale=nh_v[:, 0:1]), inc="s", deps=[e5, ev_nh])
        ev_mrb[q] = S.op("v", lambda v, q=q, sl=sl: v.tensor_tensor(
            MRb[:, sl], MEAN[:, sl], RSTDb[:, sl], OP.mult), inc="v", deps=[ev_rsq[q]])

    ev_xn0_dc = {}

    def p2_chunk0():
        scr = Xq0[:, 2 * QW: 2 * QW + C]
        for dc in range(ND):
            o2 = dc * CXN + 8
            S.op("v", lambda v, dc=dc, scr=scr: v.tensor_tensor(
                scr, xbf(0, dc), RSTDb[:, 0:C], OP.mult), inc="v",
                deps=[ev_rsq[0]])
            ev_xn0_dc[dc] = S.op("v", lambda v, o2=o2, scr=scr: v.tensor_tensor(
                XN[:, o2: o2 + C], scr, MRb[:, 0:C], OP.subtract), inc="v",
                deps=[ev_mrb[0], ev_z])

    ev_xn_dc = {}
    ev_p2q = [None] * NQ
    ev_pacc0 = [None]

    def emit_pacc0():
        prev = None
        for dc in range(ND):
            o2 = dc * CXN + 8
            prev = S.op("v", lambda v, dc=dc, o2=o2: v.tensor_tensor(
                Pacc[:, dc * T: dc * T + C], XN[:, o2: o2 + C], xbf(0, dc), OP.add),
                inc="v", deps=[])
        ev_pacc0[0] = prev

    def p2_rest(q):
        scr = Xq0[:, 0:C]
        prev = None
        for dc in range(ND):
            o2 = dc * CXN + 8 + q * C
            S.op("v", lambda v, q=q, dc=dc, scr=scr: v.tensor_tensor(
                scr, xbf(q, dc), RSTDb[:, q * QW:(q + 1) * QW], OP.mult), inc="v",
                deps=[ev_rsq[q]])
            ev_xn_dc[(q, dc)] = S.op("v", lambda v, o2=o2, scr=scr, q=q: v.tensor_tensor(
                XN[:, o2: o2 + C], scr, MRb[:, q * QW:(q + 1) * QW], OP.subtract),
                inc="v", deps=[ev_mrb[q]])
            prev = S.op("v", lambda v, q=q, dc=dc, o2=o2: v.tensor_tensor(
                Pacc[:, dc * T + q * C: dc * T + (q + 1) * C], XN[:, o2: o2 + C],
                xbf(q, dc), OP.add), inc="v", deps=[])
        ev_p2q[q] = prev

    stT.close()
    stA.close()

    stS = contextlib.ExitStack()
    VG = sb(stS, "VG", [128, NKT * D], BF16)
    RKG = sb(stS, "RKG", [128, ND * C], BF16)
    RDS = sb(stS, "RDS", [128, ND * C], BF16)
    Af = sb(stS, "Af", [128, ND * D], F32)
    Abf = sb(stS, "Abf", [128, ND * D], BF16)
    tps = [ps(stS, "tp%d" % i, [128, 1024], BF16) for i in range(2)]
    qs = [ps(stS, "q%d" % i, [128, 512]) for i in range(5)]
    assert nc.sbuf_bytes_remaining >= 0

    p512 = Pool([qs[2], qs[3], qs[0], qs[4], qs[1]])
    ptp = Pool(tps)
    ev_intra_prev = None
    ev_vg_prev = None
    ev_da_prev = None
    ev_abf_prev = None
    ev_ro_prev = None

    def xdep(c, dc):
        if c == 0:
            return ev_xn0_dc[dc]
        return ev_xn_dc[(c, dc)]

    ev_S_groups = {}
    ev_ve = {}
    ev_wk = {}
    ve_l = {}
    wk_l = {}
    mask_l = {}
    vg_l = {}
    intra_l = {}
    abf_l = {}
    DTH = ND // NEH

    def emit_v(c):
        base = c * C
        last = None
        ve_l[c] = {}
        for eh in range(NEH):
            for tt in range(NKT):
                slot_t, sdeps, slot = p512.acquire()
                for dc in range(ND):
                    deps = [xdep(c, dc)]
                    if dc == 0:
                        deps += sdeps + [ev_gww[eh]]
                    S.op("t", lambda t, slot_t=slot_t, tt=tt, eh=eh, dc=dc, base=base: t.matmul(
                        slot_t[:, 0:EW],
                        XN[:, dc * CXN + 8 + base + tt * 128: dc * CXN + 8 + base + tt * 128 + 128],
                        Wwt_sb[:, eh * HWW + dc * EW: eh * HWW + dc * EW + EW],
                        start=(dc == 0), stop=(dc == ND - 1)),
                        deps=deps,
                        inc="t" if dc == ND - 1 else None)
                evm = ("t", S.cnt["t"])
                dprev = []
                if c > 0:
                    dprev = [intra_l[c - 1][eh * DTH + DTH - 1], vg_l[c - 1][tt]]
                last = S.op("s", lambda s, slot_t=slot_t, tt=tt, eh=eh: s.copy(
                    Vtm[:, tt * D + eh * EW: tt * D + eh * EW + EW], slot_t[:, 0:EW]),
                    inc="s", deps=[evm] + dprev)
                ve_l[c][(tt, eh)] = last
                p512.release(slot, last)
        ev_ve[c] = last

    def emit_wk(c):
        base = c * C
        last = None
        wk_l[c] = []
        for kt in range(NKT):
            slot_t, sdeps, slot = ptp.acquire()
            for dc in range(ND):
                deps = [xdep(c, dc)]
                if dc == 0:
                    deps += sdeps
                S.op("t", lambda t, slot_t=slot_t, kt=kt, dc=dc, base=base: t.transpose(
                    slot_t[:, dc * 128:(dc + 1) * 128],
                    XN[:, dc * CXN + 7 + base + kt * 128: dc * CXN + 7 + base + kt * 128 + 128],
                    identb[:]), deps=deps + ([ev_id] if dc == 0 else []),
                    inc="t" if dc == ND - 1 else None)
            evm = ("t", S.cnt["t"])
            last = S.op("s", lambda s, slot_t=slot_t, kt=kt: s.copy(
                WKtm[:, kt * D:(kt + 1) * D], slot_t[:, 0:ND * 128]), inc="s",
                deps=[evm, ev_da_prev])
            wk_l[c].append(last)
            ptp.release(slot, last)
        ev_wk[c] = last

    def emit_S(c):
        base = c * C
        groups = []
        for kt in range(NKT):
            slot_t, sdeps, slot = p512.acquire()
            for dc in range(ND):
                deps = [xdep(c, dc)]
                if dc == 0:
                    deps += sdeps
                S.op("t", lambda t, slot_t=slot_t, kt=kt, dc=dc, base=base: t.matmul(
                    slot_t[:, 0:C],
                    XN[:, dc * CXN + 7 + base + kt * 128: dc * CXN + 7 + base + kt * 128 + 128],
                    XN[:, dc * CXN + 8 + base: dc * CXN + 8 + base + C],
                    start=(dc == 0), stop=(dc == ND - 1)),
                    deps=deps,
                    inc="t" if dc == ND - 1 else None)
            groups.append((slot_t, slot, ("t", S.cnt["t"])))
        ev_S_groups[c] = groups

    ev_masks = {}

    def emit_masks(c):
        last = None
        mask_l[c] = []
        for kt, (slot_t, slot, evm) in enumerate(ev_S_groups.pop(c)):
            last = S.op("v", lambda v, slot_t=slot_t, kt=kt: v.tensor_tensor(
                Stm[:, kt * C:(kt + 1) * C], slot_t[:, 0:C], Mt[:, kt * C:(kt + 1) * C], OP.mult),
                inc="v", deps=[evm, ev_intra_prev, ev_mt])
            mask_l[c].append(last)
            p512.release(slot, last)
        ev_masks[c] = last

    emit_sums(0)
    stats_q(0)
    emit_sums(1)
    stats_q(1)
    p2_chunk0()
    ptp.slots[0][1] = ev_ea[0]
    ptp.slots[1][1] = ev_ea[1]
    emit_wk(0)
    emit_sums(2)
    stats_q(2)
    p512.slots[0][1] = ev_rsq[0]
    p512.slots[1][1] = ev_rsq[1]
    p512.slots[2][1] = ev_ea[2]
    p512.slots[3][1] = ev_rsq[2]
    emit_S(0)
    emit_sums(3)
    stats_q(3)
    p512.slots[4][1] = ev_ea[3]
    emit_masks(0)
    p2_rest(1)
    emit_pacc0()
    emit_v(0)

    for c in range(NCH):
        base = c * C
        need_state = (c < NCH - 1) or NCH == 1
        ev_rkg_l = []
        if c > 0:
            for jc in range(ND):
                ev_rkg_l.append(S.op("v", lambda v, jc=jc, base=base: v.tensor_tensor(
                    RKG[:, jc * C:(jc + 1) * C],
                    XN[:, jc * CXN + 8 + base: jc * CXN + 8 + base + C],
                    GPbf[:], OP.mult), inc="v", deps=[ev_intra_prev, ev_pacc0[0]]))
        ev_st_last = ev_masks[c]

        ev_vg = None
        if need_state:
            vg_l[c] = []
            for tt in range(NKT):
                ev_vg = S.op("s", lambda s, tt=tt: s.activation(
                    VG[:, tt * D:(tt + 1) * D], Vtm[:, tt * D:(tt + 1) * D],
                    AT.Identity, scale=gwv[:, tt:tt + 1]), inc="s",
                    deps=[ve_l[c][(tt, 0)], ve_l[c][(tt, 1)], ev_da_prev, ev_mrb[NQ - 1]])
                vg_l[c].append(ev_vg)
            ev_vg_prev = ev_vg

        ev_intra = None
        ev_rd_last = None
        intra_l[c] = []
        rd_l = []
        for dt in range(ND):
            slot_t, sdeps, slot = p512.acquire()
            for kt in range(NKT):
                deps = []
                if kt == 0:
                    deps = list(sdeps)
                if dt % DTH == 0:
                    deps = deps + [mask_l[c][kt], ve_l[c][(kt, dt // DTH)]]
                S.op("t", lambda t, slot_t=slot_t, dt=dt, kt=kt, c=c: t.matmul(
                    slot_t[:, 0:C],
                    Vtm[:, kt * D + dt * 128: kt * D + dt * 128 + 128],
                    Stm[:, kt * C:(kt + 1) * C],
                    start=(kt == 0), stop=(kt == NKT - 1 and c == 0)), deps=deps,
                    inc="t" if (c == 0 and kt == NKT - 1) else None)
            if c > 0:
                for jc in range(ND):
                    deps = []
                    if dt == 0:
                        deps = [ev_rkg_l[jc], abf_l[c - 1][jc]]
                    S.op("t", lambda t, slot_t=slot_t, dt=dt, jc=jc: t.matmul(
                        slot_t[:, 0:C],
                        Abf[:, jc * D + dt * 128: jc * D + dt * 128 + 128],
                        RKG[:, jc * C:(jc + 1) * C],
                        start=False, stop=(jc == ND - 1)), deps=deps,
                        inc="t" if jc == ND - 1 else None)
            ev_intra = ("t", S.cnt["t"])
            intra_l[c].append(ev_intra)
            ev_rd_last = S.op("s", lambda s, slot_t=slot_t, dt=dt: s.copy(
                RDS[:, dt * C:(dt + 1) * C], slot_t[:, 0:C]), inc="s",
                deps=[ev_intra, ev_ro_prev, ev_p2q[1]])
            rd_l.append(ev_rd_last)
            p512.release(slot, ev_rd_last)
        ev_intra_prev = ev_intra

        def emit_da(c=c):
            ev_af = []
            ev_da = None
            for jc in range(ND):
                slots = []
                for ih in range(NEH):
                    slot_t, sdeps, slot = p512.acquire()
                    for kt in range(NKT):
                        deps = list(sdeps) if kt == 0 else []
                        if jc == 0 and ih == 0:
                            deps = deps + [wk_l[c][kt], vg_l[c][kt]]
                        S.op("t", lambda t, slot_t=slot_t, jc=jc, ih=ih, kt=kt: t.matmul(
                            slot_t[:, 0:EW],
                            WKtm[:, kt * D + jc * 128: kt * D + jc * 128 + 128],
                            VG[:, kt * D + ih * EW: kt * D + ih * EW + EW],
                            start=(kt == 0), stop=(kt == NKT - 1)), deps=deps,
                            inc="t" if kt == NKT - 1 else None)
                    slots.append((ih, slot_t, slot, ("t", S.cnt["t"])))
                ev_da = slots[-1][3]
                for (ih, slot_t, slot, evm) in slots:
                    sl = slice(jc * D + ih * EW, jc * D + ih * EW + EW)
                    if c == 0:
                        evf = S.op("s", lambda s, slot_t=slot_t, sl=sl: s.copy(
                            Af[:, sl], slot_t[:, 0:EW]), inc="s",
                            deps=[evm, ev_p2q[NQ - 1]])
                    else:
                        evf = S.op("v", lambda v, slot_t=slot_t, sl=sl: v.scalar_tensor_tensor(
                            Af[:, sl], Af[:, sl], gc_v[:, 0:1], slot_t[:, 0:EW],
                            OP.mult, OP.add), inc="v", deps=[evm])
                    ev_af.append(evf)
                    p512.release(slot, evf)
            return ev_da, ev_af

        def emit_ro(base=base):
            ev_ro = None
            for et in range(ND):
                slot_t, sdeps, slot = p512.acquire()
                for it in range(ND):
                    deps = list(sdeps) + [ev_gw] if it == 0 else []
                    if et == 0:
                        deps = deps + [rd_l[it]]
                    S.op("t", lambda t, slot_t=slot_t, et=et, it=it: t.matmul(
                        slot_t[:, 0:C],
                        Wrt_sb[:, et * D + it * 128: et * D + it * 128 + 128],
                        RDS[:, it * C:(it + 1) * C],
                        start=(it == 0), stop=(it == ND - 1)), deps=deps,
                        inc="t" if it == ND - 1 else None)
                ev_ro = ("t", S.cnt["t"])
                evp = S.op("v", lambda v, slot_t=slot_t, et=et, base=base: v.scalar_tensor_tensor(
                    Pacc[:, et * T + base: et * T + base + C], slot_t[:, 0:C],
                    al_v[:, 0:1], Pacc[:, et * T + base: et * T + base + C],
                    OP.mult, OP.add), inc="v", deps=[ev_ro])
                p512.release(slot, evp)
            return ev_ro

        if c == 0 and need_state:
            for q in range(2, NQ):
                p2_rest(q)
            ev_ro_prev = emit_ro()
            ev_da_prev, ev_af = emit_da()
        else:
            if need_state:
                ev_da_prev, ev_af = emit_da()
            ev_ro_prev = emit_ro()

        if need_state:
            ev_abf = None
            abf_l[c] = []
            for jc in range(ND):
                deps = [ev_af[jc * NEH + NEH - 1]]
                if c > 0:
                    deps.append(ev_intra_prev)
                ev_abf = S.op("s", lambda s, jc=jc: s.copy(
                    Abf[:, jc * D:(jc + 1) * D], Af[:, jc * D:(jc + 1) * D]), inc="s", deps=deps)
                abf_l[c].append(ev_abf)
            ev_abf_prev = ev_abf

        if c + 1 < NCH:
            emit_v(c + 1)
            if c + 1 < NCH - 1:
                emit_wk(c + 1)
            emit_S(c + 1)
            emit_masks(c + 1)

    lsems = ["l0", "l1", "l2", "l3", "l4", "l5"]
    lstate = [[None, 0] for _ in range(6)]
    lidx = [0]
    pre_l = []

    def issue_slab_l(src, ob):
        i = lidx[0] % 6
        lidx[0] += 1
        consumer, n = lstate[i]
        S.op("g", lambda g, i=i, src=src, ob=ob: g.dma_start(
            out=wslabs_l[i][:], in_=src[:, ob * D:(ob + 1) * D]),
            inc=lsems[i], inc_n=16, deps=[consumer] if consumer else [])
        lstate[i][1] = n + 1
        return i, (lsems[i], 16 * (n + 1))

    OB_ORDER = [NH - 1] + list(range(NH - 1))
    for (psrc, pob) in ((w1, OB_ORDER[0]), (w2, OB_ORDER[0]), (w1, OB_ORDER[1]), (w2, OB_ORDER[1])):
        pre_l.append(issue_slab_l(psrc, pob))

    S.flush()
    stS.close()
    stB.close()

    stC = contextlib.ExitStack()
    Ha = sb(stC, "Ha", [128, NH * TAUM], BF16)
    Hb = sb(stC, "Hb", [128, NH * TAUM], BF16)
    wslabs = [sb(stC, "wsl%d" % i, [128, HID], BF16) for i in range(3)]
    sa_tiles = [sb(stC, "sa%d" % i, [128, TAUM], BF16) for i in range(2)]
    ostg = sb(stC, "ostg", [128, 2 * TAUM], F32)
    fs = [ps(stC, "f%d" % i, [128, TAUM]) for i in range(4)]
    assert nc.sbuf_bytes_remaining >= 0

    fps = Pool(fs)
    sas = Pool(sa_tiles)
    wstate = [[None, 0] for _ in range(3)]
    widx = [0]
    n_pre = [0]

    def load_slab(src, ob, K):
        i = widx[0] % 3
        widx[0] += 1
        consumer, n = wstate[i]
        S.op("g", lambda g, i=i, src=src, ob=ob, K=K: g.dma_start(
            out=wslabs[i][:, 0:K], in_=src[:, ob * K:(ob + 1) * K]),
            inc=wsems[i], inc_n=16, deps=[consumer] if consumer else [])
        wstate[i][1] = n + 1
        return i, (wsems[i], 16 * (n + 1))

    def load_slab_l(src, ob):
        if n_pre[0] < len(pre_l):
            i, ev = pre_l[n_pre[0]]
            n_pre[0] += 1
            return i, ev
        return issue_slab_l(src, ob)

    def pieces(tau):
        out, off = [], 0
        while off < tau:
            w = min(512, tau - off)
            out.append((off, w))
            off += w
        return out

    ev_l4_mm_prev = None
    ev_ostg_dma = [None, None]
    t0 = 0
    for ti, tau in enumerate(taus):
        pcs = pieces(tau)
        ev_h1_list = [None] * NH
        ev_h1 = None
        for ob in OB_ORDER:
            ia, eva = load_slab_l(w1, ob)
            ib, evb = load_slab_l(w2, ob)
            grp = []
            for (wsl_i, evw) in ((ia, eva), (ib, evb)):
                slot_t, sdeps, slot = fps.acquire()
                for dc in range(ND):
                    for (off, w) in pcs:
                        deps = []
                        if dc == 0 and off == 0:
                            deps = list(sdeps) + [evw, ev_l4_mm_prev]
                        S.op("t", lambda t, slot_t=slot_t, wsl_i=wsl_i, dc=dc, off=off, w=w, t0=t0: t.matmul(
                            slot_t[:, off:off + w],
                            wslabs_l[wsl_i][:, dc * 128:(dc + 1) * 128],
                            XN[:, dc * CXN + 8 + t0 + off: dc * CXN + 8 + t0 + off + w],
                            start=(dc == 0), stop=(dc == ND - 1)), deps=deps,
                            inc="t" if (dc == ND - 1 and (off, w) == pcs[-1]) else None)
                grp.append((slot_t, slot, ("t", S.cnt["t"])))
            (pa_t, pa_slot, ev_ma), (pb_t, pb_slot, ev_mb) = grp
            lstate[ia][0] = ev_ma
            lstate[ib][0] = ev_mb
            sa_t, sa_deps, sa_slot = sas.acquire()
            ev_sa = S.op("s", lambda s, sa_t=sa_t, pa_t=pa_t, ob=ob, tau=tau: s.activation(
                sa_t[:, 0:tau], pa_t[:, 0:tau], AT.Silu, scale=s1v[:, ob:ob + 1]),
                inc="s", deps=[ev_ma] + sa_deps)
            ev_h1 = S.op("v", lambda v, pb_t=pb_t, sa_t=sa_t, ob=ob, tau=tau: v.scalar_tensor_tensor(
                Ha[:, ob * TAUM: ob * TAUM + tau], pb_t[:, 0:tau], s2v[:, ob:ob + 1],
                sa_t[:, 0:tau], OP.mult, OP.mult), inc="v",
                deps=[ev_mb, ev_sa, ev_l4_mm_prev])
            ev_h1_list[ob] = ev_h1
            fps.release(pa_slot, ev_sa)
            fps.release(pb_slot, ev_h1)
            sas.release(sa_slot, ev_h1)

        ev_prev_list = ev_h1_list
        ev_l2_mm = None
        for li, (wsrc, ssrc, Hin, Hout) in enumerate(((h1, sh1v, Ha, Hb), (h2, sh2v, Hb, Ha))):
            ev_hl = None
            ev_hl_list = [None] * NH
            for ob in OB_ORDER:
                iw, evw = load_slab(wsrc, ob, HID)
                slot_t, sdeps, slot = fps.acquire()
                for hc in range(NH):
                    for (off, w) in pcs:
                        deps = []
                        if hc == 0 and off == 0:
                            deps = list(sdeps) + [evw]
                        if ob == OB_ORDER[0] and off == 0:
                            deps = deps + [ev_prev_list[hc]]
                        S.op("t", lambda t, slot_t=slot_t, iw=iw, hc=hc, off=off, w=w, Hin=Hin: t.matmul(
                            slot_t[:, off:off + w],
                            wslabs[iw][:, hc * 128:(hc + 1) * 128],
                            Hin[:, hc * TAUM + off: hc * TAUM + off + w],
                            start=(hc == 0), stop=(hc == NH - 1)), deps=deps,
                            inc="t" if (hc == NH - 1 and (off, w) == pcs[-1]) else None)
                ev_mm = ("t", S.cnt["t"])
                wstate[iw][0] = ev_mm
                if li == 0:
                    ev_l2_mm = ev_mm
                deps = [ev_mm]
                if li == 1:
                    deps.append(ev_l2_mm)
                ev_hl = S.op("s", lambda s, slot_t=slot_t, ob=ob, tau=tau, ssrc=ssrc, Hout=Hout: s.activation(
                    Hout[:, ob * TAUM: ob * TAUM + tau], slot_t[:, 0:tau], AT.Gelu,
                    scale=ssrc[:, ob:ob + 1]), inc="s", deps=deps)
                ev_hl_list[ob] = ev_hl
                fps.release(slot, ev_hl)
            ev_prev_list = ev_hl_list

        ev_l4_mm = None
        for ob in range(ND):
            iw, evw = load_slab(w3, ob, HID)
            slot_t, sdeps, slot = fps.acquire()
            for hc in range(NH):
                for (off, w) in pcs:
                    deps = []
                    if hc == 0 and off == 0:
                        deps = list(sdeps) + [evw]
                    if ob == 0 and off == 0:
                        deps = deps + [ev_prev_list[hc]]
                    S.op("t", lambda t, slot_t=slot_t, iw=iw, hc=hc, off=off, w=w: t.matmul(
                        slot_t[:, off:off + w],
                        wslabs[iw][:, hc * 128:(hc + 1) * 128],
                        Ha[:, hc * TAUM + off: hc * TAUM + off + w],
                        start=(hc == 0), stop=(hc == NH - 1)), deps=deps,
                        inc="t" if (hc == NH - 1 and (off, w) == pcs[-1]) else None)
            ev_mm = ("t", S.cnt["t"])
            ev_l4_mm = ev_mm
            wstate[iw][0] = ev_mm
            osl = ob % 2
            evs = S.op("v", lambda v, slot_t=slot_t, ob=ob, osl=osl, tau=tau, t0=t0: v.scalar_tensor_tensor(
                ostg[:, osl * TAUM: osl * TAUM + tau], slot_t[:, 0:tau],
                s3v[:, ob:ob + 1], Pacc[:, ob * T + t0: ob * T + t0 + tau],
                OP.mult, OP.add), inc="v", deps=[ev_mm, ev_ostg_dma[osl]])
            fps.release(slot, evs)
            if ti == len(taus) - 1 and ob == ND - 1:
                h = tau // 2
                src_a = ostg[:].rearrange("p (c f) -> p c f", f=TAUM)[:, osl:osl + 1, 0:h]
                dst_a = out_d.ap().rearrange("(c p) t -> p c t", p=128)[:, ob:ob + 1, t0:t0 + h]
                src_b = ostg[:].rearrange("p (c f) -> p c f", f=TAUM)[:, osl:osl + 1, h:tau]
                dst_b = out_d.ap().rearrange("(c p) t -> p c t", p=128)[:, ob:ob + 1, t0 + h:t0 + tau]
                S.op("y", lambda y, src_a=src_a, dst_a=dst_a: y.dma_start(
                    out=dst_a, in_=src_a), inc="o", inc_n=16, deps=[evs])
                ev_ostg_dma[osl] = S.op("g", lambda g, src_b=src_b, dst_b=dst_b: g.dma_start(
                    out=dst_b, in_=src_b), inc="o", inc_n=16, deps=[evs])
            else:
                src_ap = ostg[:].rearrange("p (c f) -> p c f", f=TAUM)[:, osl:osl + 1, 0:tau]
                dst_ap = out_d.ap().rearrange("(c p) t -> p c t", p=128)[:, ob:ob + 1, t0:t0 + tau]
                ev_ostg_dma[osl] = S.op("y", lambda y, src_ap=src_ap, dst_ap=dst_ap: y.dma_start(
                    out=dst_ap, in_=src_ap), inc="o", inc_n=16, deps=[evs])
        ev_l4_mm_prev = ev_l4_mm
        t0 += tau

    S.wait("y", [("o", S.cnt["o"])])
    S.flush()
    stC.close()
    stP.close()

    return nc



def _block_weight(W):
    O, K = W.shape
    Wt = np.ascontiguousarray(W.T)
    a = Wt.reshape(K // 128, 128, O // 128, 128)
    return np.ascontiguousarray(a.transpose(1, 2, 0, 3)).reshape(128, (O // 128) * K)


def _moving_weight(W):
    Wt = np.ascontiguousarray(W.T)
    Dd, E = Wt.shape
    NEH, EW = max(E // 512, 1), min(E, 512)
    a = Wt.reshape(Dd // 128, 128, NEH, EW)
    return np.ascontiguousarray(a.transpose(1, 2, 0, 3)).reshape(128, NEH * (Dd // 128) * EW)


def _scale_vec(s):
    return np.ascontiguousarray(np.asarray(s, dtype=np.float32).reshape(-1, 128).T)


def _tables(C, NKT):
    p = np.arange(128, dtype=np.float32)
    prow = np.broadcast_to(np.arange(C, dtype=np.float32), (128, C)).copy()
    revw = np.empty((128, NKT), dtype=np.float32)
    difft = np.empty((128, NKT * C), dtype=np.float32)
    causal = np.empty((128, NKT * C), dtype=np.float32)
    cc = np.arange(C, dtype=np.float32)
    for kt in range(NKT):
        kg = kt * 128 + p
        revw[:, kt] = (C - 1) - kg
        d = cc[None, :] - 1.0 - kg[:, None]
        difft[:, kt * C:(kt + 1) * C] = np.maximum(d, 0.0)
        causal[:, kt * C:(kt + 1) * C] = (cc[None, :] > kg[:, None]).astype(np.float32)
    return prow, revw, difft, causal


def make_in_maps(inputs, C=512):
    NKT = C // 128
    x = np.asarray(inputs["L0_attn_out"], dtype=np.float32)
    B = x.shape[0]
    prow, revw, difft, causal = _tables(C, NKT)
    dec = np.full((128, 1), np.float32(np.asarray(inputs["L0_attn_decay"], dtype=np.float32)), dtype=np.float32)
    la = np.full((128, 1), np.float32(np.asarray(inputs["L0_attn_log_alpha"], dtype=np.float32)), dtype=np.float32)
    shared = {
        "wwt": _moving_weight(np.asarray(inputs["L0_attn_W_write"], dtype=np.float32)),
        "wrt": _block_weight(np.asarray(inputs["L0_attn_W_read"], dtype=np.float32)),
        "w1": _block_weight(np.asarray(inputs["L0_ffn_w1_w"], dtype=np.float32)),
        "w2": _block_weight(np.asarray(inputs["L0_ffn_w2_w"], dtype=np.float32)),
        "h1": _block_weight(np.asarray(inputs["L0_ffn_h1_w"], dtype=np.float32)),
        "h2": _block_weight(np.asarray(inputs["L0_ffn_h2_w"], dtype=np.float32)),
        "w3": _block_weight(np.asarray(inputs["L0_ffn_w3_w"], dtype=np.float32)),
        "svd": np.ascontiguousarray(np.concatenate([
            _scale_vec(inputs["L0_ffn_w1_s"]), _scale_vec(inputs["L0_ffn_w2_s"]),
            _scale_vec(inputs["L0_ffn_h1_s"]), _scale_vec(inputs["L0_ffn_h2_s"]),
            _scale_vec(inputs["L0_ffn_w3_s"]), dec, la], axis=1)),
        "csd": np.ascontiguousarray(np.concatenate([prow, revw, difft, causal], axis=1)),
        "ident_d": np.eye(128, dtype=np.float32),
    }
    maps = []
    NQ, QW = 4, 512
    for b in range(B):
        m = dict(shared)
        xt = x[b].T
        Dd, T = xt.shape
        m["xT"] = np.ascontiguousarray(
            xt.reshape(Dd // 128, 128, NQ, QW).transpose(1, 2, 0, 3).reshape(128, NQ * (Dd // 128) * QW))
        maps.append(m)
    return maps


def kernel(**inputs):
    nc = build_nc()
    in_maps = make_in_maps(inputs)
    res = run_bass_kernel_spmd(nc, in_maps, core_ids=list(range(8)))
    out = np.stack([np.ascontiguousarray(r["out"].T) for r in res.results])
    return out.astype(np.float32)


# version 35
# speedup vs baseline: 1.1055x; 1.0010x over previous
import contextlib
import numpy as np
import concourse.bass as bass
import concourse.mybir as mybir
from concourse.bass_utils import run_bass_kernel_spmd

F32 = mybir.dt.float32
BF16 = mybir.dt.bfloat16
AT = mybir.ActivationFunctionType
OP = mybir.AluOpType

ENGS = (("t", "tensor"), ("v", "vector"), ("s", "scalar"), ("g", "gpsimd"), ("y", "sync"))

OWN_SEM = {"t": "t", "v": "v", "s": "s"}


class Sched:

    def __init__(self, nc, semnames):
        self.nc = nc
        self.sems = {}
        for n in semnames:
            self.sems[n] = nc.semaphore(n).__enter__()
        self.cnt = {n: 0 for n in semnames}
        self.q = {k: [] for k, _ in ENGS}
        self.wm = {k: {} for k, _ in ENGS}

    def op(self, eng, build, inc=None, inc_n=1, deps=()):
        self.wait(eng, deps)
        ev = None
        semh = None
        if inc is not None:
            self.cnt[inc] += inc_n
            ev = (inc, self.cnt[inc])
            semh = self.sems[inc]
        def run(e, build=build, semh=semh, inc_n=inc_n):
            ins = build(e)
            if semh is not None:
                ins.then_inc(semh, inc_n)
        self.q[eng].append(run)
        return ev

    def wait(self, eng, deps):
        own = OWN_SEM.get(eng)
        for d in deps:
            if d is None:
                continue
            sem, val = d
            if val is None or val <= 0:
                continue
            if sem == own:
                continue
            if self.wm[eng].get(sem, 0) >= val:
                continue
            self.wm[eng][sem] = val
            h = self.sems[sem]
            self.q[eng].append(lambda e, h=h, v=val: e.wait_ge(h, v))

    COMPUTE_SEMS = ("t", "v", "s")

    def flush(self):
        with self.nc.Block() as blk:
            for key, attr in ENGS:
                q = self.q[key]
                if not q:
                    continue
                getattr(blk, attr)(lambda e, q=q: [f(e) for f in q])
        self.q = {k: [] for k, _ in ENGS}
        for k, _ in ENGS:
            for sname in self.COMPUTE_SEMS:
                if sname in self.cnt:
                    self.wm[k][sname] = self.cnt[sname]


class Pool:

    def __init__(self, slots):
        self.slots = [[s, None] for s in slots]
        self.i = 0

    def acquire(self):
        slot = self.slots[self.i % len(self.slots)]
        self.i += 1
        t, ev = slot
        return t, ([ev] if ev else []), slot

    def release(self, slot, ev):
        slot[1] = ev


def build_nc(T=2048, D=1024, HID=4096, C=512, taus=(512, 768, 768)):
    assert T % C == 0 and C % 128 == 0 and D % 128 == 0 and HID % 128 == 0
    assert sum(taus) == T
    ND = D // 128
    NH = HID // 128
    NKT = C // 128
    NCH = T // C
    CXN = T + 16
    TAUM = max(taus)
    EPS = 1e-5

    nc = bass.Bass("TRN2", target_bir_lowering=False)

    xT = nc.dram_tensor("xT", [128, (T // 512) * (D // 128) * 512], F32, kind="ExternalInput")
    wwt = nc.dram_tensor("wwt", [128, ND * D], F32, kind="ExternalInput")
    wrt = nc.dram_tensor("wrt", [128, ND * D], F32, kind="ExternalInput")
    w1 = nc.dram_tensor("w1", [128, NH * D], F32, kind="ExternalInput")
    w2 = nc.dram_tensor("w2", [128, NH * D], F32, kind="ExternalInput")
    h1 = nc.dram_tensor("h1", [128, NH * HID], F32, kind="ExternalInput")
    h2 = nc.dram_tensor("h2", [128, NH * HID], F32, kind="ExternalInput")
    w3 = nc.dram_tensor("w3", [128, ND * HID], F32, kind="ExternalInput")
    NSV = 4 * NH + ND + 2
    NCS = C + NKT + 2 * NKT * C
    svd = nc.dram_tensor("svd", [128, NSV], F32, kind="ExternalInput")
    csd = nc.dram_tensor("csd", [128, NCS], F32, kind="ExternalInput")
    ident_d = nc.dram_tensor("ident_d", [128, 128], F32, kind="ExternalInput")
    out_d = nc.dram_tensor("out", [D, T], F32, kind="ExternalOutput")

    S = Sched(nc, ["t", "v", "s", "g", "gw", "o", "gcs", "gid",
                   "x0", "x1", "x2", "x3", "x4", "x5", "x6", "x7",
                   "w0", "w1s", "w2s", "l0", "l1", "l2", "l3", "l4", "l5"])
    wsems = ["w0", "w1s", "w2s"]

    def sb(st, name, shape, dt):
        return st.enter_context(nc.sbuf_tensor(name, shape, dt))

    def ps(st, name, shape, dt=F32):
        return st.enter_context(nc.psum_tensor(name, shape, dt))

    stP = contextlib.ExitStack()
    XN = sb(stP, "XN", [128, ND * CXN], BF16)
    Pacc = sb(stP, "Pacc", [128, ND * T], BF16)
    gam_v = sb(stP, "gam_v", [128, 1], F32)
    lg_v = sb(stP, "lg_v", [128, 1], F32)
    al_v = sb(stP, "al_v", [128, 1], F32)
    gc_v = sb(stP, "gc_v", [128, 1], F32)
    SV = sb(stP, "SV", [128, NSV], F32)
    s1v = SV[:, 0:NH]
    s2v = SV[:, NH:2 * NH]
    sh1v = SV[:, 2 * NH:3 * NH]
    sh2v = SV[:, 3 * NH:4 * NH]
    s3v = SV[:, 4 * NH:4 * NH + ND]
    dec_v = SV[:, 4 * NH + ND:4 * NH + ND + 1]
    lad_v = SV[:, 4 * NH + ND + 1:4 * NH + ND + 2]
    eps_v = sb(stP, "eps_v", [128, 1], F32)
    cc_v = sb(stP, "cc_v", [128, 1], F32)
    invd_v = sb(stP, "invd_v", [128, 1], F32)
    nh_v = sb(stP, "nh_v", [128, 1], F32)
    wslabs_l = [sb(stP, "wl%d" % i, [128, D], BF16) for i in range(6)]

    stB = contextlib.ExitStack()
    Mt = sb(stB, "Mt", [128, NKT * C], BF16)
    GPbf = sb(stB, "GPbf", [128, C], BF16)
    gwv = sb(stB, "gwv", [128, NKT], F32)
    identb = sb(stB, "identb", [128, 128], BF16)
    onesb = sb(stB, "onesb", [128, 128], BF16)
    Wwt_sb = sb(stB, "Wwt_sb", [128, ND * D], BF16)
    Wrt_sb = sb(stB, "Wrt_sb", [128, ND * D], BF16)
    Vtm = sb(stB, "Vtm", [128, NKT * D], BF16)
    WKtm = sb(stB, "WKtm", [128, NKT * D], BF16)
    Stm = sb(stB, "Stm", [128, NKT * C], BF16)

    stA = contextlib.ExitStack()
    MEAN = sb(stA, "MEAN", [128, T], F32)
    XBF = sb(stA, "XBF", [128, ND * T], BF16)
    Xq0 = sb(stA, "Xq0", [128, T], BF16)
    RSTDb = sb(stA, "RSTDb", [128, T], BF16)
    MRb = sb(stA, "MRb", [128, T], BF16)
    NQ = max(T // 512, 1)
    QW = T // NQ
    psA = [ps(stA, "psA%d" % q, [128, QW]) for q in range(NQ)]
    psB = [ps(stA, "psB%d" % q, [128, QW]) for q in range(NQ)]
    stT = contextlib.ExitStack()
    CS = sb(stT, "CS", [128, NCS], F32)
    prow_s = CS[:, 0:C]
    revw_s = CS[:, C:C + NKT]
    difft_s = CS[:, C + NKT:C + NKT + NKT * C]
    causal_s = CS[:, C + NKT + NKT * C:NCS]
    assert nc.sbuf_bytes_remaining >= 0

    ev_g = S.op("y", lambda y: y.dma_start(out=SV[:], in_=svd[:]), inc="g", inc_n=16)
    ev_cs = S.op("y", lambda y: y.dma_start(out=CS[:], in_=csd[:]), inc="gcs", inc_n=16)
    ev_id = S.op("g", lambda g: g.dma_start(out=identb[:], in_=ident_d[:]), inc="gid", inc_n=16)
    assert QW == C
    NEH = max(D // 512, 1)
    EW = min(D, 512)
    HWW = ND * EW
    ev_ldq = {}
    for q in range(NQ):
        if q == NQ - 1:
            S.op("g", lambda g: g.dma_start(out=Wwt_sb[:, 0:HWW], in_=wwt[:, 0:HWW]), inc="gw", inc_n=16)
            S.op("g", lambda g: g.dma_start(out=Wwt_sb[:, HWW:2 * HWW], in_=wwt[:, HWW:2 * HWW]), inc="gw", inc_n=16)
        S.op("g", lambda g, q=q: g.dma_start(
            out=XBF[:, q * ND * QW:(q + 1) * ND * QW],
            in_=xT[:, q * ND * QW:(q + 1) * ND * QW]), inc="x%d" % q, inc_n=16)
        ev_ldq[q] = ("x%d" % q, 16)
    ev_gww = [("gw", 16), ("gw", 32)]
    ev_gw = S.op("g", lambda g: g.dma_start(out=Wrt_sb[:], in_=wrt[:]), inc="gw", inc_n=16)

    def xbf(q, dc):
        return XBF[:, (q * ND + dc) * QW:(q * ND + dc + 1) * QW]

    ev_eps = S.op("v", lambda v: v.memset(eps_v[:], EPS), inc="v")
    ev_cc = S.op("v", lambda v: v.memset(cc_v[:], float(C)), inc="v")
    S.op("v", lambda v: v.memset(invd_v[:], 1.0 / D), inc="v")
    ev_nh = S.op("v", lambda v: v.memset(nh_v[:], -0.5), inc="v")
    ev_gam = S.op("s", lambda s: s.activation(gam_v[:], dec_v[:], AT.Sigmoid), inc="s", deps=[ev_g])
    ev_lg = S.op("s", lambda s: s.activation(lg_v[:], gam_v[:], AT.Ln), inc="s", deps=[ev_gam])
    S.op("s", lambda s: s.activation(al_v[:], lad_v[:], AT.Exp), inc="s", deps=[ev_lg])
    S.op("s", lambda s: s.activation(gc_v[:], lg_v[:], AT.Exp, scale=cc_v[:, 0:1]), inc="s", deps=[ev_lg, ev_cc])
    S.op("s", lambda s: s.activation(GPbf[:], prow_s[:], AT.Exp, scale=lg_v[:, 0:1]), inc="s", deps=[ev_lg, ev_cs])
    S.op("s", lambda s: s.activation(gwv[:], revw_s[:], AT.Exp, scale=lg_v[:, 0:1]), inc="s", deps=[ev_lg])
    ev_me = S.op("s", lambda s: s.activation(difft_s[:], difft_s[:], AT.Exp, scale=lg_v[:, 0:1]), inc="s", deps=[ev_lg])
    ev_ones = S.op("v", lambda v: v.memset(onesb[:], 1.0), inc="v")
    ev_z = S.op("v", lambda v: v.memset(XN[:].rearrange("p (c f) -> p c f", f=CXN)[:, :, 0:8], 0.0), inc="v")
    ev_mt = S.op("v", lambda v: v.tensor_tensor(Mt[:], difft_s[:], causal_s[:], OP.mult), inc="v", deps=[ev_me])

    stT.close()

    ev_sums_q = {}
    sq_last = {0: None, 1: None}

    def emit_sums(q):
        mm = None
        for dc in range(ND):
            b = dc % 2
            sq = S.op("v", lambda v, q=q, dc=dc, b=b: v.tensor_tensor(
                Xq0[:, b * QW:(b + 1) * QW], xbf(q, dc), xbf(q, dc), OP.mult),
                inc="v", deps=[ev_ldq[q], sq_last[b]])
            S.op("t", lambda t, q=q, dc=dc: t.matmul(
                psA[q][:, 0:QW], onesb[:], xbf(q, dc),
                start=(dc == 0), stop=(dc == ND - 1)),
                deps=[ev_ldq[q], ev_ones] if dc == 0 else [])
            mm = S.op("t", lambda t, q=q, dc=dc, b=b: t.matmul(
                psB[q][:, 0:QW], onesb[:], Xq0[:, b * QW:(b + 1) * QW],
                start=(dc == 0), stop=(dc == ND - 1)), deps=[sq], inc="t")
            sq_last[b] = mm
        ev_sums_q[q] = mm

    ev_ea = [None] * NQ
    ev_rsq = [None] * NQ
    ev_mrb = [None] * NQ

    def stats_q(q):
        sl = slice(q * QW, (q + 1) * QW)
        ev_ea[q] = S.op("v", lambda v, q=q, sl=sl: v.tensor_scalar_mul(
            MEAN[:, sl], psA[q][:, 0:QW], 1.0 / D), inc="v", deps=[ev_sums_q[q]])
        e3 = S.op("s", lambda s, q=q, sl=sl: s.activation(
            MRb[:, sl], MEAN[:, sl], AT.Square), inc="s", deps=[ev_ea[q]])
        e4 = S.op("v", lambda v, q=q, sl=sl: v.scalar_tensor_tensor(
            psB[q][:, 0:QW], psB[q][:, 0:QW], invd_v[:, 0:1], MRb[:, sl],
            OP.mult, OP.subtract), inc="v", deps=[e3])
        e5 = S.op("s", lambda s, q=q: s.activation(
            psB[q][:, 0:QW], psB[q][:, 0:QW], AT.Ln, bias=eps_v[:, 0:1]), inc="s", deps=[e4, ev_eps])
        ev_rsq[q] = S.op("s", lambda s, q=q, sl=sl: s.activation(
            RSTDb[:, sl], psB[q][:, 0:QW], AT.Exp, scale=nh_v[:, 0:1]), inc="s", deps=[e5, ev_nh])
        ev_mrb[q] = S.op("v", lambda v, q=q, sl=sl: v.tensor_tensor(
            MRb[:, sl], MEAN[:, sl], RSTDb[:, sl], OP.mult), inc="v", deps=[ev_rsq[q]])

    ev_xn0_dc = {}

    def p2_chunk0():
        scr = Xq0[:, 2 * QW: 2 * QW + C]
        for dc in range(ND):
            o2 = dc * CXN + 8
            S.op("v", lambda v, dc=dc, scr=scr: v.tensor_tensor(
                scr, xbf(0, dc), RSTDb[:, 0:C], OP.mult), inc="v",
                deps=[ev_rsq[0]])
            ev_xn0_dc[dc] = S.op("v", lambda v, o2=o2, scr=scr: v.tensor_tensor(
                XN[:, o2: o2 + C], scr, MRb[:, 0:C], OP.subtract), inc="v",
                deps=[ev_mrb[0], ev_z])

    ev_xn_dc = {}
    ev_p2q = [None] * NQ
    ev_pacc0 = [None]

    def emit_pacc0():
        prev = None
        for dc in range(ND):
            o2 = dc * CXN + 8
            prev = S.op("v", lambda v, dc=dc, o2=o2: v.tensor_tensor(
                Pacc[:, dc * T: dc * T + C], XN[:, o2: o2 + C], xbf(0, dc), OP.add),
                inc="v", deps=[])
        ev_pacc0[0] = prev

    def p2_rest(q):
        scr = Xq0[:, 0:C]
        prev = None
        for dc in range(ND):
            o2 = dc * CXN + 8 + q * C
            S.op("v", lambda v, q=q, dc=dc, scr=scr: v.tensor_tensor(
                scr, xbf(q, dc), RSTDb[:, q * QW:(q + 1) * QW], OP.mult), inc="v",
                deps=[ev_rsq[q]])
            ev_xn_dc[(q, dc)] = S.op("v", lambda v, o2=o2, scr=scr, q=q: v.tensor_tensor(
                XN[:, o2: o2 + C], scr, MRb[:, q * QW:(q + 1) * QW], OP.subtract),
                inc="v", deps=[ev_mrb[q]])
            prev = S.op("v", lambda v, q=q, dc=dc, o2=o2: v.tensor_tensor(
                Pacc[:, dc * T + q * C: dc * T + (q + 1) * C], XN[:, o2: o2 + C],
                xbf(q, dc), OP.add), inc="v", deps=[])
        ev_p2q[q] = prev

    stT.close()
    stA.close()

    stS = contextlib.ExitStack()
    VG = sb(stS, "VG", [128, NKT * D], BF16)
    RKG = sb(stS, "RKG", [128, ND * C], BF16)
    RDS = sb(stS, "RDS", [128, ND * C], BF16)
    Af = sb(stS, "Af", [128, ND * D], F32)
    Abf = sb(stS, "Abf", [128, ND * D], BF16)
    tps = [ps(stS, "tp%d" % i, [128, 1024], BF16) for i in range(2)]
    qs = [ps(stS, "q%d" % i, [128, 512]) for i in range(5)]
    assert nc.sbuf_bytes_remaining >= 0

    p512 = Pool([qs[2], qs[3], qs[0], qs[4], qs[1]])
    ptp = Pool(tps)
    ev_intra_prev = None
    ev_vg_prev = None
    ev_da_prev = None
    ev_abf_prev = None
    ev_ro_prev = None

    def xdep(c, dc):
        if c == 0:
            return ev_xn0_dc[dc]
        return ev_xn_dc[(c, dc)]

    ev_S_groups = {}
    ev_ve = {}
    ev_wk = {}
    ve_l = {}
    wk_l = {}
    mask_l = {}
    vg_l = {}
    intra_l = {}
    abf_l = {}
    DTH = ND // NEH

    def emit_v(c):
        base = c * C
        last = None
        ve_l[c] = {}
        for eh in range(NEH):
            for tt in range(NKT):
                slot_t, sdeps, slot = p512.acquire()
                for dc in range(ND):
                    deps = [xdep(c, dc)]
                    if dc == 0:
                        deps += sdeps + [ev_gww[eh]]
                    S.op("t", lambda t, slot_t=slot_t, tt=tt, eh=eh, dc=dc, base=base: t.matmul(
                        slot_t[:, 0:EW],
                        XN[:, dc * CXN + 8 + base + tt * 128: dc * CXN + 8 + base + tt * 128 + 128],
                        Wwt_sb[:, eh * HWW + dc * EW: eh * HWW + dc * EW + EW],
                        start=(dc == 0), stop=(dc == ND - 1)),
                        deps=deps,
                        inc="t" if dc == ND - 1 else None)
                evm = ("t", S.cnt["t"])
                dprev = []
                if c > 0:
                    dprev = [intra_l[c - 1][eh * DTH + DTH - 1], vg_l[c - 1][tt]]
                last = S.op("s", lambda s, slot_t=slot_t, tt=tt, eh=eh: s.copy(
                    Vtm[:, tt * D + eh * EW: tt * D + eh * EW + EW], slot_t[:, 0:EW]),
                    inc="s", deps=[evm] + dprev)
                ve_l[c][(tt, eh)] = last
                p512.release(slot, last)
        ev_ve[c] = last

    def emit_wk(c):
        base = c * C
        last = None
        wk_l[c] = []
        for kt in range(NKT):
            slot_t, sdeps, slot = ptp.acquire()
            for dc in range(ND):
                deps = [xdep(c, dc)]
                if dc == 0:
                    deps += sdeps
                S.op("t", lambda t, slot_t=slot_t, kt=kt, dc=dc, base=base: t.transpose(
                    slot_t[:, dc * 128:(dc + 1) * 128],
                    XN[:, dc * CXN + 7 + base + kt * 128: dc * CXN + 7 + base + kt * 128 + 128],
                    identb[:]), deps=deps + ([ev_id] if dc == 0 else []),
                    inc="t" if dc == ND - 1 else None)
            evm = ("t", S.cnt["t"])
            last = S.op("s", lambda s, slot_t=slot_t, kt=kt: s.copy(
                WKtm[:, kt * D:(kt + 1) * D], slot_t[:, 0:ND * 128]), inc="s",
                deps=[evm, ev_da_prev])
            wk_l[c].append(last)
            ptp.release(slot, last)
        ev_wk[c] = last

    def emit_S(c):
        base = c * C
        groups = []
        for kt in range(NKT):
            slot_t, sdeps, slot = p512.acquire()
            for dc in range(ND):
                deps = [xdep(c, dc)]
                if dc == 0:
                    deps += sdeps
                S.op("t", lambda t, slot_t=slot_t, kt=kt, dc=dc, base=base: t.matmul(
                    slot_t[:, 0:C],
                    XN[:, dc * CXN + 7 + base + kt * 128: dc * CXN + 7 + base + kt * 128 + 128],
                    XN[:, dc * CXN + 8 + base: dc * CXN + 8 + base + C],
                    start=(dc == 0), stop=(dc == ND - 1)),
                    deps=deps,
                    inc="t" if dc == ND - 1 else None)
            groups.append((slot_t, slot, ("t", S.cnt["t"])))
        ev_S_groups[c] = groups

    ev_masks = {}

    def emit_masks(c):
        last = None
        mask_l[c] = []
        for kt, (slot_t, slot, evm) in enumerate(ev_S_groups.pop(c)):
            last = S.op("v", lambda v, slot_t=slot_t, kt=kt: v.tensor_tensor(
                Stm[:, kt * C:(kt + 1) * C], slot_t[:, 0:C], Mt[:, kt * C:(kt + 1) * C], OP.mult),
                inc="v", deps=[evm, ev_intra_prev, ev_mt])
            mask_l[c].append(last)
            p512.release(slot, last)
        ev_masks[c] = last

    emit_sums(0)
    stats_q(0)
    emit_sums(1)
    stats_q(1)
    p2_chunk0()
    ptp.slots[0][1] = ev_ea[0]
    ptp.slots[1][1] = ev_ea[1]
    emit_wk(0)
    emit_sums(2)
    stats_q(2)
    p512.slots[0][1] = ev_rsq[0]
    p512.slots[1][1] = ev_rsq[1]
    p512.slots[2][1] = ev_ea[2]
    p512.slots[3][1] = ev_rsq[2]
    emit_S(0)
    emit_sums(3)
    stats_q(3)
    p512.slots[4][1] = ev_ea[3]
    emit_masks(0)
    p2_rest(1)
    emit_pacc0()
    emit_v(0)

    for c in range(NCH):
        base = c * C
        need_state = (c < NCH - 1) or NCH == 1
        ev_rkg_l = []
        if c > 0:
            for jc in range(ND):
                ev_rkg_l.append(S.op("v", lambda v, jc=jc, base=base: v.tensor_tensor(
                    RKG[:, jc * C:(jc + 1) * C],
                    XN[:, jc * CXN + 8 + base: jc * CXN + 8 + base + C],
                    GPbf[:], OP.mult), inc="v", deps=[ev_intra_prev, ev_pacc0[0]]))
        ev_st_last = ev_masks[c]

        ev_vg = None
        if need_state:
            vg_l[c] = []
            for tt in range(NKT):
                ev_vg = S.op("s", lambda s, tt=tt: s.activation(
                    VG[:, tt * D:(tt + 1) * D], Vtm[:, tt * D:(tt + 1) * D],
                    AT.Identity, scale=gwv[:, tt:tt + 1]), inc="s",
                    deps=[ve_l[c][(tt, 0)], ve_l[c][(tt, 1)], ev_da_prev, ev_mrb[NQ - 1]])
                vg_l[c].append(ev_vg)
            ev_vg_prev = ev_vg

        ev_intra = None
        ev_rd_last = None
        intra_l[c] = []
        rd_l = []
        for dt in range(ND):
            slot_t, sdeps, slot = p512.acquire()
            for kt in range(NKT):
                deps = []
                if kt == 0:
                    deps = list(sdeps)
                if dt % DTH == 0:
                    deps = deps + [mask_l[c][kt], ve_l[c][(kt, dt // DTH)]]
                S.op("t", lambda t, slot_t=slot_t, dt=dt, kt=kt, c=c: t.matmul(
                    slot_t[:, 0:C],
                    Vtm[:, kt * D + dt * 128: kt * D + dt * 128 + 128],
                    Stm[:, kt * C:(kt + 1) * C],
                    start=(kt == 0), stop=(kt == NKT - 1 and c == 0)), deps=deps,
                    inc="t" if (c == 0 and kt == NKT - 1) else None)
            if c > 0:
                for jc in range(ND):
                    deps = []
                    if dt == 0:
                        deps = [ev_rkg_l[jc], abf_l[c - 1][jc]]
                    S.op("t", lambda t, slot_t=slot_t, dt=dt, jc=jc: t.matmul(
                        slot_t[:, 0:C],
                        Abf[:, jc * D + dt * 128: jc * D + dt * 128 + 128],
                        RKG[:, jc * C:(jc + 1) * C],
                        start=False, stop=(jc == ND - 1)), deps=deps,
                        inc="t" if jc == ND - 1 else None)
            ev_intra = ("t", S.cnt["t"])
            intra_l[c].append(ev_intra)
            ev_rd_last = S.op("s", lambda s, slot_t=slot_t, dt=dt: s.copy(
                RDS[:, dt * C:(dt + 1) * C], slot_t[:, 0:C]), inc="s",
                deps=[ev_intra, ev_ro_prev, ev_p2q[1]])
            rd_l.append(ev_rd_last)
            p512.release(slot, ev_rd_last)
        ev_intra_prev = ev_intra

        def emit_da(c=c):
            ev_af = []
            ev_da = None
            for jc in range(ND):
                slots = []
                for ih in range(NEH):
                    slot_t, sdeps, slot = p512.acquire()
                    for kt in range(NKT):
                        deps = list(sdeps) if kt == 0 else []
                        if jc == 0 and ih == 0:
                            deps = deps + [wk_l[c][kt], vg_l[c][kt]]
                        S.op("t", lambda t, slot_t=slot_t, jc=jc, ih=ih, kt=kt: t.matmul(
                            slot_t[:, 0:EW],
                            WKtm[:, kt * D + jc * 128: kt * D + jc * 128 + 128],
                            VG[:, kt * D + ih * EW: kt * D + ih * EW + EW],
                            start=(kt == 0), stop=(kt == NKT - 1)), deps=deps,
                            inc="t" if kt == NKT - 1 else None)
                    slots.append((ih, slot_t, slot, ("t", S.cnt["t"])))
                ev_da = slots[-1][3]
                for (ih, slot_t, slot, evm) in slots:
                    sl = slice(jc * D + ih * EW, jc * D + ih * EW + EW)
                    if c == 0:
                        evf = S.op("s", lambda s, slot_t=slot_t, sl=sl: s.copy(
                            Af[:, sl], slot_t[:, 0:EW]), inc="s",
                            deps=[evm, ev_p2q[NQ - 1]])
                    else:
                        evf = S.op("v", lambda v, slot_t=slot_t, sl=sl: v.scalar_tensor_tensor(
                            Af[:, sl], Af[:, sl], gc_v[:, 0:1], slot_t[:, 0:EW],
                            OP.mult, OP.add), inc="v", deps=[evm])
                    ev_af.append(evf)
                    p512.release(slot, evf)
            return ev_da, ev_af

        def emit_ro(base=base):
            ev_ro = None
            for et in range(ND):
                slot_t, sdeps, slot = p512.acquire()
                for it in range(ND):
                    deps = list(sdeps) + [ev_gw] if it == 0 else []
                    if et == 0:
                        deps = deps + [rd_l[it]]
                    S.op("t", lambda t, slot_t=slot_t, et=et, it=it: t.matmul(
                        slot_t[:, 0:C],
                        Wrt_sb[:, et * D + it * 128: et * D + it * 128 + 128],
                        RDS[:, it * C:(it + 1) * C],
                        start=(it == 0), stop=(it == ND - 1)), deps=deps,
                        inc="t" if it == ND - 1 else None)
                ev_ro = ("t", S.cnt["t"])
                evp = S.op("v", lambda v, slot_t=slot_t, et=et, base=base: v.scalar_tensor_tensor(
                    Pacc[:, et * T + base: et * T + base + C], slot_t[:, 0:C],
                    al_v[:, 0:1], Pacc[:, et * T + base: et * T + base + C],
                    OP.mult, OP.add), inc="v", deps=[ev_ro])
                p512.release(slot, evp)
            return ev_ro

        if c == 0 and need_state:
            for q in range(2, NQ):
                p2_rest(q)
            ev_ro_prev = emit_ro()
            ev_da_prev, ev_af = emit_da()
        else:
            if need_state:
                ev_da_prev, ev_af = emit_da()
            ev_ro_prev = emit_ro()

        if need_state:
            ev_abf = None
            abf_l[c] = []
            for jc in range(ND):
                deps = [ev_af[jc * NEH + NEH - 1]]
                if c > 0:
                    deps.append(ev_intra_prev)
                ev_abf = S.op("s", lambda s, jc=jc: s.copy(
                    Abf[:, jc * D:(jc + 1) * D], Af[:, jc * D:(jc + 1) * D]), inc="s", deps=deps)
                abf_l[c].append(ev_abf)
            ev_abf_prev = ev_abf

        if c + 1 < NCH:
            emit_v(c + 1)
            if c + 1 < NCH - 1:
                emit_wk(c + 1)
            emit_S(c + 1)
            emit_masks(c + 1)

    lsems = ["l0", "l1", "l2", "l3", "l4", "l5"]
    lstate = [[None, 0] for _ in range(6)]
    lidx = [0]
    pre_l = []

    def issue_slab_l(src, ob):
        i = lidx[0] % 6
        lidx[0] += 1
        consumer, n = lstate[i]
        S.op("g", lambda g, i=i, src=src, ob=ob: g.dma_start(
            out=wslabs_l[i][:], in_=src[:, ob * D:(ob + 1) * D]),
            inc=lsems[i], inc_n=16, deps=[consumer] if consumer else [])
        lstate[i][1] = n + 1
        return i, (lsems[i], 16 * (n + 1))

    OB_ORDER = [NH - 1] + list(range(NH - 1))
    for (psrc, pob) in ((w1, OB_ORDER[0]), (w2, OB_ORDER[0]), (w1, OB_ORDER[1]), (w2, OB_ORDER[1])):
        pre_l.append(issue_slab_l(psrc, pob))

    S.flush()
    stS.close()
    stB.close()

    stC = contextlib.ExitStack()
    Ha = sb(stC, "Ha", [128, NH * TAUM], BF16)
    Hb = sb(stC, "Hb", [128, NH * TAUM], BF16)
    wslabs = [sb(stC, "wsl%d" % i, [128, HID], BF16) for i in range(3)]
    sa_tiles = [sb(stC, "sa%d" % i, [128, TAUM], BF16) for i in range(2)]
    ostg = sb(stC, "ostg", [128, 2 * TAUM], F32)
    fs = [ps(stC, "f%d" % i, [128, TAUM]) for i in range(4)]
    assert nc.sbuf_bytes_remaining >= 0

    fps = Pool(fs)
    sas = Pool(sa_tiles)
    wstate = [[None, 0] for _ in range(3)]
    widx = [0]
    n_pre = [0]

    def load_slab(src, ob, K):
        i = widx[0] % 3
        widx[0] += 1
        consumer, n = wstate[i]
        S.op("g", lambda g, i=i, src=src, ob=ob, K=K: g.dma_start(
            out=wslabs[i][:, 0:K], in_=src[:, ob * K:(ob + 1) * K]),
            inc=wsems[i], inc_n=16, deps=[consumer] if consumer else [])
        wstate[i][1] = n + 1
        return i, (wsems[i], 16 * (n + 1))

    def load_slab_l(src, ob):
        if n_pre[0] < len(pre_l):
            i, ev = pre_l[n_pre[0]]
            n_pre[0] += 1
            return i, ev
        return issue_slab_l(src, ob)

    def pieces(tau):
        out, off = [], 0
        while off < tau:
            w = min(512, tau - off)
            out.append((off, w))
            off += w
        return out

    ev_l4_mm_prev = None
    ev_ostg_dma = [None, None]
    t0 = 0
    for ti, tau in enumerate(taus):
        pcs = pieces(tau)
        ev_h1_list = [None] * NH
        ev_h1 = None
        for ob in OB_ORDER:
            ia, eva = load_slab_l(w1, ob)
            ib, evb = load_slab_l(w2, ob)
            grp = []
            for (wsl_i, evw) in ((ia, eva), (ib, evb)):
                slot_t, sdeps, slot = fps.acquire()
                for dc in range(ND):
                    for (off, w) in pcs:
                        deps = []
                        if dc == 0 and off == 0:
                            deps = list(sdeps) + [evw, ev_l4_mm_prev]
                        S.op("t", lambda t, slot_t=slot_t, wsl_i=wsl_i, dc=dc, off=off, w=w, t0=t0: t.matmul(
                            slot_t[:, off:off + w],
                            wslabs_l[wsl_i][:, dc * 128:(dc + 1) * 128],
                            XN[:, dc * CXN + 8 + t0 + off: dc * CXN + 8 + t0 + off + w],
                            start=(dc == 0), stop=(dc == ND - 1)), deps=deps,
                            inc="t" if (dc == ND - 1 and (off, w) == pcs[-1]) else None)
                grp.append((slot_t, slot, ("t", S.cnt["t"])))
            (pa_t, pa_slot, ev_ma), (pb_t, pb_slot, ev_mb) = grp
            lstate[ia][0] = ev_ma
            lstate[ib][0] = ev_mb
            sa_t, sa_deps, sa_slot = sas.acquire()
            ev_sa = S.op("s", lambda s, sa_t=sa_t, pa_t=pa_t, ob=ob, tau=tau: s.activation(
                sa_t[:, 0:tau], pa_t[:, 0:tau], AT.Silu, scale=s1v[:, ob:ob + 1]),
                inc="s", deps=[ev_ma] + sa_deps)
            ev_h1 = S.op("v", lambda v, pb_t=pb_t, sa_t=sa_t, ob=ob, tau=tau: v.scalar_tensor_tensor(
                Ha[:, ob * TAUM: ob * TAUM + tau], pb_t[:, 0:tau], s2v[:, ob:ob + 1],
                sa_t[:, 0:tau], OP.mult, OP.mult), inc="v",
                deps=[ev_mb, ev_sa, ev_l4_mm_prev])
            ev_h1_list[ob] = ev_h1
            fps.release(pa_slot, ev_sa)
            fps.release(pb_slot, ev_h1)
            sas.release(sa_slot, ev_h1)

        ev_prev_list = ev_h1_list
        ev_l2_mm = None
        for li, (wsrc, ssrc, Hin, Hout) in enumerate(((h1, sh1v, Ha, Hb), (h2, sh2v, Hb, Ha))):
            ev_hl = None
            ev_hl_list = [None] * NH
            for ob in OB_ORDER:
                iw, evw = load_slab(wsrc, ob, HID)
                slot_t, sdeps, slot = fps.acquire()
                for hc in range(NH):
                    for (off, w) in pcs:
                        deps = []
                        if hc == 0 and off == 0:
                            deps = list(sdeps) + [evw]
                        if ob == OB_ORDER[0] and off == 0:
                            deps = deps + [ev_prev_list[hc]]
                        S.op("t", lambda t, slot_t=slot_t, iw=iw, hc=hc, off=off, w=w, Hin=Hin: t.matmul(
                            slot_t[:, off:off + w],
                            wslabs[iw][:, hc * 128:(hc + 1) * 128],
                            Hin[:, hc * TAUM + off: hc * TAUM + off + w],
                            start=(hc == 0), stop=(hc == NH - 1)), deps=deps,
                            inc="t" if (hc == NH - 1 and (off, w) == pcs[-1]) else None)
                ev_mm = ("t", S.cnt["t"])
                wstate[iw][0] = ev_mm
                if li == 0:
                    ev_l2_mm = ev_mm
                deps = [ev_mm]
                if li == 1:
                    deps.append(ev_l2_mm)
                ev_hl = S.op("s", lambda s, slot_t=slot_t, ob=ob, tau=tau, ssrc=ssrc, Hout=Hout: s.activation(
                    Hout[:, ob * TAUM: ob * TAUM + tau], slot_t[:, 0:tau], AT.Gelu,
                    scale=ssrc[:, ob:ob + 1]), inc="s", deps=deps)
                ev_hl_list[ob] = ev_hl
                fps.release(slot, ev_hl)
            ev_prev_list = ev_hl_list

        ev_l4_mm = None
        for ob in range(ND):
            final_ob = (ti == len(taus) - 1 and ob == ND - 1)
            iw, evw = load_slab(w3, ob, HID)
            slot_t, sdeps, slot = fps.acquire()
            pev = {}
            if final_ob:
                for (off, w) in pcs:
                    for hc in range(NH):
                        deps = []
                        if hc == 0 and off == 0:
                            deps = list(sdeps) + [evw]
                        S.op("t", lambda t, slot_t=slot_t, iw=iw, hc=hc, off=off, w=w: t.matmul(
                            slot_t[:, off:off + w],
                            wslabs[iw][:, hc * 128:(hc + 1) * 128],
                            Ha[:, hc * TAUM + off: hc * TAUM + off + w],
                            start=(hc == 0), stop=(hc == NH - 1)), deps=deps,
                            inc="t" if hc == NH - 1 else None)
                    pev[off] = ("t", S.cnt["t"])
                ev_mm = pev[pcs[-1][0]]
            else:
                for hc in range(NH):
                    for (off, w) in pcs:
                        deps = []
                        if hc == 0 and off == 0:
                            deps = list(sdeps) + [evw]
                        if ob == 0 and off == 0:
                            deps = deps + [ev_prev_list[hc]]
                        S.op("t", lambda t, slot_t=slot_t, iw=iw, hc=hc, off=off, w=w: t.matmul(
                            slot_t[:, off:off + w],
                            wslabs[iw][:, hc * 128:(hc + 1) * 128],
                            Ha[:, hc * TAUM + off: hc * TAUM + off + w],
                            start=(hc == 0), stop=(hc == NH - 1)), deps=deps,
                            inc="t" if (hc == NH - 1 and (off, w) == pcs[-1]) else None)
                ev_mm = ("t", S.cnt["t"])
            ev_l4_mm = ev_mm
            wstate[iw][0] = ev_mm
            osl = ob % 2
            if final_ob:
                evs = None
                for pi, (off, w) in enumerate(pcs):
                    deps = [pev[off]]
                    if pi == 0:
                        deps.append(ev_ostg_dma[osl])
                    evs = S.op("v", lambda v, slot_t=slot_t, ob=ob, osl=osl, off=off, w=w, t0=t0: v.scalar_tensor_tensor(
                        ostg[:, osl * TAUM + off: osl * TAUM + off + w], slot_t[:, off:off + w],
                        s3v[:, ob:ob + 1], Pacc[:, ob * T + t0 + off: ob * T + t0 + off + w],
                        OP.mult, OP.add), inc="v", deps=deps)
                    if pi < len(pcs) - 1:
                        src_p = ostg[:].rearrange("p (c f) -> p c f", f=TAUM)[:, osl:osl + 1, off:off + w]
                        dst_p = out_d.ap().rearrange("(c p) t -> p c t", p=128)[:, ob:ob + 1, t0 + off:t0 + off + w]
                        S.op("y", lambda y, src_p=src_p, dst_p=dst_p: y.dma_start(
                            out=dst_p, in_=src_p), inc="o", inc_n=16, deps=[evs])
                    else:
                        h = w // 2
                        src_a = ostg[:].rearrange("p (c f) -> p c f", f=TAUM)[:, osl:osl + 1, off:off + h]
                        dst_a = out_d.ap().rearrange("(c p) t -> p c t", p=128)[:, ob:ob + 1, t0 + off:t0 + off + h]
                        src_b = ostg[:].rearrange("p (c f) -> p c f", f=TAUM)[:, osl:osl + 1, off + h:off + w]
                        dst_b = out_d.ap().rearrange("(c p) t -> p c t", p=128)[:, ob:ob + 1, t0 + off + h:t0 + off + w]
                        S.op("y", lambda y, src_a=src_a, dst_a=dst_a: y.dma_start(
                            out=dst_a, in_=src_a), inc="o", inc_n=16, deps=[evs])
                        ev_ostg_dma[osl] = S.op("g", lambda g, src_b=src_b, dst_b=dst_b: g.dma_start(
                            out=dst_b, in_=src_b), inc="o", inc_n=16, deps=[evs])
                fps.release(slot, evs)
            else:
                evs = S.op("v", lambda v, slot_t=slot_t, ob=ob, osl=osl, tau=tau, t0=t0: v.scalar_tensor_tensor(
                    ostg[:, osl * TAUM: osl * TAUM + tau], slot_t[:, 0:tau],
                    s3v[:, ob:ob + 1], Pacc[:, ob * T + t0: ob * T + t0 + tau],
                    OP.mult, OP.add), inc="v", deps=[ev_mm, ev_ostg_dma[osl]])
                fps.release(slot, evs)
                src_ap = ostg[:].rearrange("p (c f) -> p c f", f=TAUM)[:, osl:osl + 1, 0:tau]
                dst_ap = out_d.ap().rearrange("(c p) t -> p c t", p=128)[:, ob:ob + 1, t0:t0 + tau]
                ev_ostg_dma[osl] = S.op("y", lambda y, src_ap=src_ap, dst_ap=dst_ap: y.dma_start(
                    out=dst_ap, in_=src_ap), inc="o", inc_n=16, deps=[evs])
        ev_l4_mm_prev = ev_l4_mm
        t0 += tau

    S.wait("y", [("o", S.cnt["o"])])
    S.flush()
    stC.close()
    stP.close()

    return nc



def _block_weight(W):
    O, K = W.shape
    Wt = np.ascontiguousarray(W.T)
    a = Wt.reshape(K // 128, 128, O // 128, 128)
    return np.ascontiguousarray(a.transpose(1, 2, 0, 3)).reshape(128, (O // 128) * K)


def _moving_weight(W):
    Wt = np.ascontiguousarray(W.T)
    Dd, E = Wt.shape
    NEH, EW = max(E // 512, 1), min(E, 512)
    a = Wt.reshape(Dd // 128, 128, NEH, EW)
    return np.ascontiguousarray(a.transpose(1, 2, 0, 3)).reshape(128, NEH * (Dd // 128) * EW)


def _scale_vec(s):
    return np.ascontiguousarray(np.asarray(s, dtype=np.float32).reshape(-1, 128).T)


def _tables(C, NKT):
    p = np.arange(128, dtype=np.float32)
    prow = np.broadcast_to(np.arange(C, dtype=np.float32), (128, C)).copy()
    revw = np.empty((128, NKT), dtype=np.float32)
    difft = np.empty((128, NKT * C), dtype=np.float32)
    causal = np.empty((128, NKT * C), dtype=np.float32)
    cc = np.arange(C, dtype=np.float32)
    for kt in range(NKT):
        kg = kt * 128 + p
        revw[:, kt] = (C - 1) - kg
        d = cc[None, :] - 1.0 - kg[:, None]
        difft[:, kt * C:(kt + 1) * C] = np.maximum(d, 0.0)
        causal[:, kt * C:(kt + 1) * C] = (cc[None, :] > kg[:, None]).astype(np.float32)
    return prow, revw, difft, causal


def make_in_maps(inputs, C=512):
    NKT = C // 128
    x = np.asarray(inputs["L0_attn_out"], dtype=np.float32)
    B = x.shape[0]
    prow, revw, difft, causal = _tables(C, NKT)
    dec = np.full((128, 1), np.float32(np.asarray(inputs["L0_attn_decay"], dtype=np.float32)), dtype=np.float32)
    la = np.full((128, 1), np.float32(np.asarray(inputs["L0_attn_log_alpha"], dtype=np.float32)), dtype=np.float32)
    shared = {
        "wwt": _moving_weight(np.asarray(inputs["L0_attn_W_write"], dtype=np.float32)),
        "wrt": _block_weight(np.asarray(inputs["L0_attn_W_read"], dtype=np.float32)),
        "w1": _block_weight(np.asarray(inputs["L0_ffn_w1_w"], dtype=np.float32)),
        "w2": _block_weight(np.asarray(inputs["L0_ffn_w2_w"], dtype=np.float32)),
        "h1": _block_weight(np.asarray(inputs["L0_ffn_h1_w"], dtype=np.float32)),
        "h2": _block_weight(np.asarray(inputs["L0_ffn_h2_w"], dtype=np.float32)),
        "w3": _block_weight(np.asarray(inputs["L0_ffn_w3_w"], dtype=np.float32)),
        "svd": np.ascontiguousarray(np.concatenate([
            _scale_vec(inputs["L0_ffn_w1_s"]), _scale_vec(inputs["L0_ffn_w2_s"]),
            _scale_vec(inputs["L0_ffn_h1_s"]), _scale_vec(inputs["L0_ffn_h2_s"]),
            _scale_vec(inputs["L0_ffn_w3_s"]), dec, la], axis=1)),
        "csd": np.ascontiguousarray(np.concatenate([prow, revw, difft, causal], axis=1)),
        "ident_d": np.eye(128, dtype=np.float32),
    }
    maps = []
    NQ, QW = 4, 512
    for b in range(B):
        m = dict(shared)
        xt = x[b].T
        Dd, T = xt.shape
        m["xT"] = np.ascontiguousarray(
            xt.reshape(Dd // 128, 128, NQ, QW).transpose(1, 2, 0, 3).reshape(128, NQ * (Dd // 128) * QW))
        maps.append(m)
    return maps


def kernel(**inputs):
    nc = build_nc()
    in_maps = make_in_maps(inputs)
    res = run_bass_kernel_spmd(nc, in_maps, core_ids=list(range(8)))
    out = np.stack([np.ascontiguousarray(r["out"].T) for r in res.results])
    return out.astype(np.float32)


# version 36
# speedup vs baseline: 1.1058x; 1.0003x over previous
import contextlib
import numpy as np
import concourse.bass as bass
import concourse.mybir as mybir
from concourse.bass_utils import run_bass_kernel_spmd

F32 = mybir.dt.float32
BF16 = mybir.dt.bfloat16
AT = mybir.ActivationFunctionType
OP = mybir.AluOpType

ENGS = (("t", "tensor"), ("v", "vector"), ("s", "scalar"), ("g", "gpsimd"), ("y", "sync"))

OWN_SEM = {"t": "t", "v": "v", "s": "s"}


class Sched:

    def __init__(self, nc, semnames):
        self.nc = nc
        self.sems = {}
        for n in semnames:
            self.sems[n] = nc.semaphore(n).__enter__()
        self.cnt = {n: 0 for n in semnames}
        self.q = {k: [] for k, _ in ENGS}
        self.wm = {k: {} for k, _ in ENGS}

    def op(self, eng, build, inc=None, inc_n=1, deps=()):
        self.wait(eng, deps)
        ev = None
        semh = None
        if inc is not None:
            self.cnt[inc] += inc_n
            ev = (inc, self.cnt[inc])
            semh = self.sems[inc]
        def run(e, build=build, semh=semh, inc_n=inc_n):
            ins = build(e)
            if semh is not None:
                ins.then_inc(semh, inc_n)
        self.q[eng].append(run)
        return ev

    def wait(self, eng, deps):
        own = OWN_SEM.get(eng)
        for d in deps:
            if d is None:
                continue
            sem, val = d
            if val is None or val <= 0:
                continue
            if sem == own:
                continue
            if self.wm[eng].get(sem, 0) >= val:
                continue
            self.wm[eng][sem] = val
            h = self.sems[sem]
            self.q[eng].append(lambda e, h=h, v=val: e.wait_ge(h, v))

    COMPUTE_SEMS = ("t", "v", "s")

    def flush(self):
        with self.nc.Block() as blk:
            for key, attr in ENGS:
                q = self.q[key]
                if not q:
                    continue
                getattr(blk, attr)(lambda e, q=q: [f(e) for f in q])
        self.q = {k: [] for k, _ in ENGS}
        for k, _ in ENGS:
            for sname in self.COMPUTE_SEMS:
                if sname in self.cnt:
                    self.wm[k][sname] = self.cnt[sname]


class Pool:

    def __init__(self, slots):
        self.slots = [[s, None] for s in slots]
        self.i = 0

    def acquire(self):
        slot = self.slots[self.i % len(self.slots)]
        self.i += 1
        t, ev = slot
        return t, ([ev] if ev else []), slot

    def release(self, slot, ev):
        slot[1] = ev


def build_nc(T=2048, D=1024, HID=4096, C=512, taus=(512, 768, 768)):
    assert T % C == 0 and C % 128 == 0 and D % 128 == 0 and HID % 128 == 0
    assert sum(taus) == T
    ND = D // 128
    NH = HID // 128
    NKT = C // 128
    NCH = T // C
    CXN = T + 16
    TAUM = max(taus)
    EPS = 1e-5

    nc = bass.Bass("TRN2", target_bir_lowering=False)

    xT = nc.dram_tensor("xT", [128, (T // 512) * (D // 128) * 512], F32, kind="ExternalInput")
    wwt = nc.dram_tensor("wwt", [128, ND * D], F32, kind="ExternalInput")
    wrt = nc.dram_tensor("wrt", [128, ND * D], F32, kind="ExternalInput")
    w1 = nc.dram_tensor("w1", [128, NH * D], F32, kind="ExternalInput")
    w2 = nc.dram_tensor("w2", [128, NH * D], F32, kind="ExternalInput")
    h1 = nc.dram_tensor("h1", [128, NH * HID], F32, kind="ExternalInput")
    h2 = nc.dram_tensor("h2", [128, NH * HID], F32, kind="ExternalInput")
    w3 = nc.dram_tensor("w3", [128, ND * HID], F32, kind="ExternalInput")
    NSV = 4 * NH + ND + 2
    NCS = C + NKT + 2 * NKT * C
    svd = nc.dram_tensor("svd", [128, NSV], F32, kind="ExternalInput")
    csd = nc.dram_tensor("csd", [128, NCS], F32, kind="ExternalInput")
    ident_d = nc.dram_tensor("ident_d", [128, 128], F32, kind="ExternalInput")
    out_d = nc.dram_tensor("out", [D, T], F32, kind="ExternalOutput")

    S = Sched(nc, ["t", "v", "s", "g", "gw", "o", "gcs", "gid",
                   "x0", "x1", "x2", "x3", "x4", "x5", "x6", "x7",
                   "w0", "w1s", "w2s", "l0", "l1", "l2", "l3", "l4", "l5"])
    wsems = ["w0", "w1s", "w2s"]

    def sb(st, name, shape, dt):
        return st.enter_context(nc.sbuf_tensor(name, shape, dt))

    def ps(st, name, shape, dt=F32):
        return st.enter_context(nc.psum_tensor(name, shape, dt))

    stP = contextlib.ExitStack()
    XN = sb(stP, "XN", [128, ND * CXN], BF16)
    Pacc = sb(stP, "Pacc", [128, ND * T], BF16)
    gam_v = sb(stP, "gam_v", [128, 1], F32)
    lg_v = sb(stP, "lg_v", [128, 1], F32)
    al_v = sb(stP, "al_v", [128, 1], F32)
    gc_v = sb(stP, "gc_v", [128, 1], F32)
    SV = sb(stP, "SV", [128, NSV], F32)
    s1v = SV[:, 0:NH]
    s2v = SV[:, NH:2 * NH]
    sh1v = SV[:, 2 * NH:3 * NH]
    sh2v = SV[:, 3 * NH:4 * NH]
    s3v = SV[:, 4 * NH:4 * NH + ND]
    dec_v = SV[:, 4 * NH + ND:4 * NH + ND + 1]
    lad_v = SV[:, 4 * NH + ND + 1:4 * NH + ND + 2]
    eps_v = sb(stP, "eps_v", [128, 1], F32)
    cc_v = sb(stP, "cc_v", [128, 1], F32)
    invd_v = sb(stP, "invd_v", [128, 1], F32)
    nh_v = sb(stP, "nh_v", [128, 1], F32)
    wslabs_l = [sb(stP, "wl%d" % i, [128, D], BF16) for i in range(6)]

    stB = contextlib.ExitStack()
    Mt = sb(stB, "Mt", [128, NKT * C], BF16)
    GPbf = sb(stB, "GPbf", [128, C], BF16)
    gwv = sb(stB, "gwv", [128, NKT], F32)
    identb = sb(stB, "identb", [128, 128], BF16)
    onesb = sb(stB, "onesb", [128, 128], BF16)
    Wwt_sb = sb(stB, "Wwt_sb", [128, ND * D], BF16)
    Wrt_sb = sb(stB, "Wrt_sb", [128, ND * D], BF16)
    Vtm = sb(stB, "Vtm", [128, NKT * D], BF16)
    WKtm = sb(stB, "WKtm", [128, NKT * D], BF16)
    Stm = sb(stB, "Stm", [128, NKT * C], BF16)

    stA = contextlib.ExitStack()
    MEAN = sb(stA, "MEAN", [128, T], F32)
    XBF = sb(stA, "XBF", [128, ND * T], BF16)
    Xq0 = sb(stA, "Xq0", [128, T], BF16)
    RSTDb = sb(stA, "RSTDb", [128, T], BF16)
    MRb = sb(stA, "MRb", [128, T], BF16)
    NQ = max(T // 512, 1)
    QW = T // NQ
    psA = [ps(stA, "psA%d" % q, [128, QW]) for q in range(NQ)]
    psB = [ps(stA, "psB%d" % q, [128, QW]) for q in range(NQ)]
    stT = contextlib.ExitStack()
    CS = sb(stT, "CS", [128, NCS], F32)
    prow_s = CS[:, 0:C]
    revw_s = CS[:, C:C + NKT]
    difft_s = CS[:, C + NKT:C + NKT + NKT * C]
    causal_s = CS[:, C + NKT + NKT * C:NCS]
    assert nc.sbuf_bytes_remaining >= 0

    ev_g = S.op("y", lambda y: y.dma_start(out=SV[:], in_=svd[:]), inc="g", inc_n=16)
    ev_cs = S.op("y", lambda y: y.dma_start(out=CS[:], in_=csd[:]), inc="gcs", inc_n=16)
    ev_id = S.op("g", lambda g: g.dma_start(out=identb[:], in_=ident_d[:]), inc="gid", inc_n=16)
    assert QW == C
    NEH = max(D // 512, 1)
    EW = min(D, 512)
    HWW = ND * EW
    ev_ldq = {}
    for q in range(NQ):
        if q == NQ - 1:
            S.op("g", lambda g: g.dma_start(out=Wwt_sb[:, 0:HWW], in_=wwt[:, 0:HWW]), inc="gw", inc_n=16)
            S.op("g", lambda g: g.dma_start(out=Wwt_sb[:, HWW:2 * HWW], in_=wwt[:, HWW:2 * HWW]), inc="gw", inc_n=16)
        S.op("g", lambda g, q=q: g.dma_start(
            out=XBF[:, q * ND * QW:(q + 1) * ND * QW],
            in_=xT[:, q * ND * QW:(q + 1) * ND * QW]), inc="x%d" % q, inc_n=16)
        ev_ldq[q] = ("x%d" % q, 16)
    ev_gww = [("gw", 16), ("gw", 32)]
    ev_gw = S.op("g", lambda g: g.dma_start(out=Wrt_sb[:], in_=wrt[:]), inc="gw", inc_n=16)

    def xbf(q, dc):
        return XBF[:, (q * ND + dc) * QW:(q * ND + dc + 1) * QW]

    ev_eps = S.op("v", lambda v: v.memset(eps_v[:], EPS), inc="v")
    ev_cc = S.op("v", lambda v: v.memset(cc_v[:], float(C)), inc="v")
    S.op("v", lambda v: v.memset(invd_v[:], 1.0 / D), inc="v")
    ev_nh = S.op("v", lambda v: v.memset(nh_v[:], -0.5), inc="v")
    ev_gam = S.op("s", lambda s: s.activation(gam_v[:], dec_v[:], AT.Sigmoid), inc="s", deps=[ev_g])
    ev_lg = S.op("s", lambda s: s.activation(lg_v[:], gam_v[:], AT.Ln), inc="s", deps=[ev_gam])
    S.op("s", lambda s: s.activation(al_v[:], lad_v[:], AT.Exp), inc="s", deps=[ev_lg])
    S.op("s", lambda s: s.activation(gc_v[:], lg_v[:], AT.Exp, scale=cc_v[:, 0:1]), inc="s", deps=[ev_lg, ev_cc])
    S.op("s", lambda s: s.activation(GPbf[:], prow_s[:], AT.Exp, scale=lg_v[:, 0:1]), inc="s", deps=[ev_lg, ev_cs])
    S.op("s", lambda s: s.activation(gwv[:], revw_s[:], AT.Exp, scale=lg_v[:, 0:1]), inc="s", deps=[ev_lg])
    ev_me = S.op("s", lambda s: s.activation(difft_s[:], difft_s[:], AT.Exp, scale=lg_v[:, 0:1]), inc="s", deps=[ev_lg])
    ev_ones = S.op("v", lambda v: v.memset(onesb[:], 1.0), inc="v")
    ev_z = S.op("v", lambda v: v.memset(XN[:].rearrange("p (c f) -> p c f", f=CXN)[:, :, 0:8], 0.0), inc="v")
    ev_mt = S.op("v", lambda v: v.tensor_tensor(Mt[:], difft_s[:], causal_s[:], OP.mult), inc="v", deps=[ev_me])

    stT.close()

    ev_sums_q = {}
    sq_last = {0: None, 1: None}

    def emit_sums(q):
        mm = None
        for dc in range(ND):
            b = dc % 2
            sq = S.op("v", lambda v, q=q, dc=dc, b=b: v.tensor_tensor(
                Xq0[:, b * QW:(b + 1) * QW], xbf(q, dc), xbf(q, dc), OP.mult),
                inc="v", deps=[ev_ldq[q], sq_last[b]])
            S.op("t", lambda t, q=q, dc=dc: t.matmul(
                psA[q][:, 0:QW], onesb[:], xbf(q, dc),
                start=(dc == 0), stop=(dc == ND - 1)),
                deps=[ev_ldq[q], ev_ones] if dc == 0 else [])
            mm = S.op("t", lambda t, q=q, dc=dc, b=b: t.matmul(
                psB[q][:, 0:QW], onesb[:], Xq0[:, b * QW:(b + 1) * QW],
                start=(dc == 0), stop=(dc == ND - 1)), deps=[sq], inc="t")
            sq_last[b] = mm
        ev_sums_q[q] = mm

    ev_ea = [None] * NQ
    ev_rsq = [None] * NQ
    ev_mrb = [None] * NQ

    def stats_q(q):
        sl = slice(q * QW, (q + 1) * QW)
        ev_ea[q] = S.op("v", lambda v, q=q, sl=sl: v.tensor_scalar_mul(
            MEAN[:, sl], psA[q][:, 0:QW], 1.0 / D), inc="v", deps=[ev_sums_q[q]])
        e3 = S.op("s", lambda s, q=q, sl=sl: s.activation(
            MRb[:, sl], MEAN[:, sl], AT.Square), inc="s", deps=[ev_ea[q]])
        e4 = S.op("v", lambda v, q=q, sl=sl: v.scalar_tensor_tensor(
            psB[q][:, 0:QW], psB[q][:, 0:QW], invd_v[:, 0:1], MRb[:, sl],
            OP.mult, OP.subtract), inc="v", deps=[e3])
        e5 = S.op("s", lambda s, q=q: s.activation(
            psB[q][:, 0:QW], psB[q][:, 0:QW], AT.Ln, bias=eps_v[:, 0:1]), inc="s", deps=[e4, ev_eps])
        ev_rsq[q] = S.op("s", lambda s, q=q, sl=sl: s.activation(
            RSTDb[:, sl], psB[q][:, 0:QW], AT.Exp, scale=nh_v[:, 0:1]), inc="s", deps=[e5, ev_nh])
        ev_mrb[q] = S.op("v", lambda v, q=q, sl=sl: v.tensor_tensor(
            MRb[:, sl], MEAN[:, sl], RSTDb[:, sl], OP.mult), inc="v", deps=[ev_rsq[q]])

    ev_xn0_dc = {}

    def p2_chunk0():
        scr = Xq0[:, 2 * QW: 2 * QW + C]
        for dc in range(ND):
            o2 = dc * CXN + 8
            S.op("v", lambda v, dc=dc, scr=scr: v.tensor_tensor(
                scr, xbf(0, dc), RSTDb[:, 0:C], OP.mult), inc="v",
                deps=[ev_rsq[0]])
            ev_xn0_dc[dc] = S.op("v", lambda v, o2=o2, scr=scr: v.tensor_tensor(
                XN[:, o2: o2 + C], scr, MRb[:, 0:C], OP.subtract), inc="v",
                deps=[ev_mrb[0], ev_z])

    ev_xn_dc = {}
    ev_p2q = [None] * NQ
    ev_pacc0 = [None]

    def emit_pacc0():
        prev = None
        for dc in range(ND):
            o2 = dc * CXN + 8
            prev = S.op("v", lambda v, dc=dc, o2=o2: v.tensor_tensor(
                Pacc[:, dc * T: dc * T + C], XN[:, o2: o2 + C], xbf(0, dc), OP.add),
                inc="v", deps=[])
        ev_pacc0[0] = prev

    def p2_rest(q):
        scr = Xq0[:, 0:C]
        prev = None
        for dc in range(ND):
            o2 = dc * CXN + 8 + q * C
            S.op("v", lambda v, q=q, dc=dc, scr=scr: v.tensor_tensor(
                scr, xbf(q, dc), RSTDb[:, q * QW:(q + 1) * QW], OP.mult), inc="v",
                deps=[ev_rsq[q]])
            ev_xn_dc[(q, dc)] = S.op("v", lambda v, o2=o2, scr=scr, q=q: v.tensor_tensor(
                XN[:, o2: o2 + C], scr, MRb[:, q * QW:(q + 1) * QW], OP.subtract),
                inc="v", deps=[ev_mrb[q]])
            prev = S.op("v", lambda v, q=q, dc=dc, o2=o2: v.tensor_tensor(
                Pacc[:, dc * T + q * C: dc * T + (q + 1) * C], XN[:, o2: o2 + C],
                xbf(q, dc), OP.add), inc="v", deps=[])
        ev_p2q[q] = prev

    stT.close()
    stA.close()

    stS = contextlib.ExitStack()
    VG = sb(stS, "VG", [128, NKT * D], BF16)
    RKG = sb(stS, "RKG", [128, ND * C], BF16)
    RDS = sb(stS, "RDS", [128, ND * C], BF16)
    Af = sb(stS, "Af", [128, ND * D], F32)
    Abf = sb(stS, "Abf", [128, ND * D], BF16)
    tps = [ps(stS, "tp%d" % i, [128, 1024], BF16) for i in range(2)]
    qs = [ps(stS, "q%d" % i, [128, 512]) for i in range(5)]
    assert nc.sbuf_bytes_remaining >= 0

    p512 = Pool([qs[2], qs[3], qs[0], qs[4], qs[1]])
    ptp = Pool(tps)
    ev_intra_prev = None
    ev_vg_prev = None
    ev_da_prev = None
    ev_abf_prev = None
    ev_ro_prev = None

    def xdep(c, dc):
        if c == 0:
            return ev_xn0_dc[dc]
        return ev_xn_dc[(c, dc)]

    ev_S_groups = {}
    ev_ve = {}
    ev_wk = {}
    ve_l = {}
    wk_l = {}
    mask_l = {}
    vg_l = {}
    intra_l = {}
    abf_l = {}
    DTH = ND // NEH

    def emit_v(c):
        base = c * C
        last = None
        ve_l[c] = {}
        for eh in range(NEH):
            for tt in range(NKT):
                slot_t, sdeps, slot = p512.acquire()
                for dc in range(ND):
                    deps = [xdep(c, dc)]
                    if dc == 0:
                        deps += sdeps + [ev_gww[eh]]
                    S.op("t", lambda t, slot_t=slot_t, tt=tt, eh=eh, dc=dc, base=base: t.matmul(
                        slot_t[:, 0:EW],
                        XN[:, dc * CXN + 8 + base + tt * 128: dc * CXN + 8 + base + tt * 128 + 128],
                        Wwt_sb[:, eh * HWW + dc * EW: eh * HWW + dc * EW + EW],
                        start=(dc == 0), stop=(dc == ND - 1)),
                        deps=deps,
                        inc="t" if dc == ND - 1 else None)
                evm = ("t", S.cnt["t"])
                dprev = []
                if c > 0:
                    dprev = [intra_l[c - 1][eh * DTH + DTH - 1], vg_l[c - 1][tt]]
                last = S.op("s", lambda s, slot_t=slot_t, tt=tt, eh=eh: s.copy(
                    Vtm[:, tt * D + eh * EW: tt * D + eh * EW + EW], slot_t[:, 0:EW]),
                    inc="s", deps=[evm] + dprev)
                ve_l[c][(tt, eh)] = last
                p512.release(slot, last)
        ev_ve[c] = last

    def emit_wk(c):
        base = c * C
        last = None
        wk_l[c] = []
        for kt in range(NKT):
            slot_t, sdeps, slot = ptp.acquire()
            for dc in range(ND):
                deps = [xdep(c, dc)]
                if dc == 0:
                    deps += sdeps
                S.op("t", lambda t, slot_t=slot_t, kt=kt, dc=dc, base=base: t.transpose(
                    slot_t[:, dc * 128:(dc + 1) * 128],
                    XN[:, dc * CXN + 7 + base + kt * 128: dc * CXN + 7 + base + kt * 128 + 128],
                    identb[:]), deps=deps + ([ev_id] if dc == 0 else []),
                    inc="t" if dc == ND - 1 else None)
            evm = ("t", S.cnt["t"])
            if c == 0:
                last = S.op("v", lambda v, slot_t=slot_t, kt=kt: v.tensor_copy(
                    WKtm[:, kt * D:(kt + 1) * D], slot_t[:, 0:ND * 128]), inc="v",
                    deps=[evm, ev_da_prev])
            else:
                last = S.op("s", lambda s, slot_t=slot_t, kt=kt: s.copy(
                    WKtm[:, kt * D:(kt + 1) * D], slot_t[:, 0:ND * 128]), inc="s",
                    deps=[evm, ev_da_prev])
            wk_l[c].append(last)
            ptp.release(slot, last)
        ev_wk[c] = last

    def emit_S(c):
        base = c * C
        groups = []
        for kt in range(NKT):
            slot_t, sdeps, slot = p512.acquire()
            for dc in range(ND):
                deps = [xdep(c, dc)]
                if dc == 0:
                    deps += sdeps
                S.op("t", lambda t, slot_t=slot_t, kt=kt, dc=dc, base=base: t.matmul(
                    slot_t[:, 0:C],
                    XN[:, dc * CXN + 7 + base + kt * 128: dc * CXN + 7 + base + kt * 128 + 128],
                    XN[:, dc * CXN + 8 + base: dc * CXN + 8 + base + C],
                    start=(dc == 0), stop=(dc == ND - 1)),
                    deps=deps,
                    inc="t" if dc == ND - 1 else None)
            groups.append((slot_t, slot, ("t", S.cnt["t"])))
        ev_S_groups[c] = groups

    ev_masks = {}

    def emit_masks(c):
        last = None
        mask_l[c] = []
        for kt, (slot_t, slot, evm) in enumerate(ev_S_groups.pop(c)):
            last = S.op("v", lambda v, slot_t=slot_t, kt=kt: v.tensor_tensor(
                Stm[:, kt * C:(kt + 1) * C], slot_t[:, 0:C], Mt[:, kt * C:(kt + 1) * C], OP.mult),
                inc="v", deps=[evm, ev_intra_prev, ev_mt])
            mask_l[c].append(last)
            p512.release(slot, last)
        ev_masks[c] = last

    emit_sums(0)
    stats_q(0)
    emit_sums(1)
    stats_q(1)
    p2_chunk0()
    ptp.slots[0][1] = ev_ea[0]
    ptp.slots[1][1] = ev_ea[1]
    emit_sums(2)
    stats_q(2)
    emit_wk(0)
    p512.slots[0][1] = ev_rsq[0]
    p512.slots[1][1] = ev_rsq[1]
    p512.slots[2][1] = ev_ea[2]
    p512.slots[3][1] = ev_rsq[2]
    emit_S(0)
    emit_sums(3)
    stats_q(3)
    p512.slots[4][1] = ev_ea[3]
    emit_masks(0)
    p2_rest(1)
    emit_pacc0()
    emit_v(0)

    for c in range(NCH):
        base = c * C
        need_state = (c < NCH - 1) or NCH == 1
        ev_rkg_l = []
        if c > 0:
            for jc in range(ND):
                ev_rkg_l.append(S.op("v", lambda v, jc=jc, base=base: v.tensor_tensor(
                    RKG[:, jc * C:(jc + 1) * C],
                    XN[:, jc * CXN + 8 + base: jc * CXN + 8 + base + C],
                    GPbf[:], OP.mult), inc="v", deps=[ev_intra_prev, ev_pacc0[0]]))
        ev_st_last = ev_masks[c]

        ev_vg = None
        if need_state:
            vg_l[c] = []
            for tt in range(NKT):
                ev_vg = S.op("s", lambda s, tt=tt: s.activation(
                    VG[:, tt * D:(tt + 1) * D], Vtm[:, tt * D:(tt + 1) * D],
                    AT.Identity, scale=gwv[:, tt:tt + 1]), inc="s",
                    deps=[ve_l[c][(tt, 0)], ve_l[c][(tt, 1)], ev_da_prev, ev_mrb[NQ - 1]])
                vg_l[c].append(ev_vg)
            ev_vg_prev = ev_vg

        ev_intra = None
        ev_rd_last = None
        intra_l[c] = []
        rd_l = []
        for dt in range(ND):
            slot_t, sdeps, slot = p512.acquire()
            for kt in range(NKT):
                deps = []
                if kt == 0:
                    deps = list(sdeps)
                if dt % DTH == 0:
                    deps = deps + [mask_l[c][kt], ve_l[c][(kt, dt // DTH)]]
                S.op("t", lambda t, slot_t=slot_t, dt=dt, kt=kt, c=c: t.matmul(
                    slot_t[:, 0:C],
                    Vtm[:, kt * D + dt * 128: kt * D + dt * 128 + 128],
                    Stm[:, kt * C:(kt + 1) * C],
                    start=(kt == 0), stop=(kt == NKT - 1 and c == 0)), deps=deps,
                    inc="t" if (c == 0 and kt == NKT - 1) else None)
            if c > 0:
                for jc in range(ND):
                    deps = []
                    if dt == 0:
                        deps = [ev_rkg_l[jc], abf_l[c - 1][jc]]
                    S.op("t", lambda t, slot_t=slot_t, dt=dt, jc=jc: t.matmul(
                        slot_t[:, 0:C],
                        Abf[:, jc * D + dt * 128: jc * D + dt * 128 + 128],
                        RKG[:, jc * C:(jc + 1) * C],
                        start=False, stop=(jc == ND - 1)), deps=deps,
                        inc="t" if jc == ND - 1 else None)
            ev_intra = ("t", S.cnt["t"])
            intra_l[c].append(ev_intra)
            ev_rd_last = S.op("s", lambda s, slot_t=slot_t, dt=dt: s.copy(
                RDS[:, dt * C:(dt + 1) * C], slot_t[:, 0:C]), inc="s",
                deps=[ev_intra, ev_ro_prev, ev_p2q[1]])
            rd_l.append(ev_rd_last)
            p512.release(slot, ev_rd_last)
        ev_intra_prev = ev_intra

        def emit_da(c=c):
            ev_af = []
            ev_da = None
            for jc in range(ND):
                slots = []
                for ih in range(NEH):
                    slot_t, sdeps, slot = p512.acquire()
                    for kt in range(NKT):
                        deps = list(sdeps) if kt == 0 else []
                        if jc == 0 and ih == 0:
                            deps = deps + [wk_l[c][kt], vg_l[c][kt]]
                        S.op("t", lambda t, slot_t=slot_t, jc=jc, ih=ih, kt=kt: t.matmul(
                            slot_t[:, 0:EW],
                            WKtm[:, kt * D + jc * 128: kt * D + jc * 128 + 128],
                            VG[:, kt * D + ih * EW: kt * D + ih * EW + EW],
                            start=(kt == 0), stop=(kt == NKT - 1)), deps=deps,
                            inc="t" if kt == NKT - 1 else None)
                    slots.append((ih, slot_t, slot, ("t", S.cnt["t"])))
                ev_da = slots[-1][3]
                for (ih, slot_t, slot, evm) in slots:
                    sl = slice(jc * D + ih * EW, jc * D + ih * EW + EW)
                    if c == 0:
                        evf = S.op("s", lambda s, slot_t=slot_t, sl=sl: s.copy(
                            Af[:, sl], slot_t[:, 0:EW]), inc="s",
                            deps=[evm, ev_p2q[NQ - 1]])
                    else:
                        evf = S.op("v", lambda v, slot_t=slot_t, sl=sl: v.scalar_tensor_tensor(
                            Af[:, sl], Af[:, sl], gc_v[:, 0:1], slot_t[:, 0:EW],
                            OP.mult, OP.add), inc="v", deps=[evm])
                    ev_af.append(evf)
                    p512.release(slot, evf)
            return ev_da, ev_af

        def emit_ro(base=base):
            ev_ro = None
            for et in range(ND):
                slot_t, sdeps, slot = p512.acquire()
                for it in range(ND):
                    deps = list(sdeps) + [ev_gw] if it == 0 else []
                    if et == 0:
                        deps = deps + [rd_l[it]]
                    S.op("t", lambda t, slot_t=slot_t, et=et, it=it: t.matmul(
                        slot_t[:, 0:C],
                        Wrt_sb[:, et * D + it * 128: et * D + it * 128 + 128],
                        RDS[:, it * C:(it + 1) * C],
                        start=(it == 0), stop=(it == ND - 1)), deps=deps,
                        inc="t" if it == ND - 1 else None)
                ev_ro = ("t", S.cnt["t"])
                evp = S.op("v", lambda v, slot_t=slot_t, et=et, base=base: v.scalar_tensor_tensor(
                    Pacc[:, et * T + base: et * T + base + C], slot_t[:, 0:C],
                    al_v[:, 0:1], Pacc[:, et * T + base: et * T + base + C],
                    OP.mult, OP.add), inc="v", deps=[ev_ro])
                p512.release(slot, evp)
            return ev_ro

        if c == 0 and need_state:
            for q in range(2, NQ):
                p2_rest(q)
            ev_ro_prev = emit_ro()
            ev_da_prev, ev_af = emit_da()
        else:
            if need_state:
                ev_da_prev, ev_af = emit_da()
            ev_ro_prev = emit_ro()

        if need_state:
            ev_abf = None
            abf_l[c] = []
            for jc in range(ND):
                deps = [ev_af[jc * NEH + NEH - 1]]
                if c > 0:
                    deps.append(ev_intra_prev)
                ev_abf = S.op("s", lambda s, jc=jc: s.copy(
                    Abf[:, jc * D:(jc + 1) * D], Af[:, jc * D:(jc + 1) * D]), inc="s", deps=deps)
                abf_l[c].append(ev_abf)
            ev_abf_prev = ev_abf

        if c + 1 < NCH:
            emit_v(c + 1)
            if c + 1 < NCH - 1:
                emit_wk(c + 1)
            emit_S(c + 1)
            emit_masks(c + 1)

    lsems = ["l0", "l1", "l2", "l3", "l4", "l5"]
    lstate = [[None, 0] for _ in range(6)]
    lidx = [0]
    pre_l = []

    def issue_slab_l(src, ob):
        i = lidx[0] % 6
        lidx[0] += 1
        consumer, n = lstate[i]
        S.op("g", lambda g, i=i, src=src, ob=ob: g.dma_start(
            out=wslabs_l[i][:], in_=src[:, ob * D:(ob + 1) * D]),
            inc=lsems[i], inc_n=16, deps=[consumer] if consumer else [])
        lstate[i][1] = n + 1
        return i, (lsems[i], 16 * (n + 1))

    OB_ORDER = [NH - 1] + list(range(NH - 1))
    for (psrc, pob) in ((w1, OB_ORDER[0]), (w2, OB_ORDER[0]), (w1, OB_ORDER[1]), (w2, OB_ORDER[1])):
        pre_l.append(issue_slab_l(psrc, pob))

    S.flush()
    stS.close()
    stB.close()

    stC = contextlib.ExitStack()
    Ha = sb(stC, "Ha", [128, NH * TAUM], BF16)
    Hb = sb(stC, "Hb", [128, NH * TAUM], BF16)
    wslabs = [sb(stC, "wsl%d" % i, [128, HID], BF16) for i in range(3)]
    sa_tiles = [sb(stC, "sa%d" % i, [128, TAUM], BF16) for i in range(2)]
    ostg = sb(stC, "ostg", [128, 2 * TAUM], F32)
    fs = [ps(stC, "f%d" % i, [128, TAUM]) for i in range(4)]
    assert nc.sbuf_bytes_remaining >= 0

    fps = Pool(fs)
    sas = Pool(sa_tiles)
    wstate = [[None, 0] for _ in range(3)]
    widx = [0]
    n_pre = [0]

    def load_slab(src, ob, K):
        i = widx[0] % 3
        widx[0] += 1
        consumer, n = wstate[i]
        S.op("g", lambda g, i=i, src=src, ob=ob, K=K: g.dma_start(
            out=wslabs[i][:, 0:K], in_=src[:, ob * K:(ob + 1) * K]),
            inc=wsems[i], inc_n=16, deps=[consumer] if consumer else [])
        wstate[i][1] = n + 1
        return i, (wsems[i], 16 * (n + 1))

    def load_slab_l(src, ob):
        if n_pre[0] < len(pre_l):
            i, ev = pre_l[n_pre[0]]
            n_pre[0] += 1
            return i, ev
        return issue_slab_l(src, ob)

    def pieces(tau):
        out, off = [], 0
        while off < tau:
            w = min(512, tau - off)
            out.append((off, w))
            off += w
        return out

    ev_l4_mm_prev = None
    ev_ostg_dma = [None, None]
    t0 = 0
    for ti, tau in enumerate(taus):
        pcs = pieces(tau)
        ev_h1_list = [None] * NH
        ev_h1 = None
        for ob in OB_ORDER:
            ia, eva = load_slab_l(w1, ob)
            ib, evb = load_slab_l(w2, ob)
            grp = []
            for (wsl_i, evw) in ((ia, eva), (ib, evb)):
                slot_t, sdeps, slot = fps.acquire()
                for dc in range(ND):
                    for (off, w) in pcs:
                        deps = []
                        if dc == 0 and off == 0:
                            deps = list(sdeps) + [evw, ev_l4_mm_prev]
                        S.op("t", lambda t, slot_t=slot_t, wsl_i=wsl_i, dc=dc, off=off, w=w, t0=t0: t.matmul(
                            slot_t[:, off:off + w],
                            wslabs_l[wsl_i][:, dc * 128:(dc + 1) * 128],
                            XN[:, dc * CXN + 8 + t0 + off: dc * CXN + 8 + t0 + off + w],
                            start=(dc == 0), stop=(dc == ND - 1)), deps=deps,
                            inc="t" if (dc == ND - 1 and (off, w) == pcs[-1]) else None)
                grp.append((slot_t, slot, ("t", S.cnt["t"])))
            (pa_t, pa_slot, ev_ma), (pb_t, pb_slot, ev_mb) = grp
            lstate[ia][0] = ev_ma
            lstate[ib][0] = ev_mb
            sa_t, sa_deps, sa_slot = sas.acquire()
            ev_sa = S.op("s", lambda s, sa_t=sa_t, pa_t=pa_t, ob=ob, tau=tau: s.activation(
                sa_t[:, 0:tau], pa_t[:, 0:tau], AT.Silu, scale=s1v[:, ob:ob + 1]),
                inc="s", deps=[ev_ma] + sa_deps)
            ev_h1 = S.op("v", lambda v, pb_t=pb_t, sa_t=sa_t, ob=ob, tau=tau: v.scalar_tensor_tensor(
                Ha[:, ob * TAUM: ob * TAUM + tau], pb_t[:, 0:tau], s2v[:, ob:ob + 1],
                sa_t[:, 0:tau], OP.mult, OP.mult), inc="v",
                deps=[ev_mb, ev_sa, ev_l4_mm_prev])
            ev_h1_list[ob] = ev_h1
            fps.release(pa_slot, ev_sa)
            fps.release(pb_slot, ev_h1)
            sas.release(sa_slot, ev_h1)

        ev_prev_list = ev_h1_list
        ev_l2_mm = None
        for li, (wsrc, ssrc, Hin, Hout) in enumerate(((h1, sh1v, Ha, Hb), (h2, sh2v, Hb, Ha))):
            ev_hl = None
            ev_hl_list = [None] * NH
            for ob in OB_ORDER:
                iw, evw = load_slab(wsrc, ob, HID)
                slot_t, sdeps, slot = fps.acquire()
                for hc in range(NH):
                    for (off, w) in pcs:
                        deps = []
                        if hc == 0 and off == 0:
                            deps = list(sdeps) + [evw]
                        if ob == OB_ORDER[0] and off == 0:
                            deps = deps + [ev_prev_list[hc]]
                        S.op("t", lambda t, slot_t=slot_t, iw=iw, hc=hc, off=off, w=w, Hin=Hin: t.matmul(
                            slot_t[:, off:off + w],
                            wslabs[iw][:, hc * 128:(hc + 1) * 128],
                            Hin[:, hc * TAUM + off: hc * TAUM + off + w],
                            start=(hc == 0), stop=(hc == NH - 1)), deps=deps,
                            inc="t" if (hc == NH - 1 and (off, w) == pcs[-1]) else None)
                ev_mm = ("t", S.cnt["t"])
                wstate[iw][0] = ev_mm
                if li == 0:
                    ev_l2_mm = ev_mm
                deps = [ev_mm]
                if li == 1:
                    deps.append(ev_l2_mm)
                ev_hl = S.op("s", lambda s, slot_t=slot_t, ob=ob, tau=tau, ssrc=ssrc, Hout=Hout: s.activation(
                    Hout[:, ob * TAUM: ob * TAUM + tau], slot_t[:, 0:tau], AT.Gelu,
                    scale=ssrc[:, ob:ob + 1]), inc="s", deps=deps)
                ev_hl_list[ob] = ev_hl
                fps.release(slot, ev_hl)
            ev_prev_list = ev_hl_list

        ev_l4_mm = None
        for ob in range(ND):
            final_ob = (ti == len(taus) - 1 and ob == ND - 1)
            iw, evw = load_slab(w3, ob, HID)
            slot_t, sdeps, slot = fps.acquire()
            pev = {}
            if final_ob:
                for (off, w) in pcs:
                    for hc in range(NH):
                        deps = []
                        if hc == 0 and off == 0:
                            deps = list(sdeps) + [evw]
                        S.op("t", lambda t, slot_t=slot_t, iw=iw, hc=hc, off=off, w=w: t.matmul(
                            slot_t[:, off:off + w],
                            wslabs[iw][:, hc * 128:(hc + 1) * 128],
                            Ha[:, hc * TAUM + off: hc * TAUM + off + w],
                            start=(hc == 0), stop=(hc == NH - 1)), deps=deps,
                            inc="t" if hc == NH - 1 else None)
                    pev[off] = ("t", S.cnt["t"])
                ev_mm = pev[pcs[-1][0]]
            else:
                for hc in range(NH):
                    for (off, w) in pcs:
                        deps = []
                        if hc == 0 and off == 0:
                            deps = list(sdeps) + [evw]
                        if ob == 0 and off == 0:
                            deps = deps + [ev_prev_list[hc]]
                        S.op("t", lambda t, slot_t=slot_t, iw=iw, hc=hc, off=off, w=w: t.matmul(
                            slot_t[:, off:off + w],
                            wslabs[iw][:, hc * 128:(hc + 1) * 128],
                            Ha[:, hc * TAUM + off: hc * TAUM + off + w],
                            start=(hc == 0), stop=(hc == NH - 1)), deps=deps,
                            inc="t" if (hc == NH - 1 and (off, w) == pcs[-1]) else None)
                ev_mm = ("t", S.cnt["t"])
            ev_l4_mm = ev_mm
            wstate[iw][0] = ev_mm
            osl = ob % 2
            if final_ob:
                evs = None
                for pi, (off, w) in enumerate(pcs):
                    deps = [pev[off]]
                    if pi == 0:
                        deps.append(ev_ostg_dma[osl])
                    evs = S.op("v", lambda v, slot_t=slot_t, ob=ob, osl=osl, off=off, w=w, t0=t0: v.scalar_tensor_tensor(
                        ostg[:, osl * TAUM + off: osl * TAUM + off + w], slot_t[:, off:off + w],
                        s3v[:, ob:ob + 1], Pacc[:, ob * T + t0 + off: ob * T + t0 + off + w],
                        OP.mult, OP.add), inc="v", deps=deps)
                    if pi < len(pcs) - 1:
                        src_p = ostg[:].rearrange("p (c f) -> p c f", f=TAUM)[:, osl:osl + 1, off:off + w]
                        dst_p = out_d.ap().rearrange("(c p) t -> p c t", p=128)[:, ob:ob + 1, t0 + off:t0 + off + w]
                        S.op("y", lambda y, src_p=src_p, dst_p=dst_p: y.dma_start(
                            out=dst_p, in_=src_p), inc="o", inc_n=16, deps=[evs])
                    else:
                        h = w // 2
                        src_a = ostg[:].rearrange("p (c f) -> p c f", f=TAUM)[:, osl:osl + 1, off:off + h]
                        dst_a = out_d.ap().rearrange("(c p) t -> p c t", p=128)[:, ob:ob + 1, t0 + off:t0 + off + h]
                        src_b = ostg[:].rearrange("p (c f) -> p c f", f=TAUM)[:, osl:osl + 1, off + h:off + w]
                        dst_b = out_d.ap().rearrange("(c p) t -> p c t", p=128)[:, ob:ob + 1, t0 + off + h:t0 + off + w]
                        S.op("y", lambda y, src_a=src_a, dst_a=dst_a: y.dma_start(
                            out=dst_a, in_=src_a), inc="o", inc_n=16, deps=[evs])
                        ev_ostg_dma[osl] = S.op("g", lambda g, src_b=src_b, dst_b=dst_b: g.dma_start(
                            out=dst_b, in_=src_b), inc="o", inc_n=16, deps=[evs])
                fps.release(slot, evs)
            else:
                evs = S.op("v", lambda v, slot_t=slot_t, ob=ob, osl=osl, tau=tau, t0=t0: v.scalar_tensor_tensor(
                    ostg[:, osl * TAUM: osl * TAUM + tau], slot_t[:, 0:tau],
                    s3v[:, ob:ob + 1], Pacc[:, ob * T + t0: ob * T + t0 + tau],
                    OP.mult, OP.add), inc="v", deps=[ev_mm, ev_ostg_dma[osl]])
                fps.release(slot, evs)
                src_ap = ostg[:].rearrange("p (c f) -> p c f", f=TAUM)[:, osl:osl + 1, 0:tau]
                dst_ap = out_d.ap().rearrange("(c p) t -> p c t", p=128)[:, ob:ob + 1, t0:t0 + tau]
                ev_ostg_dma[osl] = S.op("y", lambda y, src_ap=src_ap, dst_ap=dst_ap: y.dma_start(
                    out=dst_ap, in_=src_ap), inc="o", inc_n=16, deps=[evs])
        ev_l4_mm_prev = ev_l4_mm
        t0 += tau

    S.wait("y", [("o", S.cnt["o"])])
    S.flush()
    stC.close()
    stP.close()

    return nc



def _block_weight(W):
    O, K = W.shape
    Wt = np.ascontiguousarray(W.T)
    a = Wt.reshape(K // 128, 128, O // 128, 128)
    return np.ascontiguousarray(a.transpose(1, 2, 0, 3)).reshape(128, (O // 128) * K)


def _moving_weight(W):
    Wt = np.ascontiguousarray(W.T)
    Dd, E = Wt.shape
    NEH, EW = max(E // 512, 1), min(E, 512)
    a = Wt.reshape(Dd // 128, 128, NEH, EW)
    return np.ascontiguousarray(a.transpose(1, 2, 0, 3)).reshape(128, NEH * (Dd // 128) * EW)


def _scale_vec(s):
    return np.ascontiguousarray(np.asarray(s, dtype=np.float32).reshape(-1, 128).T)


def _tables(C, NKT):
    p = np.arange(128, dtype=np.float32)
    prow = np.broadcast_to(np.arange(C, dtype=np.float32), (128, C)).copy()
    revw = np.empty((128, NKT), dtype=np.float32)
    difft = np.empty((128, NKT * C), dtype=np.float32)
    causal = np.empty((128, NKT * C), dtype=np.float32)
    cc = np.arange(C, dtype=np.float32)
    for kt in range(NKT):
        kg = kt * 128 + p
        revw[:, kt] = (C - 1) - kg
        d = cc[None, :] - 1.0 - kg[:, None]
        difft[:, kt * C:(kt + 1) * C] = np.maximum(d, 0.0)
        causal[:, kt * C:(kt + 1) * C] = (cc[None, :] > kg[:, None]).astype(np.float32)
    return prow, revw, difft, causal


def make_in_maps(inputs, C=512):
    NKT = C // 128
    x = np.asarray(inputs["L0_attn_out"], dtype=np.float32)
    B = x.shape[0]
    prow, revw, difft, causal = _tables(C, NKT)
    dec = np.full((128, 1), np.float32(np.asarray(inputs["L0_attn_decay"], dtype=np.float32)), dtype=np.float32)
    la = np.full((128, 1), np.float32(np.asarray(inputs["L0_attn_log_alpha"], dtype=np.float32)), dtype=np.float32)
    shared = {
        "wwt": _moving_weight(np.asarray(inputs["L0_attn_W_write"], dtype=np.float32)),
        "wrt": _block_weight(np.asarray(inputs["L0_attn_W_read"], dtype=np.float32)),
        "w1": _block_weight(np.asarray(inputs["L0_ffn_w1_w"], dtype=np.float32)),
        "w2": _block_weight(np.asarray(inputs["L0_ffn_w2_w"], dtype=np.float32)),
        "h1": _block_weight(np.asarray(inputs["L0_ffn_h1_w"], dtype=np.float32)),
        "h2": _block_weight(np.asarray(inputs["L0_ffn_h2_w"], dtype=np.float32)),
        "w3": _block_weight(np.asarray(inputs["L0_ffn_w3_w"], dtype=np.float32)),
        "svd": np.ascontiguousarray(np.concatenate([
            _scale_vec(inputs["L0_ffn_w1_s"]), _scale_vec(inputs["L0_ffn_w2_s"]),
            _scale_vec(inputs["L0_ffn_h1_s"]), _scale_vec(inputs["L0_ffn_h2_s"]),
            _scale_vec(inputs["L0_ffn_w3_s"]), dec, la], axis=1)),
        "csd": np.ascontiguousarray(np.concatenate([prow, revw, difft, causal], axis=1)),
        "ident_d": np.eye(128, dtype=np.float32),
    }
    maps = []
    NQ, QW = 4, 512
    for b in range(B):
        m = dict(shared)
        xt = x[b].T
        Dd, T = xt.shape
        m["xT"] = np.ascontiguousarray(
            xt.reshape(Dd // 128, 128, NQ, QW).transpose(1, 2, 0, 3).reshape(128, NQ * (Dd // 128) * QW))
        maps.append(m)
    return maps


def kernel(**inputs):
    nc = build_nc()
    in_maps = make_in_maps(inputs)
    res = run_bass_kernel_spmd(nc, in_maps, core_ids=list(range(8)))
    out = np.stack([np.ascontiguousarray(r["out"].T) for r in res.results])
    return out.astype(np.float32)


# version 37 (final)
# speedup vs baseline: 1.1083x; 1.0023x over previous
import contextlib
import numpy as np
import concourse.bass as bass
import concourse.mybir as mybir
from concourse.bass_utils import run_bass_kernel_spmd

F32 = mybir.dt.float32
BF16 = mybir.dt.bfloat16
AT = mybir.ActivationFunctionType
OP = mybir.AluOpType

ENGS = (("t", "tensor"), ("v", "vector"), ("s", "scalar"), ("g", "gpsimd"), ("y", "sync"))

OWN_SEM = {"t": "t", "v": "v", "s": "s"}


class Sched:

    def __init__(self, nc, semnames):
        self.nc = nc
        self.sems = {}
        for n in semnames:
            self.sems[n] = nc.semaphore(n).__enter__()
        self.cnt = {n: 0 for n in semnames}
        self.q = {k: [] for k, _ in ENGS}
        self.wm = {k: {} for k, _ in ENGS}

    def op(self, eng, build, inc=None, inc_n=1, deps=()):
        self.wait(eng, deps)
        ev = None
        semh = None
        if inc is not None:
            self.cnt[inc] += inc_n
            ev = (inc, self.cnt[inc])
            semh = self.sems[inc]
        def run(e, build=build, semh=semh, inc_n=inc_n):
            ins = build(e)
            if semh is not None:
                ins.then_inc(semh, inc_n)
        self.q[eng].append(run)
        return ev

    def wait(self, eng, deps):
        own = OWN_SEM.get(eng)
        for d in deps:
            if d is None:
                continue
            sem, val = d
            if val is None or val <= 0:
                continue
            if sem == own:
                continue
            if self.wm[eng].get(sem, 0) >= val:
                continue
            self.wm[eng][sem] = val
            h = self.sems[sem]
            self.q[eng].append(lambda e, h=h, v=val: e.wait_ge(h, v))

    COMPUTE_SEMS = ("t", "v", "s")

    def flush(self):
        with self.nc.Block() as blk:
            for key, attr in ENGS:
                q = self.q[key]
                if not q:
                    continue
                getattr(blk, attr)(lambda e, q=q: [f(e) for f in q])
        self.q = {k: [] for k, _ in ENGS}
        for k, _ in ENGS:
            for sname in self.COMPUTE_SEMS:
                if sname in self.cnt:
                    self.wm[k][sname] = self.cnt[sname]


class Pool:

    def __init__(self, slots):
        self.slots = [[s, None] for s in slots]
        self.i = 0

    def acquire(self):
        slot = self.slots[self.i % len(self.slots)]
        self.i += 1
        t, ev = slot
        return t, ([ev] if ev else []), slot

    def release(self, slot, ev):
        slot[1] = ev


def build_nc(T=2048, D=1024, HID=4096, C=512, taus=(512, 768, 768)):
    assert T % C == 0 and C % 128 == 0 and D % 128 == 0 and HID % 128 == 0
    assert sum(taus) == T
    ND = D // 128
    NH = HID // 128
    NKT = C // 128
    NCH = T // C
    CXN = T + 16
    TAUM = max(taus)
    EPS = 1e-5

    nc = bass.Bass("TRN2", target_bir_lowering=False)

    xT = nc.dram_tensor("xT", [128, (T // 512) * (D // 128) * 512], F32, kind="ExternalInput")
    wwt = nc.dram_tensor("wwt", [128, ND * D], F32, kind="ExternalInput")
    wrt = nc.dram_tensor("wrt", [128, ND * D], F32, kind="ExternalInput")
    w1 = nc.dram_tensor("w1", [128, NH * D], F32, kind="ExternalInput")
    w2 = nc.dram_tensor("w2", [128, NH * D], F32, kind="ExternalInput")
    h1 = nc.dram_tensor("h1", [128, NH * HID], F32, kind="ExternalInput")
    h2 = nc.dram_tensor("h2", [128, NH * HID], F32, kind="ExternalInput")
    w3 = nc.dram_tensor("w3", [128, ND * HID], F32, kind="ExternalInput")
    NSV = 4 * NH + ND + 2
    NCS = C + NKT + 2 * NKT * C
    svd = nc.dram_tensor("svd", [128, NSV], F32, kind="ExternalInput")
    csd = nc.dram_tensor("csd", [128, NCS], F32, kind="ExternalInput")
    ident_d = nc.dram_tensor("ident_d", [128, 128], F32, kind="ExternalInput")
    out_d = nc.dram_tensor("out", [D, T], F32, kind="ExternalOutput")

    S = Sched(nc, ["t", "v", "s", "g", "gw", "o", "gcs", "gid",
                   "x0", "x1", "x2", "x3", "x4", "x5", "x6", "x7",
                   "w0", "w1s", "w2s", "l0", "l1", "l2", "l3", "l4", "l5"])
    wsems = ["w0", "w1s", "w2s"]

    def sb(st, name, shape, dt):
        return st.enter_context(nc.sbuf_tensor(name, shape, dt))

    def ps(st, name, shape, dt=F32):
        return st.enter_context(nc.psum_tensor(name, shape, dt))

    stP = contextlib.ExitStack()
    XN = sb(stP, "XN", [128, ND * CXN], BF16)
    Pacc = sb(stP, "Pacc", [128, ND * T], BF16)
    gam_v = sb(stP, "gam_v", [128, 1], F32)
    lg_v = sb(stP, "lg_v", [128, 1], F32)
    al_v = sb(stP, "al_v", [128, 1], F32)
    gc_v = sb(stP, "gc_v", [128, 1], F32)
    SV = sb(stP, "SV", [128, NSV], F32)
    s1v = SV[:, 0:NH]
    s2v = SV[:, NH:2 * NH]
    sh1v = SV[:, 2 * NH:3 * NH]
    sh2v = SV[:, 3 * NH:4 * NH]
    s3v = SV[:, 4 * NH:4 * NH + ND]
    dec_v = SV[:, 4 * NH + ND:4 * NH + ND + 1]
    lad_v = SV[:, 4 * NH + ND + 1:4 * NH + ND + 2]
    eps_v = sb(stP, "eps_v", [128, 1], F32)
    cc_v = sb(stP, "cc_v", [128, 1], F32)
    invd_v = sb(stP, "invd_v", [128, 1], F32)
    nh_v = sb(stP, "nh_v", [128, 1], F32)
    wslabs_l = [sb(stP, "wl%d" % i, [128, D], BF16) for i in range(6)]

    stB = contextlib.ExitStack()
    Mt = sb(stB, "Mt", [128, NKT * C], BF16)
    GPbf = sb(stB, "GPbf", [128, C], BF16)
    gwv = sb(stB, "gwv", [128, NKT], F32)
    identb = sb(stB, "identb", [128, 128], BF16)
    onesb = sb(stB, "onesb", [128, 128], BF16)
    Wwt_sb = sb(stB, "Wwt_sb", [128, ND * D], BF16)
    Wrt_sb = sb(stB, "Wrt_sb", [128, ND * D], BF16)
    Vtm = sb(stB, "Vtm", [128, NKT * D], BF16)
    WKtm = sb(stB, "WKtm", [128, NKT * D], BF16)
    Stm = sb(stB, "Stm", [128, NKT * C], BF16)

    stA = contextlib.ExitStack()
    MEAN = sb(stA, "MEAN", [128, T], F32)
    XBF = sb(stA, "XBF", [128, ND * T], BF16)
    Xq0 = sb(stA, "Xq0", [128, T], BF16)
    RSTDb = sb(stA, "RSTDb", [128, T], BF16)
    MRb = sb(stA, "MRb", [128, T], BF16)
    NQ = max(T // 512, 1)
    QW = T // NQ
    psA = [ps(stA, "psA%d" % q, [128, QW]) for q in range(NQ)]
    psB = [ps(stA, "psB%d" % q, [128, QW]) for q in range(NQ)]
    stT = contextlib.ExitStack()
    CS = sb(stT, "CS", [128, NCS], F32)
    prow_s = CS[:, 0:C]
    revw_s = CS[:, C:C + NKT]
    difft_s = CS[:, C + NKT:C + NKT + NKT * C]
    causal_s = CS[:, C + NKT + NKT * C:NCS]
    assert nc.sbuf_bytes_remaining >= 0

    ev_g = S.op("y", lambda y: y.dma_start(out=SV[:], in_=svd[:]), inc="g", inc_n=16)
    ev_cs = S.op("y", lambda y: y.dma_start(out=CS[:], in_=csd[:]), inc="gcs", inc_n=16)
    ev_id = S.op("g", lambda g: g.dma_start(out=identb[:], in_=ident_d[:]), inc="gid", inc_n=16)
    assert QW == C
    NEH = max(D // 512, 1)
    EW = min(D, 512)
    HWW = ND * EW
    ev_ldq = {}
    for q in range(NQ):
        if q == NQ - 1:
            S.op("g", lambda g: g.dma_start(out=Wwt_sb[:, 0:HWW], in_=wwt[:, 0:HWW]), inc="gw", inc_n=16)
            S.op("g", lambda g: g.dma_start(out=Wwt_sb[:, HWW:2 * HWW], in_=wwt[:, HWW:2 * HWW]), inc="gw", inc_n=16)
        S.op("g", lambda g, q=q: g.dma_start(
            out=XBF[:, q * ND * QW:(q + 1) * ND * QW],
            in_=xT[:, q * ND * QW:(q + 1) * ND * QW]), inc="x%d" % q, inc_n=16)
        ev_ldq[q] = ("x%d" % q, 16)
    ev_gww = [("gw", 16), ("gw", 32)]
    ev_gw = S.op("g", lambda g: g.dma_start(out=Wrt_sb[:], in_=wrt[:]), inc="gw", inc_n=16)

    def xbf(q, dc):
        return XBF[:, (q * ND + dc) * QW:(q * ND + dc + 1) * QW]

    ev_eps = S.op("v", lambda v: v.memset(eps_v[:], EPS), inc="v")
    ev_cc = S.op("v", lambda v: v.memset(cc_v[:], float(C)), inc="v")
    S.op("v", lambda v: v.memset(invd_v[:], 1.0 / D), inc="v")
    ev_nh = S.op("v", lambda v: v.memset(nh_v[:], -0.5), inc="v")
    ev_gam = S.op("s", lambda s: s.activation(gam_v[:], dec_v[:], AT.Sigmoid), inc="s", deps=[ev_g])
    ev_lg = S.op("s", lambda s: s.activation(lg_v[:], gam_v[:], AT.Ln), inc="s", deps=[ev_gam])
    S.op("s", lambda s: s.activation(al_v[:], lad_v[:], AT.Exp), inc="s", deps=[ev_lg])
    S.op("s", lambda s: s.activation(gc_v[:], lg_v[:], AT.Exp, scale=cc_v[:, 0:1]), inc="s", deps=[ev_lg, ev_cc])
    S.op("s", lambda s: s.activation(GPbf[:], prow_s[:], AT.Exp, scale=lg_v[:, 0:1]), inc="s", deps=[ev_lg, ev_cs])
    S.op("s", lambda s: s.activation(gwv[:], revw_s[:], AT.Exp, scale=lg_v[:, 0:1]), inc="s", deps=[ev_lg])
    ev_me = S.op("s", lambda s: s.activation(difft_s[:], difft_s[:], AT.Exp, scale=lg_v[:, 0:1]), inc="s", deps=[ev_lg])
    ev_ones = S.op("v", lambda v: v.memset(onesb[:], 1.0), inc="v")
    ev_z = S.op("v", lambda v: v.memset(XN[:].rearrange("p (c f) -> p c f", f=CXN)[:, :, 0:8], 0.0), inc="v")
    ev_mt = S.op("v", lambda v: v.tensor_tensor(Mt[:], difft_s[:], causal_s[:], OP.mult), inc="v", deps=[ev_me])

    stT.close()

    ev_sums_q = {}
    sq_last = {0: None, 1: None}

    def emit_sums(q):
        mm = None
        for dc in range(ND):
            b = dc % 2
            sq = S.op("v", lambda v, q=q, dc=dc, b=b: v.tensor_tensor(
                Xq0[:, b * QW:(b + 1) * QW], xbf(q, dc), xbf(q, dc), OP.mult),
                inc="v", deps=[ev_ldq[q], sq_last[b]])
            S.op("t", lambda t, q=q, dc=dc: t.matmul(
                psA[q][:, 0:QW], onesb[:], xbf(q, dc),
                start=(dc == 0), stop=(dc == ND - 1)),
                deps=[ev_ldq[q], ev_ones] if dc == 0 else [])
            mm = S.op("t", lambda t, q=q, dc=dc, b=b: t.matmul(
                psB[q][:, 0:QW], onesb[:], Xq0[:, b * QW:(b + 1) * QW],
                start=(dc == 0), stop=(dc == ND - 1)), deps=[sq], inc="t")
            sq_last[b] = mm
        ev_sums_q[q] = mm

    ev_ea = [None] * NQ
    ev_rsq = [None] * NQ
    ev_mrb = [None] * NQ

    def stats_q(q):
        sl = slice(q * QW, (q + 1) * QW)
        ev_ea[q] = S.op("v", lambda v, q=q, sl=sl: v.tensor_scalar_mul(
            MEAN[:, sl], psA[q][:, 0:QW], 1.0 / D), inc="v", deps=[ev_sums_q[q]])
        e3 = S.op("s", lambda s, q=q, sl=sl: s.activation(
            MRb[:, sl], MEAN[:, sl], AT.Square), inc="s", deps=[ev_ea[q]])
        e4 = S.op("v", lambda v, q=q, sl=sl: v.scalar_tensor_tensor(
            psB[q][:, 0:QW], psB[q][:, 0:QW], invd_v[:, 0:1], MRb[:, sl],
            OP.mult, OP.subtract), inc="v", deps=[e3])
        e5 = S.op("s", lambda s, q=q: s.activation(
            psB[q][:, 0:QW], psB[q][:, 0:QW], AT.Ln, bias=eps_v[:, 0:1]), inc="s", deps=[e4, ev_eps])
        ev_rsq[q] = S.op("s", lambda s, q=q, sl=sl: s.activation(
            RSTDb[:, sl], psB[q][:, 0:QW], AT.Exp, scale=nh_v[:, 0:1]), inc="s", deps=[e5, ev_nh])
        ev_mrb[q] = S.op("v", lambda v, q=q, sl=sl: v.tensor_tensor(
            MRb[:, sl], MEAN[:, sl], RSTDb[:, sl], OP.mult), inc="v", deps=[ev_rsq[q]])

    ev_xn0_dc = {}

    def p2_chunk0():
        scr = Xq0[:, 2 * QW: 2 * QW + C]
        for dc in range(ND):
            o2 = dc * CXN + 8
            S.op("v", lambda v, dc=dc, scr=scr: v.tensor_tensor(
                scr, xbf(0, dc), RSTDb[:, 0:C], OP.mult), inc="v",
                deps=[ev_rsq[0]])
            ev_xn0_dc[dc] = S.op("v", lambda v, o2=o2, scr=scr: v.tensor_tensor(
                XN[:, o2: o2 + C], scr, MRb[:, 0:C], OP.subtract), inc="v",
                deps=[ev_mrb[0], ev_z])

    ev_xn_dc = {}
    ev_p2q = [None] * NQ
    ev_pacc0 = [None]

    def emit_pacc0():
        prev = None
        for dc in range(ND):
            o2 = dc * CXN + 8
            prev = S.op("v", lambda v, dc=dc, o2=o2: v.tensor_tensor(
                Pacc[:, dc * T: dc * T + C], XN[:, o2: o2 + C], xbf(0, dc), OP.add),
                inc="v", deps=[])
        ev_pacc0[0] = prev

    def p2_rest(q):
        scr = Xq0[:, 0:C]
        prev = None
        for dc in range(ND):
            o2 = dc * CXN + 8 + q * C
            S.op("v", lambda v, q=q, dc=dc, scr=scr: v.tensor_tensor(
                scr, xbf(q, dc), RSTDb[:, q * QW:(q + 1) * QW], OP.mult), inc="v",
                deps=[ev_rsq[q]])
            ev_xn_dc[(q, dc)] = S.op("v", lambda v, o2=o2, scr=scr, q=q: v.tensor_tensor(
                XN[:, o2: o2 + C], scr, MRb[:, q * QW:(q + 1) * QW], OP.subtract),
                inc="v", deps=[ev_mrb[q]])
            prev = S.op("v", lambda v, q=q, dc=dc, o2=o2: v.tensor_tensor(
                Pacc[:, dc * T + q * C: dc * T + (q + 1) * C], XN[:, o2: o2 + C],
                xbf(q, dc), OP.add), inc="v", deps=[])
        ev_p2q[q] = prev

    stT.close()
    stA.close()

    stS = contextlib.ExitStack()
    VG = sb(stS, "VG", [128, NKT * D], BF16)
    RKG = sb(stS, "RKG", [128, ND * C], BF16)
    RDS = sb(stS, "RDS", [128, ND * C], BF16)
    Af = sb(stS, "Af", [128, ND * D], F32)
    Abf = sb(stS, "Abf", [128, ND * D], BF16)
    tps = [ps(stS, "tp%d" % i, [128, 1024], BF16) for i in range(2)]
    qs = [ps(stS, "q%d" % i, [128, 512]) for i in range(5)]
    assert nc.sbuf_bytes_remaining >= 0

    p512 = Pool([qs[2], qs[3], qs[0], qs[4], qs[1]])
    ptp = Pool(tps)
    ev_intra_prev = None
    ev_vg_prev = None
    ev_da_prev = None
    ev_abf_prev = None
    ev_ro_prev = None

    def xdep(c, dc):
        if c == 0:
            return ev_xn0_dc[dc]
        return ev_xn_dc[(c, dc)]

    ev_S_groups = {}
    ev_ve = {}
    ev_wk = {}
    ve_l = {}
    wk_l = {}
    mask_l = {}
    vg_l = {}
    intra_l = {}
    abf_l = {}
    DTH = ND // NEH

    def emit_v(c):
        base = c * C
        last = None
        ve_l[c] = {}
        for eh in range(NEH):
            for tt in range(NKT):
                slot_t, sdeps, slot = p512.acquire()
                for dc in range(ND):
                    deps = [xdep(c, dc)]
                    if dc == 0:
                        deps += sdeps + [ev_gww[eh]]
                    S.op("t", lambda t, slot_t=slot_t, tt=tt, eh=eh, dc=dc, base=base: t.matmul(
                        slot_t[:, 0:EW],
                        XN[:, dc * CXN + 8 + base + tt * 128: dc * CXN + 8 + base + tt * 128 + 128],
                        Wwt_sb[:, eh * HWW + dc * EW: eh * HWW + dc * EW + EW],
                        start=(dc == 0), stop=(dc == ND - 1)),
                        deps=deps,
                        inc="t" if dc == ND - 1 else None)
                evm = ("t", S.cnt["t"])
                dprev = []
                if c > 0:
                    dprev = [intra_l[c - 1][eh * DTH + DTH - 1], vg_l[c - 1][tt]]
                last = S.op("s", lambda s, slot_t=slot_t, tt=tt, eh=eh: s.copy(
                    Vtm[:, tt * D + eh * EW: tt * D + eh * EW + EW], slot_t[:, 0:EW]),
                    inc="s", deps=[evm] + dprev)
                ve_l[c][(tt, eh)] = last
                p512.release(slot, last)
        ev_ve[c] = last

    def emit_wk(c):
        base = c * C
        last = None
        wk_l[c] = []
        for kt in range(NKT):
            slot_t, sdeps, slot = ptp.acquire()
            for dc in range(ND):
                deps = [xdep(c, dc)]
                if dc == 0:
                    deps += sdeps
                S.op("t", lambda t, slot_t=slot_t, kt=kt, dc=dc, base=base: t.transpose(
                    slot_t[:, dc * 128:(dc + 1) * 128],
                    XN[:, dc * CXN + 7 + base + kt * 128: dc * CXN + 7 + base + kt * 128 + 128],
                    identb[:]), deps=deps + ([ev_id] if dc == 0 else []),
                    inc="t" if dc == ND - 1 else None)
            evm = ("t", S.cnt["t"])
            last = S.op("s", lambda s, slot_t=slot_t, kt=kt: s.copy(
                WKtm[:, kt * D:(kt + 1) * D], slot_t[:, 0:ND * 128]), inc="s",
                deps=[evm, ev_da_prev])
            wk_l[c].append(last)
            ptp.release(slot, last)
        ev_wk[c] = last

    def emit_S(c):
        base = c * C
        groups = []
        for kt in range(NKT):
            slot_t, sdeps, slot = p512.acquire()
            for dc in range(ND):
                deps = [xdep(c, dc)]
                if dc == 0:
                    deps += sdeps
                S.op("t", lambda t, slot_t=slot_t, kt=kt, dc=dc, base=base: t.matmul(
                    slot_t[:, 0:C],
                    XN[:, dc * CXN + 7 + base + kt * 128: dc * CXN + 7 + base + kt * 128 + 128],
                    XN[:, dc * CXN + 8 + base: dc * CXN + 8 + base + C],
                    start=(dc == 0), stop=(dc == ND - 1)),
                    deps=deps,
                    inc="t" if dc == ND - 1 else None)
            groups.append((slot_t, slot, ("t", S.cnt["t"])))
        ev_S_groups[c] = groups

    ev_masks = {}

    def emit_masks(c):
        last = None
        mask_l[c] = []
        for kt, (slot_t, slot, evm) in enumerate(ev_S_groups.pop(c)):
            last = S.op("v", lambda v, slot_t=slot_t, kt=kt: v.tensor_tensor(
                Stm[:, kt * C:(kt + 1) * C], slot_t[:, 0:C], Mt[:, kt * C:(kt + 1) * C], OP.mult),
                inc="v", deps=[evm, ev_intra_prev, ev_mt])
            mask_l[c].append(last)
            p512.release(slot, last)
        ev_masks[c] = last

    emit_sums(0)
    stats_q(0)
    emit_sums(1)
    stats_q(1)
    p2_chunk0()
    ptp.slots[0][1] = ev_ea[0]
    ptp.slots[1][1] = ev_ea[1]
    emit_wk(0)
    emit_sums(2)
    stats_q(2)
    p512.slots[0][1] = ev_rsq[0]
    p512.slots[1][1] = ev_rsq[1]
    p512.slots[2][1] = ev_ea[2]
    p512.slots[3][1] = ev_rsq[2]
    emit_S(0)
    emit_sums(3)
    stats_q(3)
    p512.slots[4][1] = ev_ea[3]
    emit_masks(0)
    p2_rest(1)
    emit_pacc0()
    emit_v(0)

    for c in range(NCH):
        base = c * C
        need_state = (c < NCH - 1) or NCH == 1
        ev_rkg_l = []
        if c > 0:
            for jc in range(ND):
                ev_rkg_l.append(S.op("v", lambda v, jc=jc, base=base: v.tensor_tensor(
                    RKG[:, jc * C:(jc + 1) * C],
                    XN[:, jc * CXN + 8 + base: jc * CXN + 8 + base + C],
                    GPbf[:], OP.mult), inc="v", deps=[ev_intra_prev, ev_pacc0[0]]))
        ev_st_last = ev_masks[c]

        ev_vg = None
        if need_state:
            vg_l[c] = []
            for tt in range(NKT):
                ev_vg = S.op("s", lambda s, tt=tt: s.activation(
                    VG[:, tt * D:(tt + 1) * D], Vtm[:, tt * D:(tt + 1) * D],
                    AT.Identity, scale=gwv[:, tt:tt + 1]), inc="s",
                    deps=[ve_l[c][(tt, 0)], ve_l[c][(tt, 1)], ev_da_prev, ev_mrb[NQ - 1]])
                vg_l[c].append(ev_vg)
            ev_vg_prev = ev_vg

        ev_intra = None
        ev_rd_last = None
        intra_l[c] = []
        rd_l = []
        for dt in range(ND):
            slot_t, sdeps, slot = p512.acquire()
            for kt in range(NKT):
                deps = []
                if kt == 0:
                    deps = list(sdeps)
                if dt % DTH == 0:
                    deps = deps + [mask_l[c][kt], ve_l[c][(kt, dt // DTH)]]
                S.op("t", lambda t, slot_t=slot_t, dt=dt, kt=kt, c=c: t.matmul(
                    slot_t[:, 0:C],
                    Vtm[:, kt * D + dt * 128: kt * D + dt * 128 + 128],
                    Stm[:, kt * C:(kt + 1) * C],
                    start=(kt == 0), stop=(kt == NKT - 1 and c == 0)), deps=deps,
                    inc="t" if (c == 0 and kt == NKT - 1) else None)
            if c > 0:
                for jc in range(ND):
                    deps = []
                    if dt == 0:
                        deps = [ev_rkg_l[jc], abf_l[c - 1][jc]]
                    S.op("t", lambda t, slot_t=slot_t, dt=dt, jc=jc: t.matmul(
                        slot_t[:, 0:C],
                        Abf[:, jc * D + dt * 128: jc * D + dt * 128 + 128],
                        RKG[:, jc * C:(jc + 1) * C],
                        start=False, stop=(jc == ND - 1)), deps=deps,
                        inc="t" if jc == ND - 1 else None)
            ev_intra = ("t", S.cnt["t"])
            intra_l[c].append(ev_intra)
            ev_rd_last = S.op("s", lambda s, slot_t=slot_t, dt=dt: s.copy(
                RDS[:, dt * C:(dt + 1) * C], slot_t[:, 0:C]), inc="s",
                deps=[ev_intra, ev_ro_prev, ev_p2q[1]])
            rd_l.append(ev_rd_last)
            p512.release(slot, ev_rd_last)
        ev_intra_prev = ev_intra

        def emit_da(c=c):
            ev_af = []
            ev_da = None
            for jc in range(ND):
                slots = []
                for ih in range(NEH):
                    slot_t, sdeps, slot = p512.acquire()
                    for kt in range(NKT):
                        deps = list(sdeps) if kt == 0 else []
                        if jc == 0 and ih == 0:
                            deps = deps + [wk_l[c][kt], vg_l[c][kt]]
                        S.op("t", lambda t, slot_t=slot_t, jc=jc, ih=ih, kt=kt: t.matmul(
                            slot_t[:, 0:EW],
                            WKtm[:, kt * D + jc * 128: kt * D + jc * 128 + 128],
                            VG[:, kt * D + ih * EW: kt * D + ih * EW + EW],
                            start=(kt == 0), stop=(kt == NKT - 1)), deps=deps,
                            inc="t" if kt == NKT - 1 else None)
                    slots.append((ih, slot_t, slot, ("t", S.cnt["t"])))
                ev_da = slots[-1][3]
                for (ih, slot_t, slot, evm) in slots:
                    sl = slice(jc * D + ih * EW, jc * D + ih * EW + EW)
                    if c == 0:
                        evf = S.op("s", lambda s, slot_t=slot_t, sl=sl: s.copy(
                            Af[:, sl], slot_t[:, 0:EW]), inc="s",
                            deps=[evm, ev_p2q[NQ - 1]])
                    else:
                        evf = S.op("v", lambda v, slot_t=slot_t, sl=sl: v.scalar_tensor_tensor(
                            Af[:, sl], Af[:, sl], gc_v[:, 0:1], slot_t[:, 0:EW],
                            OP.mult, OP.add), inc="v", deps=[evm])
                    ev_af.append(evf)
                    p512.release(slot, evf)
            return ev_da, ev_af

        def emit_ro(base=base):
            ev_ro = None
            for et in range(ND):
                slot_t, sdeps, slot = p512.acquire()
                for it in range(ND):
                    deps = list(sdeps) + [ev_gw] if it == 0 else []
                    if et == 0:
                        deps = deps + [rd_l[it]]
                    S.op("t", lambda t, slot_t=slot_t, et=et, it=it: t.matmul(
                        slot_t[:, 0:C],
                        Wrt_sb[:, et * D + it * 128: et * D + it * 128 + 128],
                        RDS[:, it * C:(it + 1) * C],
                        start=(it == 0), stop=(it == ND - 1)), deps=deps,
                        inc="t" if it == ND - 1 else None)
                ev_ro = ("t", S.cnt["t"])
                evp = S.op("v", lambda v, slot_t=slot_t, et=et, base=base: v.scalar_tensor_tensor(
                    Pacc[:, et * T + base: et * T + base + C], slot_t[:, 0:C],
                    al_v[:, 0:1], Pacc[:, et * T + base: et * T + base + C],
                    OP.mult, OP.add), inc="v", deps=[ev_ro])
                p512.release(slot, evp)
            return ev_ro

        if c == 0 and need_state:
            for q in range(2, NQ):
                p2_rest(q)
            ev_ro_prev = emit_ro()
            ev_da_prev, ev_af = emit_da()
        else:
            if need_state:
                ev_da_prev, ev_af = emit_da()
            ev_ro_prev = emit_ro()

        if need_state:
            ev_abf = None
            abf_l[c] = []
            for jc in range(ND):
                deps = [ev_af[jc * NEH + NEH - 1]]
                if c > 0:
                    deps.append(ev_intra_prev)
                ev_abf = S.op("s", lambda s, jc=jc: s.copy(
                    Abf[:, jc * D:(jc + 1) * D], Af[:, jc * D:(jc + 1) * D]), inc="s", deps=deps)
                abf_l[c].append(ev_abf)
            ev_abf_prev = ev_abf

        if c + 1 < NCH:
            emit_v(c + 1)
            if c + 1 < NCH - 1:
                emit_wk(c + 1)
            emit_S(c + 1)
            emit_masks(c + 1)

    lsems = ["l0", "l1", "l2", "l3", "l4", "l5"]
    lstate = [[None, 0] for _ in range(6)]
    lidx = [0]
    pre_l = []

    def issue_slab_l(src, ob):
        i = lidx[0] % 6
        lidx[0] += 1
        consumer, n = lstate[i]
        S.op("g", lambda g, i=i, src=src, ob=ob: g.dma_start(
            out=wslabs_l[i][:], in_=src[:, ob * D:(ob + 1) * D]),
            inc=lsems[i], inc_n=16, deps=[consumer] if consumer else [])
        lstate[i][1] = n + 1
        return i, (lsems[i], 16 * (n + 1))

    OB_ORDER = [NH - 1] + list(range(NH - 1))
    for (psrc, pob) in ((w1, OB_ORDER[0]), (w2, OB_ORDER[0]), (w1, OB_ORDER[1]), (w2, OB_ORDER[1])):
        pre_l.append(issue_slab_l(psrc, pob))

    S.flush()
    stS.close()
    stB.close()

    stC = contextlib.ExitStack()
    Ha = sb(stC, "Ha", [128, NH * TAUM], BF16)
    Hb = sb(stC, "Hb", [128, NH * TAUM], BF16)
    wslabs = [sb(stC, "wsl%d" % i, [128, HID], BF16) for i in range(3)]
    sa_tiles = [sb(stC, "sa%d" % i, [128, TAUM], BF16) for i in range(2)]
    ostg = sb(stC, "ostg", [128, 2 * TAUM], F32)
    fs = [ps(stC, "f%d" % i, [128, TAUM]) for i in range(4)]
    assert nc.sbuf_bytes_remaining >= 0

    fps = Pool(fs)
    sas = Pool(sa_tiles)
    wstate = [[None, 0] for _ in range(3)]
    widx = [0]
    n_pre = [0]

    def load_slab(src, ob, K):
        i = widx[0] % 3
        widx[0] += 1
        consumer, n = wstate[i]
        S.op("g", lambda g, i=i, src=src, ob=ob, K=K: g.dma_start(
            out=wslabs[i][:, 0:K], in_=src[:, ob * K:(ob + 1) * K]),
            inc=wsems[i], inc_n=16, deps=[consumer] if consumer else [])
        wstate[i][1] = n + 1
        return i, (wsems[i], 16 * (n + 1))

    def load_slab_l(src, ob):
        if n_pre[0] < len(pre_l):
            i, ev = pre_l[n_pre[0]]
            n_pre[0] += 1
            return i, ev
        return issue_slab_l(src, ob)

    def pieces(tau):
        out, off = [], 0
        while off < tau:
            w = min(512, tau - off)
            out.append((off, w))
            off += w
        return out

    ev_l4_mm_prev = None
    ev_ostg_dma = [None, None]
    t0 = 0
    for ti, tau in enumerate(taus):
        pcs = pieces(tau)
        ev_h1_list = [None] * NH
        ev_h1 = None
        for ob in OB_ORDER:
            ia, eva = load_slab_l(w1, ob)
            ib, evb = load_slab_l(w2, ob)
            grp = []
            for (wsl_i, evw) in ((ia, eva), (ib, evb)):
                slot_t, sdeps, slot = fps.acquire()
                for dc in range(ND):
                    for (off, w) in pcs:
                        deps = []
                        if dc == 0 and off == 0:
                            deps = list(sdeps) + [evw, ev_l4_mm_prev]
                        S.op("t", lambda t, slot_t=slot_t, wsl_i=wsl_i, dc=dc, off=off, w=w, t0=t0: t.matmul(
                            slot_t[:, off:off + w],
                            wslabs_l[wsl_i][:, dc * 128:(dc + 1) * 128],
                            XN[:, dc * CXN + 8 + t0 + off: dc * CXN + 8 + t0 + off + w],
                            start=(dc == 0), stop=(dc == ND - 1)), deps=deps,
                            inc="t" if (dc == ND - 1 and (off, w) == pcs[-1]) else None)
                grp.append((slot_t, slot, ("t", S.cnt["t"])))
            (pa_t, pa_slot, ev_ma), (pb_t, pb_slot, ev_mb) = grp
            lstate[ia][0] = ev_ma
            lstate[ib][0] = ev_mb
            sa_t, sa_deps, sa_slot = sas.acquire()
            ev_sa = S.op("s", lambda s, sa_t=sa_t, pa_t=pa_t, ob=ob, tau=tau: s.activation(
                sa_t[:, 0:tau], pa_t[:, 0:tau], AT.Silu, scale=s1v[:, ob:ob + 1]),
                inc="s", deps=[ev_ma] + sa_deps)
            ev_h1 = S.op("v", lambda v, pb_t=pb_t, sa_t=sa_t, ob=ob, tau=tau: v.scalar_tensor_tensor(
                Ha[:, ob * TAUM: ob * TAUM + tau], pb_t[:, 0:tau], s2v[:, ob:ob + 1],
                sa_t[:, 0:tau], OP.mult, OP.mult), inc="v",
                deps=[ev_mb, ev_sa, ev_l4_mm_prev])
            ev_h1_list[ob] = ev_h1
            fps.release(pa_slot, ev_sa)
            fps.release(pb_slot, ev_h1)
            sas.release(sa_slot, ev_h1)

        ev_prev_list = ev_h1_list
        ev_l2_mm = None
        for li, (wsrc, ssrc, Hin, Hout) in enumerate(((h1, sh1v, Ha, Hb), (h2, sh2v, Hb, Ha))):
            ev_hl = None
            ev_hl_list = [None] * NH
            for ob in OB_ORDER:
                iw, evw = load_slab(wsrc, ob, HID)
                slot_t, sdeps, slot = fps.acquire()
                for hc in range(NH):
                    for (off, w) in pcs:
                        deps = []
                        if hc == 0 and off == 0:
                            deps = list(sdeps) + [evw]
                        if ob == OB_ORDER[0] and off == 0:
                            deps = deps + [ev_prev_list[hc]]
                        S.op("t", lambda t, slot_t=slot_t, iw=iw, hc=hc, off=off, w=w, Hin=Hin: t.matmul(
                            slot_t[:, off:off + w],
                            wslabs[iw][:, hc * 128:(hc + 1) * 128],
                            Hin[:, hc * TAUM + off: hc * TAUM + off + w],
                            start=(hc == 0), stop=(hc == NH - 1)), deps=deps,
                            inc="t" if (hc == NH - 1 and (off, w) == pcs[-1]) else None)
                ev_mm = ("t", S.cnt["t"])
                wstate[iw][0] = ev_mm
                if li == 0:
                    ev_l2_mm = ev_mm
                deps = [ev_mm]
                if li == 1:
                    deps.append(ev_l2_mm)
                ev_hl = S.op("s", lambda s, slot_t=slot_t, ob=ob, tau=tau, ssrc=ssrc, Hout=Hout: s.activation(
                    Hout[:, ob * TAUM: ob * TAUM + tau], slot_t[:, 0:tau], AT.Gelu,
                    scale=ssrc[:, ob:ob + 1]), inc="s", deps=deps)
                ev_hl_list[ob] = ev_hl
                fps.release(slot, ev_hl)
            ev_prev_list = ev_hl_list

        ev_l4_mm = None
        for ob in range(ND):
            final_ob = (ti == len(taus) - 1 and ob == ND - 1)
            iw, evw = load_slab(w3, ob, HID)
            slot_t, sdeps, slot = fps.acquire()
            pev = {}
            if final_ob:
                for (off, w) in pcs:
                    for hc in range(NH):
                        deps = []
                        if hc == 0 and off == 0:
                            deps = list(sdeps) + [evw]
                        S.op("t", lambda t, slot_t=slot_t, iw=iw, hc=hc, off=off, w=w: t.matmul(
                            slot_t[:, off:off + w],
                            wslabs[iw][:, hc * 128:(hc + 1) * 128],
                            Ha[:, hc * TAUM + off: hc * TAUM + off + w],
                            start=(hc == 0), stop=(hc == NH - 1)), deps=deps,
                            inc="t" if hc == NH - 1 else None)
                    pev[off] = ("t", S.cnt["t"])
                ev_mm = pev[pcs[-1][0]]
            else:
                for hc in range(NH):
                    for (off, w) in pcs:
                        deps = []
                        if hc == 0 and off == 0:
                            deps = list(sdeps) + [evw]
                        if ob == 0 and off == 0:
                            deps = deps + [ev_prev_list[hc]]
                        S.op("t", lambda t, slot_t=slot_t, iw=iw, hc=hc, off=off, w=w: t.matmul(
                            slot_t[:, off:off + w],
                            wslabs[iw][:, hc * 128:(hc + 1) * 128],
                            Ha[:, hc * TAUM + off: hc * TAUM + off + w],
                            start=(hc == 0), stop=(hc == NH - 1)), deps=deps,
                            inc="t" if (hc == NH - 1 and (off, w) == pcs[-1]) else None)
                ev_mm = ("t", S.cnt["t"])
            ev_l4_mm = ev_mm
            wstate[iw][0] = ev_mm
            osl = ob % 2
            if final_ob:
                evs = None
                for pi, (off, w) in enumerate(pcs):
                    deps = [pev[off]]
                    if pi == 0:
                        deps.append(ev_ostg_dma[osl])
                    evs = S.op("v", lambda v, slot_t=slot_t, ob=ob, osl=osl, off=off, w=w, t0=t0: v.scalar_tensor_tensor(
                        ostg[:, osl * TAUM + off: osl * TAUM + off + w], slot_t[:, off:off + w],
                        s3v[:, ob:ob + 1], Pacc[:, ob * T + t0 + off: ob * T + t0 + off + w],
                        OP.mult, OP.add), inc="v", deps=deps)
                    if pi < len(pcs) - 1:
                        src_p = ostg[:].rearrange("p (c f) -> p c f", f=TAUM)[:, osl:osl + 1, off:off + w]
                        dst_p = out_d.ap().rearrange("(c p) t -> p c t", p=128)[:, ob:ob + 1, t0 + off:t0 + off + w]
                        S.op("y", lambda y, src_p=src_p, dst_p=dst_p: y.dma_start(
                            out=dst_p, in_=src_p), inc="o", inc_n=16, deps=[evs])
                    else:
                        h = w // 2
                        src_a = ostg[:].rearrange("p (c f) -> p c f", f=TAUM)[:, osl:osl + 1, off:off + h]
                        dst_a = out_d.ap().rearrange("(c p) t -> p c t", p=128)[:, ob:ob + 1, t0 + off:t0 + off + h]
                        src_b = ostg[:].rearrange("p (c f) -> p c f", f=TAUM)[:, osl:osl + 1, off + h:off + w]
                        dst_b = out_d.ap().rearrange("(c p) t -> p c t", p=128)[:, ob:ob + 1, t0 + off + h:t0 + off + w]
                        S.op("y", lambda y, src_a=src_a, dst_a=dst_a: y.dma_start(
                            out=dst_a, in_=src_a), inc="o", inc_n=16, deps=[evs])
                        ev_ostg_dma[osl] = S.op("y", lambda y, src_b=src_b, dst_b=dst_b: y.dma_start(
                            out=dst_b, in_=src_b), inc="o", inc_n=16, deps=[evs])
                fps.release(slot, evs)
            else:
                evs = S.op("v", lambda v, slot_t=slot_t, ob=ob, osl=osl, tau=tau, t0=t0: v.scalar_tensor_tensor(
                    ostg[:, osl * TAUM: osl * TAUM + tau], slot_t[:, 0:tau],
                    s3v[:, ob:ob + 1], Pacc[:, ob * T + t0: ob * T + t0 + tau],
                    OP.mult, OP.add), inc="v", deps=[ev_mm, ev_ostg_dma[osl]])
                fps.release(slot, evs)
                src_ap = ostg[:].rearrange("p (c f) -> p c f", f=TAUM)[:, osl:osl + 1, 0:tau]
                dst_ap = out_d.ap().rearrange("(c p) t -> p c t", p=128)[:, ob:ob + 1, t0:t0 + tau]
                ev_ostg_dma[osl] = S.op("y", lambda y, src_ap=src_ap, dst_ap=dst_ap: y.dma_start(
                    out=dst_ap, in_=src_ap), inc="o", inc_n=16, deps=[evs])
        ev_l4_mm_prev = ev_l4_mm
        t0 += tau

    S.wait("y", [("o", S.cnt["o"])])
    S.flush()
    stC.close()
    stP.close()

    return nc



def _block_weight(W):
    O, K = W.shape
    Wt = np.ascontiguousarray(W.T)
    a = Wt.reshape(K // 128, 128, O // 128, 128)
    return np.ascontiguousarray(a.transpose(1, 2, 0, 3)).reshape(128, (O // 128) * K)


def _moving_weight(W):
    Wt = np.ascontiguousarray(W.T)
    Dd, E = Wt.shape
    NEH, EW = max(E // 512, 1), min(E, 512)
    a = Wt.reshape(Dd // 128, 128, NEH, EW)
    return np.ascontiguousarray(a.transpose(1, 2, 0, 3)).reshape(128, NEH * (Dd // 128) * EW)


def _scale_vec(s):
    return np.ascontiguousarray(np.asarray(s, dtype=np.float32).reshape(-1, 128).T)


def _tables(C, NKT):
    p = np.arange(128, dtype=np.float32)
    prow = np.broadcast_to(np.arange(C, dtype=np.float32), (128, C)).copy()
    revw = np.empty((128, NKT), dtype=np.float32)
    difft = np.empty((128, NKT * C), dtype=np.float32)
    causal = np.empty((128, NKT * C), dtype=np.float32)
    cc = np.arange(C, dtype=np.float32)
    for kt in range(NKT):
        kg = kt * 128 + p
        revw[:, kt] = (C - 1) - kg
        d = cc[None, :] - 1.0 - kg[:, None]
        difft[:, kt * C:(kt + 1) * C] = np.maximum(d, 0.0)
        causal[:, kt * C:(kt + 1) * C] = (cc[None, :] > kg[:, None]).astype(np.float32)
    return prow, revw, difft, causal


def make_in_maps(inputs, C=512):
    NKT = C // 128
    x = np.asarray(inputs["L0_attn_out"], dtype=np.float32)
    B = x.shape[0]
    prow, revw, difft, causal = _tables(C, NKT)
    dec = np.full((128, 1), np.float32(np.asarray(inputs["L0_attn_decay"], dtype=np.float32)), dtype=np.float32)
    la = np.full((128, 1), np.float32(np.asarray(inputs["L0_attn_log_alpha"], dtype=np.float32)), dtype=np.float32)
    shared = {
        "wwt": _moving_weight(np.asarray(inputs["L0_attn_W_write"], dtype=np.float32)),
        "wrt": _block_weight(np.asarray(inputs["L0_attn_W_read"], dtype=np.float32)),
        "w1": _block_weight(np.asarray(inputs["L0_ffn_w1_w"], dtype=np.float32)),
        "w2": _block_weight(np.asarray(inputs["L0_ffn_w2_w"], dtype=np.float32)),
        "h1": _block_weight(np.asarray(inputs["L0_ffn_h1_w"], dtype=np.float32)),
        "h2": _block_weight(np.asarray(inputs["L0_ffn_h2_w"], dtype=np.float32)),
        "w3": _block_weight(np.asarray(inputs["L0_ffn_w3_w"], dtype=np.float32)),
        "svd": np.ascontiguousarray(np.concatenate([
            _scale_vec(inputs["L0_ffn_w1_s"]), _scale_vec(inputs["L0_ffn_w2_s"]),
            _scale_vec(inputs["L0_ffn_h1_s"]), _scale_vec(inputs["L0_ffn_h2_s"]),
            _scale_vec(inputs["L0_ffn_w3_s"]), dec, la], axis=1)),
        "csd": np.ascontiguousarray(np.concatenate([prow, revw, difft, causal], axis=1)),
        "ident_d": np.eye(128, dtype=np.float32),
    }
    maps = []
    NQ, QW = 4, 512
    for b in range(B):
        m = dict(shared)
        xt = x[b].T
        Dd, T = xt.shape
        m["xT"] = np.ascontiguousarray(
            xt.reshape(Dd // 128, 128, NQ, QW).transpose(1, 2, 0, 3).reshape(128, NQ * (Dd // 128) * QW))
        maps.append(m)
    return maps


def kernel(**inputs):
    nc = build_nc()
    in_maps = make_in_maps(inputs)
    res = run_bass_kernel_spmd(nc, in_maps, core_ids=list(range(8)))
    out = np.stack([np.ascontiguousarray(r["out"].T) for r in res.results])
    return out.astype(np.float32)
